# version 03
# baseline (speedup 1.0000x reference)
import numpy as np
from contextlib import ExitStack
import concourse.bass as bass
import concourse.mybir as mybir
from concourse.bass_utils import run_bass_kernel_spmd
import ml_dtypes

F32 = mybir.dt.float32
BF16 = mybir.dt.bfloat16
AF = mybir.ActivationFunctionType
ALU = mybir.AluOpType
AX = mybir.AxisListType.X
bfd = ml_dtypes.bfloat16
f32 = np.float32

T = 2048
D = 1024
N_CORES = 8
EPS = f32(1e-5)


def ap0(ap, extra_dims):
    dims = [list(d) for d in ap.ap] + [list(d) for d in extra_dims]
    return bass.AP(ap.tensor, ap.offset, dims)


class Ctx:
    def __init__(self):
        self.nc = bass.Bass("TRN2", target_bir_lowering=False)
        self.es = ExitStack()
        self.semc = {}
        self._sems = {}

    def sem(self, name):
        if name not in self._sems:
            self._sems[name] = self.es.enter_context(self.nc.semaphore(name))
            self.semc[name] = 0
        return self._sems[name]

    def _uniq(self, name):
        if not hasattr(self, "_names"):
            self._names = set()
        base, n = name, 1
        while name in self._names:
            n += 1
            name = f"{base}_{n}"
        self._names.add(name)
        return name

    def sb(self, name, shape, dt=F32):
        es = self.scopes[-1] if getattr(self, "scopes", None) else self.es
        return es.enter_context(self.nc.sbuf_tensor(self._uniq(name), shape, dt))

    def push_scope(self):
        if not hasattr(self, "scopes"):
            self.scopes = []
        self.scopes.append(ExitStack())

    def pop_scope(self):
        self.scopes.pop().close()

    def ps(self, name, shape, dt=F32):
        return self.es.enter_context(self.nc.psum_tensor(name, shape, dt))

    def dtmp(self, name, shape, dt=F32):
        return self.nc.dram_tensor(self._uniq(name), list(shape), dt)

    def din(self, name, shape, dt=F32):
        return self.nc.dram_tensor(name, list(shape), dt, kind="ExternalInput")

    def dout(self, name, shape, dt=F32):
        return self.nc.dram_tensor(name, list(shape), dt, kind="ExternalOutput")


def get_pool(c):
    if not hasattr(c, "_pool"):
        psa = c.ps("P_psa", [128, 1024])
        psb = c.ps("P_psb", [128, 1024])
        pp0 = c.ps("P_pp0", [128, 512])
        pp1 = c.ps("P_pp1", [128, 512])
        pbf = c.ps("P_pbf", [128, 2048], BF16)
        c._pool = dict(psa=psa, psb=psb, pbf=pbf,
                       pp=[pp0[:], pp1[:], psb[:, 0:512], psb[:, 512:1024]])
    return c._pool


def loadblk(c, pairs):
    nc, sem = c.nc, c.sem("dma")
    n0 = c.semc["dma"]
    with nc.Block() as blk:
        @blk.gpsimd
        def _(g):
            n = n0
            for dst, src in pairs:
                g.dma_start(dst, src).then_inc(sem, 16)
                n += 16
            g.wait_ge(sem, n)
    c.semc["dma"] = n0 + 16 * len(pairs)


def fview(d, t0, Tt):
    return d.ap()[:, t0:t0 + Tt].rearrange("(kc p) t -> p kc t", p=128)


def wslice(dw, o0, ow):
    return dw.ap()[:, o0:o0 + ow].rearrange("(kc p) o -> p kc o", p=128)


def emit_ln(c, xnsl, ones_bf, S, Tt, kcs=8, cast_from=None):
    nc = c.nc
    sq = S["sq"]
    if cast_from is not None:
        with nc.Block() as blk:
            @blk.vector
            def _(v):
                for kc in range(kcs):
                    v.tensor_copy(xnsl(kc), cast_from(kc))
    ps_a, ps_b = S["ps_a"], S["ps_b"]
    mu, va, rstd = S["mu"], S["va"], S["rstd"]
    rdim = 1.0 / (kcs * 128)
    with nc.Block() as blk:
        @blk.scalar
        def _(s):
            for kc in range(kcs):
                s.activation(sq[:, kc * Tt:(kc + 1) * Tt], xnsl(kc), AF.Square)
    with nc.Block() as blk:
        @blk.tensor
        def _(t):
            ntt = (Tt + 511) // 512
            for tt in range(ntt):
                w = min(512, Tt - tt * 512)
                for kc in range(kcs):
                    t.matmul(ps_a[:, tt * 512:tt * 512 + w], ones_bf[:],
                             xnsl(kc)[:, tt * 512:tt * 512 + w],
                             start=(kc == 0), stop=(kc == kcs - 1))
                for kc in range(kcs):
                    t.matmul(ps_b[:, tt * 512:tt * 512 + w], ones_bf[:],
                             sq[:, kc * Tt + tt * 512:kc * Tt + tt * 512 + w],
                             start=(kc == 0), stop=(kc == kcs - 1))
    smf = c.sem("fv")
    with nc.Block() as blk:
        @blk.vector
        def _(v):
            f0 = c.semc["fv"]
            v.tensor_single_scalar(mu[:, 0:Tt], ps_a[:, 0:Tt], rdim, ALU.mult)
            v.tensor_single_scalar(va[:, 0:Tt], ps_b[:, 0:Tt], rdim,
                                   ALU.mult).then_inc(smf, 1)
            v.wait_ge(smf, f0 + 1)
            v.tensor_mul(rstd[:, 0:Tt], mu[:, 0:Tt], mu[:, 0:Tt]).then_inc(smf, 1)
            v.wait_ge(smf, f0 + 2)
            v.tensor_sub(va[:, 0:Tt], va[:, 0:Tt], rstd[:, 0:Tt])
            c.semc["fv"] = f0 + 2
    with nc.Block() as blk:
        @blk.scalar
        def _(s):
            f0 = c.semc["fv"]
            s.activation(va[:, 0:Tt], va[:, 0:Tt], AF.Ln, bias=EPS).then_inc(smf, 1)
            s.wait_ge(smf, f0 + 1)
            s.activation(rstd[:, 0:Tt], va[:, 0:Tt], AF.Exp, scale=f32(-0.5))
            c.semc["fv"] = f0 + 1
    with nc.Block() as blk:
        @blk.vector
        def _(v):
            st = S["st"]
            f0 = c.semc["fv"]
            for kc in range(kcs):
                ntt = (Tt + 511) // 512
                for tt in range(ntt):
                    w = min(512, Tt - tt * 512)
                    v.tensor_sub(st[:, 0:w], xnsl(kc)[:, tt * 512:tt * 512 + w],
                                 mu[:, tt * 512:tt * 512 + w]).then_inc(smf, 1)
                    f0 += 1
                    v.wait_ge(smf, f0)
                    v.tensor_mul(xnsl(kc)[:, tt * 512:tt * 512 + w], st[:, 0:w],
                                 rstd[:, tt * 512:tt * 512 + w])
            c.semc["fv"] = f0


def emit_l1ffn(c, dx3, dW, dout_x4):
    nc = c.nc
    TH = 1024
    ones_bf = c.sb("l4_ones", [128, 128], BF16)
    s1 = c.sb("l4_s1", [128, 32]); s2 = c.sb("l4_s2", [128, 32])
    h1s = c.sb("l4_h1s", [128, 32]); h2s = c.sb("l4_h2s", [128, 32])
    w3s = c.sb("l4_w3s", [128, 8])
    loadblk(c, [(ones_bf[:], dW["ones_bf"].ap()),
                (s1[:], dW["l1f_s1"].ap()), (s2[:], dW["l1f_s2"].ap()),
                (h1s[:], dW["l1f_h1s"].ap()), (h2s[:], dW["l1f_h2s"].ap()),
                (w3s[:], dW["l1f_w3s"].ap())])

    xn = c.sb("l4_xn", [128, 8 * TH], BF16)
    hh = c.sb("l4_h", [128, 32 * TH], BF16)
    h2 = c.sb("l4_h2", [128, 32 * TH], BF16)
    P = get_pool(c)
    S = dict(sq=c.sb("l4_sq", [128, 8 * TH], BF16),
             ps_a=P["psa"], ps_b=P["psb"],
             mu=c.sb("l4_mu", [128, TH]), va=c.sb("l4_va", [128, TH]),
             rstd=c.sb("l4_rstd", [128, TH]), st=None)
    wb1 = [c.sb(f"l4_wb1_{i}", [128, 8 * 128], BF16) for i in range(2)]
    wb2 = [c.sb(f"l4_wb2_{i}", [128, 8 * 128], BF16) for i in range(2)]
    wbig = [c.sb(f"l4_wbg{i}", [128, 32 * 128], BF16) for i in range(2)]
    pp = P["pp"]
    atmp = [c.sb(f"l4_at{i}", [128, 512], BF16) for i in range(2)]
    ost = [c.sb(f"l4_ost{i}", [128, 512]) for i in range(2)]
    xt = [c.sb(f"l4_xt{i}", [128, 512]) for i in range(2)]
    S["st"] = ost[0]
    sm_mm, sm_ev, sm_as, sm_wl = (c.sem("mm"), c.sem("ev"), c.sem("as"), c.sem("wl"))
    dma = c.sem("dma")

    for half in range(2):
        t0 = half * TH
        loadblk(c, [(xn[:].rearrange("p (kc t) -> p kc t", kc=8), fview(dx3, t0, TH))])
        emit_ln(c, lambda kc: xn[:, kc * TH:(kc + 1) * TH], ones_bf, S, TH)

        mm0, ev0, as0, wl0 = (c.semc[s] for s in ("mm", "ev", "as", "wl"))
        with nc.Block() as blk:
            @blk.gpsimd
            def _(g):
                for o in range(32):
                    if o >= 2:
                        g.wait_ge(sm_mm, mm0 + 2 * (o - 1))
                    g.dma_start(wb1[o % 2][:], wslice(dW["l1f_w1T"], o * 128, 128)).then_inc(sm_wl, 16)
                    g.dma_start(wb2[o % 2][:], wslice(dW["l1f_w2T"], o * 128, 128)).then_inc(sm_wl, 16)
            @blk.tensor
            def _(t):
                for o in range(32):
                    t.wait_ge(sm_wl, wl0 + 32 * (o + 1))
                    for tt in range(2):
                        j = o * 2 + tt
                        if j >= 2:
                            t.wait_ge(sm_ev, ev0 + j - 1)
                            t.wait_ge(sm_as, as0 + j - 1)
                        p1, p2 = pp[2 * (j % 2)], pp[2 * (j % 2) + 1]
                        for kc in range(8):
                            t.matmul(p1[:], wb1[o % 2][:, kc * 128:(kc + 1) * 128],
                                     xn[:, kc * TH + tt * 512: kc * TH + (tt + 1) * 512],
                                     start=(kc == 0), stop=(kc == 7))
                        for kc in range(8):
                            mmi = t.matmul(p2[:], wb2[o % 2][:, kc * 128:(kc + 1) * 128],
                                           xn[:, kc * TH + tt * 512: kc * TH + (tt + 1) * 512],
                                           start=(kc == 0), stop=(kc == 7))
                        mmi.then_inc(sm_mm, 1)
            @blk.scalar
            def _(s):
                for o in range(32):
                    for tt in range(2):
                        j = o * 2 + tt
                        s.wait_ge(sm_mm, mm0 + j + 1)
                        if j >= 2:
                            s.wait_ge(sm_ev, ev0 + j - 1)
                        s.activation(atmp[j % 2][:], pp[2 * (j % 2)][:], AF.Silu,
                                     scale=s1[:, o:o + 1]).then_inc(sm_as, 1)
            @blk.vector
            def _(v):
                for o in range(32):
                    for tt in range(2):
                        j = o * 2 + tt
                        v.wait_ge(sm_mm, mm0 + j + 1)
                        v.tensor_scalar_mul(ost[j % 2][:], pp[2 * (j % 2) + 1][:], s2[:, o:o + 1])
                        v.wait_ge(sm_as, as0 + j + 1)
                        v.tensor_mul(hh[:, o * TH + tt * 512: o * TH + (tt + 1) * 512],
                                     atmp[j % 2][:], ost[j % 2][:]).then_inc(sm_ev, 1)
        c.semc["mm"] = mm0 + 64; c.semc["ev"] = ev0 + 64
        c.semc["as"] = as0 + 64; c.semc["wl"] = wl0 + 32 * 32

        for (dwname, src_t, dst_t, scol) in (("l1f_h1T", hh, h2, h1s),
                                             ("l1f_h2T", h2, hh, h2s)):
            mm0, ev0, wl0 = c.semc["mm"], c.semc["ev"], c.semc["wl"]
            dw = dW[dwname]
            with nc.Block() as blk:
                @blk.gpsimd
                def _(g):
                    for o in range(32):
                        if o >= 2:
                            g.wait_ge(sm_mm, mm0 + 2 * (o - 1))
                        g.dma_start(wbig[o % 2][:], wslice(dw, o * 128, 128)).then_inc(sm_wl, 16)
                @blk.tensor
                def _(t):
                    for o in range(32):
                        t.wait_ge(sm_wl, wl0 + 16 * (o + 1))
                        for tt in range(2):
                            j = o * 2 + tt
                            if j >= 2:
                                t.wait_ge(sm_ev, ev0 + j - 1)
                            for kc in range(32):
                                mmi = t.matmul(pp[j % 2][:],
                                               wbig[o % 2][:, kc * 128:(kc + 1) * 128],
                                               src_t[:, kc * TH + tt * 512: kc * TH + (tt + 1) * 512],
                                               start=(kc == 0), stop=(kc == 31))
                            mmi.then_inc(sm_mm, 1)
                @blk.scalar
                def _(s):
                    for o in range(32):
                        for tt in range(2):
                            j = o * 2 + tt
                            s.wait_ge(sm_mm, mm0 + j + 1)
                            s.activation(dst_t[:, o * TH + tt * 512: o * TH + (tt + 1) * 512],
                                         pp[j % 2][:], AF.Gelu,
                                         scale=scol[:, o:o + 1]).then_inc(sm_ev, 1)
            c.semc["mm"] = mm0 + 64; c.semc["ev"] = ev0 + 64
            c.semc["wl"] = wl0 + 16 * 32

        st_sem, st0 = c.sem("st"), c.semc["st"]
        mm0, ev0, wl0, dma0 = c.semc["mm"], c.semc["ev"], c.semc["wl"], c.semc["dma"]
        with nc.Block() as blk:
            @blk.gpsimd
            def _(g):
                for o in range(8):
                    if o >= 2:
                        g.wait_ge(sm_mm, mm0 + 2 * (o - 1))
                    g.dma_start(wbig[o % 2][:], wslice(dW["l1f_w3T"], o * 128, 128)).then_inc(sm_wl, 16)
                    for tt in range(2):
                        j = o * 2 + tt
                        if j >= 2:
                            g.wait_ge(sm_ev, ev0 + j - 1)
                        g.dma_start(xt[j % 2][:],
                                    dx3.ap()[o * 128:(o + 1) * 128,
                                             t0 + tt * 512:t0 + (tt + 1) * 512]).then_inc(dma, 16)
            @blk.tensor
            def _(t):
                for o in range(8):
                    t.wait_ge(sm_wl, wl0 + 16 * (o + 1))
                    for tt in range(2):
                        j = o * 2 + tt
                        if j >= 2:
                            t.wait_ge(sm_ev, ev0 + j - 1)
                        for kc in range(32):
                            mmi = t.matmul(pp[j % 2][:],
                                           wbig[o % 2][:, kc * 128:(kc + 1) * 128],
                                           hh[:, kc * TH + tt * 512: kc * TH + (tt + 1) * 512],
                                           start=(kc == 0), stop=(kc == 31))
                        mmi.then_inc(sm_mm, 1)
            @blk.vector
            def _(v):
                for o in range(8):
                    for tt in range(2):
                        j = o * 2 + tt
                        v.wait_ge(sm_mm, mm0 + j + 1)
                        v.wait_ge(dma, dma0 + 16 * (j + 1))
                        if j >= 2:
                            v.wait_ge(st_sem, st0 + 16 * (j - 1))
                        v.scalar_tensor_tensor(ost[j % 2][:], pp[j % 2][:], w3s[:, o:o + 1],
                                               xt[j % 2][:], ALU.mult, ALU.add).then_inc(sm_ev, 1)
            @blk.sync
            def _(sy):
                for o in range(8):
                    for tt in range(2):
                        j = o * 2 + tt
                        sy.wait_ge(sm_ev, ev0 + j + 1)
                        sy.dma_start(dout_x4.ap()[o * 128:(o + 1) * 128,
                                                  t0 + tt * 512:t0 + (tt + 1) * 512],
                                     ost[j % 2][:]).then_inc(st_sem, 16)
                sy.wait_ge(st_sem, st0 + 16 * 16)
        c.semc["mm"] = mm0 + 16; c.semc["ev"] = ev0 + 16
        c.semc["wl"] = wl0 + 16 * 8; c.semc["dma"] = dma0 + 16 * 16
        c.semc["st"] = st0 + 16 * 16



def bcol(ap_col, n):
    return bass.AP(ap_col.tensor, ap_col.offset, [list(ap_col.ap[0]), [0, n]])


def brep(ap2d, rep, inner):
    return bass.AP(ap2d.tensor, ap2d.offset,
                   [list(ap2d.ap[0]), [0, rep], list(ap2d.ap[1])[:1] + [inner]])


def emit_l0ffn(c, dx1, dW, dout_x2):
    nc = c.nc
    TH = 1024
    ones_bf = c.sb("m_ones", [128, 128], BF16)
    onescol = c.sb("m_onec", [128, 1])
    gw = c.sb("m_gw", [128, 8 * 8])
    oh8 = c.sb("m_oh8", [8, 8 * 128])
    b1c = c.sb("m_b1c", [128, 8 * 16])
    b2sb = c.sb("m_b2", [8, 1024])
    idf = c.sb("m_idf", [128, 128])
    loadblk(c, [(ones_bf[:], dW["ones_bf"].ap()), (onescol[:], dW["onescol"].ap()),
                (gw[:].rearrange("p (kc e) -> p kc e", kc=8),
                 dW["moe_gw"].ap().rearrange("(kc p) e -> p kc e", p=128)),
                (oh8[:], dW["moe_oh8"].ap()), (b1c[:], dW["moe_b1c"].ap()),
                (b2sb[:], dW["moe_b2"].ap()), (idf[:], dW["ident_f32"].ap())])

    xn = c.sb("m_xn", [128, 8 * TH], BF16)
    hsb = c.sb("m_h", [128, 16 * 512], BF16)
    acc = c.sb("m_acc", [128, 8 * TH])
    w1e = [c.sb(f"m_w1e{i}", [128, 8 * 2048], BF16) for i in range(2)]
    w2e = c.sb("m_w2e", [128, 16 * 1024], BF16)
    cwb = c.sb("m_cwb", [128, TH], BF16)
    cwT = c.sb("m_cwT", [8, TH])
    cw_all = c.sb("m_cwall", [128, 8 * 8])
    negcs = c.sb("m_negcs", [1, 8])
    ga = c.sb("m_ga", [128, 64]); gb = c.sb("m_gb", [128, 64])
    gsc = c.sb("m_gsc", [128, 8]); gsm = c.sb("m_gsm", [128, 8])
    gss = c.sb("m_gss", [128, 8])

    gx = [c.sb(f"m_gx{i}", [128, 8 * 128]) for i in range(2)]
    P = get_pool(c)
    S = dict(sq=acc[:, 3072:7168].bitcast(BF16),
             ps_a=P["psa"], ps_b=P["psb"],
             mu=acc[:, 0:1024], va=acc[:, 1024:2048],
             rstd=acc[:, 2048:3072], st=None)
    xt = [c.sb(f"m_xt{i}", [128, 512]) for i in range(2)]
    ost = [c.sb(f"m_ost{i}", [128, 512]) for i in range(2)]
    S["st"] = ost[0]
    pp = P["pp"]
    psg = S["ps_a"]
    sm_mm, sm_ev, sm_as, sm_wl = (c.sem("mm"), c.sem("ev"), c.sem("as"), c.sem("wl"))
    sm_v = c.sem("vv")
    dma = c.sem("dma")
    st_sem = c.sem("st")

    with nc.Block() as blk:
        @blk.tensor
        def _(t):
            for kc in range(8):
                mmi = t.matmul(psg[0:1, 0:8], onescol[:], gw[:, kc * 8:(kc + 1) * 8],
                               start=(kc == 0), stop=(kc == 7))
            mmi.then_inc(sm_mm, 1)
        @blk.vector
        def _(v):
            v.wait_ge(sm_mm, c.semc["mm"] + 1)
            v.tensor_single_scalar(negcs[:], psg[0:1, 0:8], -1.0, ALU.mult)
    c.semc["mm"] += 1

    for half in range(2):
        t0 = half * TH

        loadblk(c, [(xn[:].rearrange("p (kc t) -> p kc t", kc=8), fview(dx1, t0, TH))])
        emit_ln(c, lambda kc: xn[:, kc * TH:(kc + 1) * TH], ones_bf, S, TH)

        mm0, ev0, dma0 = (c.semc[s] for s in ("mm", "ev", "dma"))
        mu, rstd = S["mu"], S["rstd"]
        with nc.Block() as blk:
            @blk.gpsimd
            def _(g):
                for ti in range(8):
                    if ti >= 2:
                        g.wait_ge(sm_mm, mm0 + ti - 1)
                    g.dma_start(gx[ti % 2][:].rearrange("p (kc t) -> p kc t", kc=8),
                                dx1.ap()[:, t0 + ti * 128:t0 + (ti + 1) * 128]
                                .rearrange("(kc p) t -> p kc t", p=128)).then_inc(dma, 16)
            @blk.tensor
            def _(t):
                for ti in range(8):
                    t.wait_ge(dma, dma0 + 16 * (ti + 1))
                    po = ti * 16
                    for kc in range(8):
                        t.matmul(psg[:, po:po + 8], gx[ti % 2][:, kc * 128:(kc + 1) * 128],
                                 gw[:, kc * 8:(kc + 1) * 8], start=(kc == 0), stop=False)
                    t.matmul(psg[:, po:po + 8], mu[0:1, ti * 128:(ti + 1) * 128],
                             negcs[:], start=False, stop=True)
                    t.matmul(psg[:, po + 8:po + 9], rstd[0:1, ti * 128:(ti + 1) * 128],
                             onescol[0:1, 0:1], start=True, stop=True).then_inc(sm_mm, 1)
        c.semc["mm"] = mm0 + 8; c.semc["dma"] = dma0 + 16 * 8
        smf = c.sem("fv"); f0 = c.semc["fv"]
        with nc.Block() as blk:
            @blk.vector
            def _(v):
                fc = f0
                for ti in range(8):
                    ins = v.tensor_copy(gsc[:, ti:ti + 1], psg[:, ti * 16 + 8:ti * 16 + 9])
                ins.then_inc(smf, 1); fc += 1
                v.wait_ge(smf, fc)
                for ti in range(8):
                    ins = v.tensor_tensor(ga[:, ti * 8:(ti + 1) * 8], psg[:, ti * 16:ti * 16 + 8],
                                          bcol(gsc[:, ti:ti + 1], 8), ALU.mult)
                for ti in range(8):
                    ins = v.reduce_max(gsm[:, ti:ti + 1], ga[:, ti * 8:(ti + 1) * 8], axis=AX)
                ins.then_inc(smf, 1); fc += 1
                v.wait_ge(smf, fc)
                for ti in range(8):
                    v.tensor_tensor(ga[:, ti * 8:(ti + 1) * 8], ga[:, ti * 8:(ti + 1) * 8],
                                    bcol(gsm[:, ti:ti + 1], 8), ALU.subtract)
        c.semc["fv"] = f0 + 2
        with nc.Block() as blk:
            @blk.scalar
            def _(s):
                for ti in range(8):
                    s.activation(gb[:, ti * 8:(ti + 1) * 8], ga[:, ti * 8:(ti + 1) * 8],
                                 AF.Exp)
        f0 = c.semc["fv"]
        with nc.Block() as blk:
            @blk.vector
            def _(v):
                fc = f0
                def fence(ins):
                    nonlocal fc
                    ins.then_inc(smf, 1); fc += 1
                    v.wait_ge(smf, fc)
                for ti in range(8):
                    ins = v.reduce_sum(gss[:, ti:ti + 1], gb[:, ti * 8:(ti + 1) * 8], axis=AX)
                for ti in range(8):
                    ins = v.reciprocal(gsc[:, ti:ti + 1], gss[:, ti:ti + 1])
                fence(ins)
                for ti in range(8):
                    sl = slice(ti * 8, (ti + 1) * 8)
                    v.tensor_tensor(ga[:, sl], gb[:, sl], bcol(gsc[:, ti:ti + 1], 8),
                                    ALU.mult)
                for ti in range(8):
                    ins = v.reduce_max(gsm[:, ti:ti + 1], ga[:, ti * 8:(ti + 1) * 8], axis=AX)
                fence(ins)
                for ti in range(8):
                    sl = slice(ti * 8, (ti + 1) * 8)
                    v.tensor_tensor(gb[:, sl], ga[:, sl], bcol(gsm[:, ti:ti + 1], 8),
                                    ALU.is_ge)
                for ti in range(8):
                    sl = slice(ti * 8, (ti + 1) * 8)
                    v.scalar_tensor_tensor(gb[:, sl], gb[:, sl], -1e9, ga[:, sl],
                                           ALU.mult, ALU.add)
                for ti in range(8):
                    ins = v.reduce_max(gsm[:, ti:ti + 1], gb[:, ti * 8:(ti + 1) * 8], axis=AX)
                fence(ins)
                for ti in range(8):
                    sl = slice(ti * 8, (ti + 1) * 8)
                    v.tensor_tensor(gb[:, sl], ga[:, sl], bcol(gsm[:, ti:ti + 1], 8),
                                    ALU.is_ge)
                for ti in range(8):
                    sl = slice(ti * 8, (ti + 1) * 8)
                    v.tensor_mul(gb[:, sl], ga[:, sl], gb[:, sl])
                for ti in range(8):
                    v.reduce_sum(gss[:, ti:ti + 1], gb[:, ti * 8:(ti + 1) * 8], axis=AX)
                for ti in range(8):
                    ins = v.tensor_single_scalar(gss[:, ti:ti + 1], gss[:, ti:ti + 1],
                                                 1e-8, ALU.add)
                for ti in range(8):
                    ins = v.reciprocal(gsc[:, ti:ti + 1], gss[:, ti:ti + 1])
                fence(ins)
                for ti in range(8):
                    sl = slice(ti * 8, (ti + 1) * 8)
                    v.tensor_tensor(cw_all[:, sl], gb[:, sl],
                                    bcol(gsc[:, ti:ti + 1], 8), ALU.mult)
        c.semc["fv"] = fc if False else c.semc["fv"] + 4
        mm0, ev0 = c.semc["mm"], c.semc["ev"]
        with nc.Block() as blk:
            @blk.tensor
            def _(t):
                for ti in range(8):
                    if ti >= 2:
                        t.wait_ge(sm_ev, ev0 + ti - 1)
                    t.transpose(pp[ti % 2][0:8, 0:128], cw_all[:, ti * 8:(ti + 1) * 8],
                                idf[:]).then_inc(sm_mm, 1)
            @blk.vector
            def _(v):
                for ti in range(8):
                    v.wait_ge(sm_mm, mm0 + ti + 1)
                    v.tensor_copy(cwT[:, ti * 128:(ti + 1) * 128],
                                  pp[ti % 2][0:8, 0:128]).then_inc(sm_ev, 1)
        c.semc["mm"] = mm0 + 8; c.semc["ev"] = ev0 + 8

        if half == 0:
            pass
        mm_base = c.semc["mm"]; wl0 = c.semc["wl"]
        loadblk(c, [(w1e[0][:].rearrange("p (kc o) -> p kc o", kc=8),
                     dW["moe_W1f"].ap()[0:1024, :].rearrange("(kc p) o -> p kc o", p=128)),
                    (w2e[:].rearrange("p (kc o) -> p kc o", kc=16),
                     dW["moe_W2f"].ap()[0:2048, :].rearrange("(kc p) o -> p kc o", p=128))])
        for e in range(8):
            mm0, ev0 = c.semc["mm"], c.semc["ev"]
            with nc.Block() as blk:
                @blk.tensor
                def _(t):
                    for tt in range(2):
                        t.matmul(pp[tt][:], oh8[:, e * 128:(e + 1) * 128],
                                 cwT[:, tt * 512:(tt + 1) * 512],
                                 start=True, stop=True).then_inc(sm_mm, 1)
                @blk.vector
                def _(v):
                    for tt in range(2):
                        v.wait_ge(sm_mm, mm0 + tt + 1)
                        v.tensor_copy(cwb[:, tt * 512:(tt + 1) * 512], pp[tt][:])
            c.semc["mm"] = mm0 + 2; c.semc["ev"] = ev0

            mm0, ev0, as0, v0, wl0 = (c.semc[s] for s in ("mm", "ev", "as", "vv", "wl"))
            with nc.Block() as blk:
                @blk.gpsimd
                def _(g):
                    if e < 7:
                        g.dma_start(w1e[(e + 1) % 2][:].rearrange("p (kc o) -> p kc o", kc=8),
                                    dW["moe_W1f"].ap()[(e + 1) * 1024:(e + 2) * 1024, :]
                                    .rearrange("(kc p) o -> p kc o", p=128)).then_inc(sm_wl, 16)
                        g.wait_ge(sm_mm, mm0 + 48)
                        g.dma_start(w2e[:].rearrange("p (kc o) -> p kc o", kc=16),
                                    dW["moe_W2f"].ap()[(e + 1) * 2048:(e + 2) * 2048, :]
                                    .rearrange("(kc p) o -> p kc o", p=128)).then_inc(sm_wl, 16)
                @blk.tensor
                def _(t):
                    if e > 0:
                        t.wait_ge(sm_wl, wl0 - 16)
                    for tt in range(2):
                        for ho in range(16):
                            jh = tt * 16 + ho
                            if jh >= 2:
                                t.wait_ge(sm_as, as0 + jh - 1)
                            for kc in range(8):
                                mmi = t.matmul(pp[2 + jh % 2][:],
                                               w1e[e % 2][:, kc * 2048 + ho * 128: kc * 2048 + (ho + 1) * 128],
                                               xn[:, kc * TH + tt * 512: kc * TH + (tt + 1) * 512],
                                               start=(kc == 0), stop=(kc == 7))
                            mmi.then_inc(sm_mm, 1)
                        t.wait_ge(sm_v, v0 + tt + 1)
                        if tt == 0 and e > 0:
                            t.wait_ge(sm_wl, wl0)
                        for oo in range(8):
                            jw = tt * 8 + oo
                            if jw >= 2:
                                t.wait_ge(sm_ev, ev0 + jw - 1)
                            for kc in range(16):
                                mmi = t.matmul(pp[jw % 2][:],
                                               w2e[:, kc * 1024 + oo * 128: kc * 1024 + (oo + 1) * 128],
                                               hsb[:, kc * 512:(kc + 1) * 512],
                                               start=(kc == 0), stop=(kc == 15),
                                               skip_group_check=True)
                            mmi.then_inc(sm_mm, 1)
                @blk.scalar
                def _(s):
                    for tt in range(2):
                        for ho in range(16):
                            jh = tt * 16 + ho
                            s.wait_ge(sm_mm, mm0 + tt * 24 + ho + 1)
                            s.activation(hsb[:, ho * 512:(ho + 1) * 512], pp[2 + jh % 2][:],
                                         AF.Relu, bias=b1c[:, e * 16 + ho:e * 16 + ho + 1]
                                         ).then_inc(sm_as, 1)
                @blk.vector
                def _(v):
                    for tt in range(2):
                        v.wait_ge(sm_as, as0 + 16 * (tt + 1))
                        v.tensor_mul(hsb[:], hsb[:],
                                     brep(cwb[:, tt * 512:(tt + 1) * 512], 16, 512)
                                     ).then_inc(sm_v, 1)
                        for oo in range(8):
                            jw = tt * 8 + oo
                            v.wait_ge(sm_mm, mm0 + tt * 24 + 16 + oo + 1)
                            dsl = acc[:, oo * TH + tt * 512: oo * TH + (tt + 1) * 512]
                            if e == 0:
                                v.tensor_copy(dsl, pp[jw % 2][:]).then_inc(sm_ev, 1)
                            else:
                                v.tensor_add(dsl, dsl, pp[jw % 2][:]).then_inc(sm_ev, 1)
            c.semc["mm"] = mm0 + 48; c.semc["ev"] = ev0 + 16
            c.semc["as"] = as0 + 32; c.semc["vv"] = v0 + 2
            if e < 7:
                c.semc["wl"] = c.semc["wl"] + 32

        mm0, ev0 = c.semc["mm"], c.semc["ev"]
        with nc.Block() as blk:
            @blk.tensor
            def _(t):
                for j in range(16):
                    oo, tt = j // 2, j % 2
                    if j >= 2:
                        t.wait_ge(sm_ev, ev0 + j - 1)
                    t.matmul(pp[j % 2][:], b2sb[0:8, oo * 128:(oo + 1) * 128],
                             cwT[:, tt * 512:(tt + 1) * 512],
                             start=True, stop=True).then_inc(sm_mm, 1)
            @blk.vector
            def _(v):
                for j in range(16):
                    oo, tt = j // 2, j % 2
                    v.wait_ge(sm_mm, mm0 + j + 1)
                    dsl = acc[:, oo * TH + tt * 512: oo * TH + (tt + 1) * 512]
                    v.tensor_add(dsl, dsl, pp[j % 2][:]).then_inc(sm_ev, 1)
        c.semc["mm"] = mm0 + 16; c.semc["ev"] = ev0 + 16

        mm0, ev0, dma0, st0 = c.semc["mm"], c.semc["ev"], c.semc["dma"], c.semc["st"]
        with nc.Block() as blk:
            @blk.gpsimd
            def _(g):
                for j in range(16):
                    oo, tt = j // 2, j % 2
                    if j >= 2:
                        g.wait_ge(sm_ev, ev0 + j - 1)
                    g.dma_start(xt[j % 2][:],
                                dx1.ap()[oo * 128:(oo + 1) * 128,
                                         t0 + tt * 512:t0 + (tt + 1) * 512]).then_inc(dma, 16)
            @blk.vector
            def _(v):
                for j in range(16):
                    oo, tt = j // 2, j % 2
                    v.wait_ge(dma, dma0 + 16 * (j + 1))
                    if j >= 2:
                        v.wait_ge(st_sem, st0 + 16 * (j - 1))
                    asl = acc[:, oo * TH + tt * 512: oo * TH + (tt + 1) * 512]
                    v.tensor_add(ost[j % 2][:], asl, xt[j % 2][:])
                    v.tensor_add(ost[j % 2][:], ost[j % 2][:],
                                 xn[:, oo * TH + tt * 512: oo * TH + (tt + 1) * 512]
                                 ).then_inc(sm_ev, 1)
            @blk.sync
            def _(sy):
                for j in range(16):
                    oo, tt = j // 2, j % 2
                    sy.wait_ge(sm_ev, ev0 + j + 1)
                    sy.dma_start(dout_x2.ap()[oo * 128:(oo + 1) * 128,
                                              t0 + tt * 512:t0 + (tt + 1) * 512],
                                 ost[j % 2][:]).then_inc(st_sem, 16)
                sy.wait_ge(st_sem, st0 + 16 * 16)
        c.semc["ev"] = ev0 + 16; c.semc["dma"] = dma0 + 16 * 16
        c.semc["st"] = st0 + 16 * 16
    return dict(xn=xn, acc=acc, cw_all=cw_all, cwT=cwT, hsb=hsb,
                mu=S["mu"], rstd=S["rstd"], negcs=negcs, cwb=cwb,
                ga=ga, gb=gb, gsc=gsc, gsm=gsm, gss=gss)


def dumpblk(c, pairs):
    loadblk(c, pairs)



def emit_l1attn(c, dx2, dxh63, dW, dout_x3):
    nc = c.nc
    TF = 2176
    ones_bf = c.sb("a_ones", [128, 128], BF16)
    onesrow_bf = c.sb("a_onesr", [1, 128], BF16)
    bq = c.sb("a_bq", [128, 8]); bk = c.sb("a_bk", [128, 8])
    bout = c.sb("a_bout", [128, 8])
    bv = c.sb("a_bv", [1, 1024], BF16)
    maskR = c.sb("a_maskR", [128, 256]); mask0 = c.sb("a_mask0", [128, 256])
    idb = c.sb("a_idb", [128, 128], BF16)
    wv = c.sb("a_wv", [128, 8 * 1024], BF16)
    wout = c.sb("a_wout", [128, 8 * 1024], BF16)
    loadblk(c, [(ones_bf[:], dW["ones_bf"].ap()), (onesrow_bf[:], dW["onesrow_bf"].ap()),
                (bq[:], dW["l1a_bq"].ap()), (bk[:], dW["l1a_bk"].ap()),
                (bout[:], dW["l1a_bout"].ap()),
                (bv[:], dW["l1a_bv"].ap()),
                (maskR[:], dW["l1a_maskR"].ap()), (mask0[:], dW["l1a_mask0"].ap()),
                (idb[:], dW["ident_bf"].ap()),
                (wv[:].rearrange("p (kc o) -> p kc o", kc=8),
                 dW["l1a_wvT"].ap().rearrange("(kc p) o -> p kc o", p=128)),
                (wout[:].rearrange("p (kc o) -> p kc o", kc=8),
                 dW["l1a_woutT"].ap().rearrange("(kc p) o -> p kc o", p=128))])

    xn2 = c.sb("a_xn", [128, 8 * TF], BF16)
    q_sb = c.sb("a_q", [128, 8 * 2048], BF16)
    k_sb = c.sb("a_k", [128, 8 * TF], BF16)
    v_sb = c.sb("a_v", [128, 17 * 1024], BF16)
    ssb = c.sb("a_ssb", [128, 256])
    P_sb = [c.sb(f"a_P{i}", [128, 256], BF16) for i in range(2)]
    pn = [c.sb(f"a_pn{i}", [128, 256], BF16) for i in range(2)]
    ptsb = [c.sb(f"a_pt{i}", [128, 256], BF16) for i in range(2)]
    o_sb = c.sb("a_o", [128, 8 * 128], BF16)
    sc = c.sb("a_sc", [128, 8])
    x2t = c.sb("a_x2t", [128, 8 * 128])

    ost = [c.sb(f"a_ost{i}", [128, 128]) for i in range(2)]
    wbq = [c.sb(f"a_wbq{i}", [128, 8 * 128], BF16) for i in range(2)]
    P = get_pool(c)
    S = dict(sq=q_sb,
             ps_a=P["psa"], ps_b=P["psb"],
             mu=c.sb("a_mu", [128, 1024]), va=c.sb("a_va", [128, 1024]),
             rstd=c.sb("a_rstd", [128, 1024]), st=None)
    S["st"] = x2t
    pbf = P["pbf"]
    pp = P["pp"]
    sm_mm, sm_ev, sm_as, sm_wl = (c.sem("mm"), c.sem("ev"), c.sem("as"), c.sem("wl"))
    sm_v = c.sem("vv"); smf = c.sem("fv")
    dma = c.sem("dma"); st_sem = c.sem("st")

    with nc.Block() as blk:
        @blk.gpsimd
        def _(g):
            g.memset(xn2[:].rearrange("p (kc t) -> p kc t", kc=8)[:, :, 0:65], 0.0)
    loadblk(c, [(xn2[:].rearrange("p (kc t) -> p kc t", kc=8)[:, :, 65:128],
                 dxh63.rearrange("(kc p) t -> p kc t", p=128)),
                (xn2[:].rearrange("p (kc t) -> p kc t", kc=8)[:, :, 128:TF],
                 fview(dx2, 0, 2048))])
    for (off, Tt) in ((0, 1024), (1024, 1024), (2048, 128)):
        emit_ln(c, lambda kc, off=off, Tt=Tt: xn2[:, kc * TF + off: kc * TF + off + Tt],
                ones_bf, S, Tt)

    for (wname, dst, bias, Tt, toff) in (("l1a_wqT", q_sb, bq, 2048, 128),
                                         ("l1a_wkT", k_sb, bk, TF, 0)):
        mm0, ev0, wl0 = c.semc["mm"], c.semc["ev"], c.semc["wl"]
        ntt = (Tt + 511) // 512
        with nc.Block() as blk:
            @blk.gpsimd
            def _(g):
                for o in range(8):
                    if o >= 2:
                        g.wait_ge(sm_mm, mm0 + ntt * (o - 1))
                    g.dma_start(wbq[o % 2][:], wslice(dW[wname], o * 128, 128)).then_inc(sm_wl, 16)
            @blk.tensor
            def _(t):
                for o in range(8):
                    t.wait_ge(sm_wl, wl0 + 16 * (o + 1))
                    for tt in range(ntt):
                        w = min(512, Tt - tt * 512)
                        j = o * ntt + tt
                        if j >= 2:
                            t.wait_ge(sm_ev, ev0 + j - 1)
                        for kc in range(8):
                            mmi = t.matmul(pp[j % 2][:, 0:w],
                                           wbq[o % 2][:, kc * 128:(kc + 1) * 128],
                                           xn2[:, kc * TF + toff + tt * 512: kc * TF + toff + tt * 512 + w],
                                           start=(kc == 0), stop=(kc == 7))
                        mmi.then_inc(sm_mm, 1)
            @blk.vector
            def _(v):
                for o in range(8):
                    for tt in range(ntt):
                        w = min(512, Tt - tt * 512)
                        j = o * ntt + tt
                        v.wait_ge(sm_mm, mm0 + j + 1)
                        v.tensor_scalar(dst[:, o * Tt + tt * 512: o * Tt + tt * 512 + w],
                                        pp[j % 2][:, 0:w], bias[:, o:o + 1], None,
                                        ALU.add).then_inc(sm_ev, 1)
        c.semc["mm"] = mm0 + 8 * ntt; c.semc["ev"] = ev0 + 8 * ntt
        c.semc["wl"] = wl0 + 16 * 8

    mm0, ev0 = c.semc["mm"], c.semc["ev"]
    with nc.Block() as blk:
        @blk.tensor
        def _(t):
            for ti in range(17):
                for oh in range(2):
                    j = ti * 2 + oh
                    if j >= 2:
                        t.wait_ge(sm_ev, ev0 + j - 1)
                    for kc in range(8):
                        t.matmul(pp[j % 2][:],
                                 xn2[:, kc * TF + ti * 128:kc * TF + (ti + 1) * 128],
                                 wv[:, kc * 1024 + oh * 512: kc * 1024 + (oh + 1) * 512],
                                 start=(kc == 0), stop=False)
                    mmi = t.matmul(pp[j % 2][:], onesrow_bf[:],
                                   bv[:, oh * 512:(oh + 1) * 512], start=False, stop=True)
                    mmi.then_inc(sm_mm, 1)
        @blk.vector
        def _(v):
            for ti in range(17):
                for oh in range(2):
                    j = ti * 2 + oh
                    v.wait_ge(sm_mm, mm0 + j + 1)
                    v.tensor_copy(v_sb[:, ti * 1024 + oh * 512: ti * 1024 + (oh + 1) * 512],
                                  pp[j % 2][:]).then_inc(sm_ev, 1)
    c.semc["mm"] = mm0 + 34; c.semc["ev"] = ev0 + 34

    SCALE = 0.125
    for qb in range(16):
        mm0, ev0, as0, v0, f0, dma0, st0 = (c.semc[s] for s in
                                            ("mm", "ev", "as", "vv", "fv", "dma", "st"))
        mask = mask0 if qb == 0 else maskR
        with nc.Block() as blk:
            @blk.gpsimd
            def _(g):
                g.dma_start(x2t[:].rearrange("p (kc t) -> p kc t", kc=8),
                            dx2.ap()[:, qb * 128:(qb + 1) * 128]
                            .rearrange("(kc p) t -> p kc t", p=128)).then_inc(dma, 16)
            @blk.tensor
            def _(t):
                for h in range(16):
                    kc, pb = h // 2, (h % 2) * 64
                    if h >= 2:
                        t.wait_ge(sm_v, v0 + 3 * (h - 2) + 1)
                    t.matmul(S["ps_a"][:, (h % 2) * 512:(h % 2) * 512 + 256],
                             q_sb[pb:pb + 64, kc * 2048 + qb * 128: kc * 2048 + (qb + 1) * 128],
                             k_sb[pb:pb + 64, kc * TF + qb * 128: kc * TF + qb * 128 + 256],
                             start=True, stop=True).then_inc(sm_mm, 1)
                    t.wait_ge(sm_v, v0 + 3 * h + 2)
                    t.transpose(pbf[:, (h % 2) * 1024:(h % 2) * 1024 + 128],
                                pn[h % 2][:, 0:128], idb[:])
                    t.transpose(pbf[:, (h % 2) * 1024 + 128:(h % 2) * 1024 + 256],
                                pn[h % 2][:, 128:256], idb[:]).then_inc(sm_mm, 1)
                    t.wait_ge(sm_v, v0 + 3 * h + 3)
                    if h >= 1:
                        t.wait_ge(sm_ev, ev0 + h)
                    for ki in range(2):
                        mmi = t.matmul(S["ps_b"][0:64, (h % 2) * 512:(h % 2) * 512 + 128],
                                       v_sb[:, (qb + ki) * 1024 + h * 64: (qb + ki) * 1024 + (h + 1) * 64],
                                       ptsb[h % 2][:, ki * 128:(ki + 1) * 128],
                                       start=(ki == 0), stop=(ki == 1))
                    mmi.then_inc(sm_mm, 1)
                for o in range(8):
                    t.wait_ge(sm_ev, ev0 + 16 + max(0, o - 1))
                    for kc in range(8):
                        mmi = t.matmul(pp[o % 2][:, 0:128],
                                       wout[:, kc * 1024 + o * 128: kc * 1024 + (o + 1) * 128],
                                       o_sb[:, kc * 128:(kc + 1) * 128],
                                       start=(kc == 0), stop=(kc == 7))
                    mmi.then_inc(sm_mm, 1)
            @blk.vector
            def _(v):
                fc = f0
                for h in range(16):
                    v.wait_ge(sm_mm, mm0 + 3 * h + 1)
                    v.scalar_tensor_tensor(ssb[:], S["ps_a"][:, (h % 2) * 512:(h % 2) * 512 + 256],
                                           SCALE, mask[:], ALU.mult, ALU.add)
                    ins = v.reduce_max(sc[:, 0:1], ssb[:], axis=AX)
                    ins.then_inc(smf, 1); fc += 1
                    v.wait_ge(smf, fc)
                    v.tensor_tensor(ssb[:], ssb[:], bcol(sc[:, 0:1], 256),
                                    ALU.subtract).then_inc(sm_v, 1)
                    v.wait_ge(sm_as, as0 + h + 1)
                    ins = v.reduce_sum(sc[:, 1:2], P_sb[h % 2][:], axis=AX)
                    ins.then_inc(smf, 1); fc += 1
                    v.wait_ge(smf, fc)
                    ins = v.reciprocal(sc[:, 2:3], sc[:, 1:2])
                    ins.then_inc(smf, 1); fc += 1
                    v.wait_ge(smf, fc)
                    v.tensor_tensor(pn[h % 2][:], P_sb[h % 2][:], bcol(sc[:, 2:3], 256),
                                    ALU.mult).then_inc(sm_v, 1)
                    v.wait_ge(sm_mm, mm0 + 3 * h + 2)
                    v.tensor_copy(ptsb[h % 2][:, 0:128], pbf[:, (h % 2) * 1024:(h % 2) * 1024 + 128])
                    v.tensor_copy(ptsb[h % 2][:, 128:256],
                                  pbf[:, (h % 2) * 1024 + 128:(h % 2) * 1024 + 256]
                                  ).then_inc(sm_v, 1)
                    v.wait_ge(sm_mm, mm0 + 3 * h + 3)
                    v.tensor_copy(o_sb[(h % 2) * 64:(h % 2) * 64 + 64,
                                       (h // 2) * 128:(h // 2) * 128 + 128],
                                  S["ps_b"][0:64, (h % 2) * 512:(h % 2) * 512 + 128]
                                  ).then_inc(sm_ev, 1)
                for o in range(8):
                    v.wait_ge(sm_mm, mm0 + 48 + o + 1)
                    v.wait_ge(dma, dma0 + 16)
                    if o >= 2:
                        v.wait_ge(st_sem, st0 + 16 * (o - 1))
                    v.tensor_scalar(ost[o % 2][:], pp[o % 2][:, 0:128], bout[:, o:o + 1],
                                    None, ALU.add)
                    v.tensor_add(ost[o % 2][:], ost[o % 2][:],
                                 x2t[:, o * 128:(o + 1) * 128]).then_inc(sm_ev, 1)
            @blk.scalar
            def _(s):
                for h in range(16):
                    s.wait_ge(sm_v, v0 + 3 * h + 1)
                    if h >= 2:
                        s.wait_ge(sm_v, v0 + 3 * (h - 2) + 2)
                    s.activation(P_sb[h % 2][:], ssb[:], AF.Exp).then_inc(sm_as, 1)
            @blk.sync
            def _(sy):
                for o in range(8):
                    sy.wait_ge(sm_ev, ev0 + 16 + o + 1)
                    sy.dma_start(dout_x3.ap()[o * 128:(o + 1) * 128, qb * 128:(qb + 1) * 128],
                                 ost[o % 2][:]).then_inc(st_sem, 16)
                sy.wait_ge(st_sem, st0 + 16 * 8)
        c.semc["mm"] = mm0 + 48 + 8
        c.semc["ev"] = ev0 + 16 + 8
        c.semc["as"] = as0 + 16
        c.semc["vv"] = v0 + 48
        c.semc["fv"] = fc if False else f0 + 48
        c.semc["dma"] = dma0 + 16
        c.semc["st"] = st0 + 16 * 8


def restride(ap2d, off, step, count):
    return bass.AP(ap2d.tensor, ap2d.offset + off, [list(ap2d.ap[0]), [step, count]])


def emit_l0attn(c, dxin, dS0, dW, dout_x1, dstate, dbg=False, dbg_outs=None,
                front=False, dmask=None):
    nc = c.nc
    TH = 1024
    ones_bf = c.sb("b_ones", [128, 128], BF16)
    onesrow_f = c.sb("b_onesrf", [1, 128])
    idf = c.sb("b_idf", [128, 128])
    idb = c.sb("b_idb", [128, 128], BF16)
    triu = c.sb("b_triu", [128, 128])
    convw = c.sb("b_convw", [128, 72])
    convb = c.sb("b_convb", [128, 18])
    dtb = c.sb("b_dtb", [32, 1])
    Acol = c.sb("b_acol", [32, 1])
    Drep = c.sb("b_drep", [128, 32])
    normw = c.sb("b_normw", [128, 16])
    ones32 = c.sb("b_ones32", [32, 128])
    woutr = c.sb("b_woutr", [128, 16 * 1024], BF16)
    carry = c.sb("b_carry", [128, 2048])
    prevb = c.sb("b_prevb", [128, 2048], BF16)
    da32 = c.sb("b_da32", [32, 128])
    cdl = c.sb("b_cdl", [32, 8])
    cdflat = c.sb("b_cdflat", [1, 256])
    Lm = c.sb("b_Lm", [33, 128])
    TD = c.sb("b_TD", [128, 64])
    ETexp = c.sb("b_ETexp", [128, 32])
    ddT = c.sb("b_ddT", [128, 32])
    GmT = c.sb("b_GmT", [128, 128], BF16)
    BT = c.sb("b_BT", [128, 128], BF16)
    rms1 = c.sb("b_rms1", [128, 4])
    ost = [c.sb(f"b_ost{i}", [128, 512]) for i in range(2)]
    xres = c.sb("b_xres", [128, 8 * 128])
    zst = [c.sb(f"b_zst{i}", [128, 512], BF16) for i in range(2)]
    winb = [c.sb(f"b_winb{i}", [128, 8 * 128], BF16) for i in range(2)]
    P = get_pool(c)
    psa = P["psa"]; psb = P["psb"]; pp = P["pp"]; pbf = P["pbf"]
    dz = c.dtmp("b_dz", [2048, 2048], BF16)
    sm_mm, sm_ev, sm_as, sm_wl = (c.sem("mm"), c.sem("ev"), c.sem("as"), c.sem("wl"))
    sm_v = c.sem("vv"); smf = c.sem("fv")
    dma = c.sem("dma"); st_sem = c.sem("st")

    loadblk(c, [(ones_bf[:], dW["ones_bf"].ap()), (onesrow_f[:], dW["onesrow_f"].ap()),
                (idf[:], dW["ident_f32"].ap()), (idb[:], dW["ident_bf"].ap()),
                (triu[:], dW["mb_triu"].ap()), (convw[:], dW["mb_convw"].ap()),
                (convb[:], dW["mb_convb"].ap()), (dtb[:], dW["mb_dtb"].ap()),
                (Acol[:], dW["mb_alog"].ap()), (Drep[:], dW["mb_drep"].ap()),
                (normw[:], dW["mb_normw"].ap()), (ones32[:], dW["ones32"].ap()),
                (woutr[:].rearrange("p (kc o) -> p kc o", kc=16),
                 dW["mb_woutT"].ap().rearrange("(kc p) o -> p kc o", p=128))]
            + ([(carry[:], dS0)] if dS0 is not None else []))
    if dS0 is None:
        with nc.Block() as blk:
            @blk.gpsimd
            def _(g):
                g.memset(carry[:], 0.0)
    if dmask is not None:
        mcol = c.sb("b_mcol", [128, 1])
        loadblk(c, [(mcol[:], dmask.ap())])
        with nc.Block() as blk:
            @blk.vector
            def _(v):
                v.tensor_mul(carry[:], carry[:], bcol(mcol[:], 2048))
    with nc.Block() as blk:
        @blk.scalar
        def _(s):
            s.activation(Acol[:], Acol[:], AF.Exp)
        @blk.gpsimd
        def _(g):
            g.memset(Lm[32:33, 0:128], 1.0)
    with nc.Block() as blk:
        @blk.vector
        def _(v):
            v.tensor_single_scalar(Acol[:], Acol[:], -1.0, ALU.mult)

    for half in range(1 if dbg else 2):
        t0g = half * TH
        c.push_scope()
        x4b = c.sb(f"h_x4b{half}", [128, 16 * 1024], BF16)
        BCh = c.sb(f"h_BC{half}", [128, 2 * 1024], BF16)
        CD2 = c.sb(f"h_CD2{half}", [64, 1024])
        CDB = c.sb(f"h_CDB{half}", [128, 256])

        c.push_scope()
        xnh = c.sb(f"p1_xn{half}", [128, 8 * 1027], BF16)
        xbc = c.sb(f"p1_xbc{half}", [128, 18 * 1027], BF16)
        wzb = [c.sb(f"p1_wzb{half}_{i}", [128, 8 * 512], BF16) for i in range(2)]
        S = dict(sq=xbc, ps_a=psa, ps_b=psb,
                 mu=c.sb(f"p1_mu{half}", [128, 512]), va=c.sb(f"p1_va{half}", [128, 512]),
                 rstd=c.sb(f"p1_rstd{half}", [128, 512]), st=ost[0])
        loadblk(c, [(xnh[:].rearrange("p (kc t) -> p kc t", kc=8),
                     fview(dxin, t0g, 1027))])
        for (off, Tt) in ((0, 512), (512, 512), (1024, 3)):
            emit_ln(c, lambda kc, off=off, Tt=Tt: xnh[:, kc * 1027 + off: kc * 1027 + off + Tt],
                    ones_bf, S, Tt)

        tjobs = ((0, 512), (512, 512), (1024, 3))
        mm0, ev0, wl0 = c.semc["mm"], c.semc["ev"], c.semc["wl"]
        with nc.Block() as blk:
            @blk.gpsimd
            def _(g):
                for o in range(19):
                    if o >= 2:
                        g.wait_ge(sm_mm, mm0 + 3 * (o - 1))
                    ow = 128 if o < 18 else 32
                    g.dma_start(winb[o % 2][:, 0:8 * ow],
                                wslice(dW["mb_winT"], 2048 + o * 128, ow)).then_inc(sm_wl, 16)
            @blk.tensor
            def _(t):
                for o in range(19):
                    ow = 128 if o < 18 else 32
                    t.wait_ge(sm_wl, wl0 + 16 * (o + 1))
                    for tj, (tt0, w) in enumerate(tjobs):
                        j = o * 3 + tj
                        if j >= 2:
                            t.wait_ge(sm_ev, ev0 + j - 1)
                        for kc in range(8):
                            mmi = t.matmul(pp[j % 2][0:ow, 0:w],
                                           winb[o % 2][:, kc * ow:(kc + 1) * ow],
                                           xnh[:, kc * 1027 + tt0: kc * 1027 + tt0 + w],
                                           start=(kc == 0), stop=(kc == 7))
                        mmi.then_inc(sm_mm, 1)
            @blk.vector
            def _(v):
                for o in range(19):
                    for tj, (tt0, w) in enumerate(tjobs):
                        j = o * 3 + tj
                        v.wait_ge(sm_mm, mm0 + j + 1)
                        if o < 18:
                            ins = v.tensor_copy(xbc[:, o * 1027 + tt0: o * 1027 + tt0 + w],
                                                pp[j % 2][:, 0:w])
                        else:
                            lo = max(tt0, 3); hi = tt0 + w
                            ins = v.tensor_copy(CD2[0:32, lo - 3:hi - 3],
                                                pp[j % 2][0:32, lo - tt0:hi - tt0])
                        ins.then_inc(sm_ev, 1)
        c.semc["mm"] = mm0 + 57; c.semc["ev"] = ev0 + 57; c.semc["wl"] = wl0 + 19 * 16

        if not front:
            mm0, ev0, wl0, st0 = (c.semc[s] for s in ("mm", "ev", "wl", "st"))
            with nc.Block() as blk:
                @blk.gpsimd
                def _(g):
                    for oz in range(4):
                        if oz >= 2:
                            g.wait_ge(sm_mm, mm0 + 8 * (oz - 1))
                        g.dma_start(wzb[oz % 2][:].rearrange("p (kc o) -> p kc o", kc=8),
                                    wslice(dW["mb_winT"], oz * 512, 512)).then_inc(sm_wl, 16)
                @blk.tensor
                def _(t):
                    for oz in range(4):
                        t.wait_ge(sm_wl, wl0 + 16 * (oz + 1))
                        for ti in range(8):
                            j = oz * 8 + ti
                            if j >= 2:
                                t.wait_ge(sm_ev, ev0 + j - 1)
                            for kc in range(8):
                                mmi = t.matmul(pp[j % 2][:],
                                               xnh[:, kc * 1027 + 3 + ti * 128: kc * 1027 + 3 + (ti + 1) * 128],
                                               wzb[oz % 2][:, kc * 512:(kc + 1) * 512],
                                               start=(kc == 0), stop=(kc == 7))
                            mmi.then_inc(sm_mm, 1)
                @blk.vector
                def _(v):
                    for oz in range(4):
                        for ti in range(8):
                            j = oz * 8 + ti
                            v.wait_ge(sm_mm, mm0 + j + 1)
                            if j >= 2:
                                v.wait_ge(st_sem, st0 + 16 * (j - 1))
                            v.tensor_copy(zst[j % 2][:], pp[j % 2][:]).then_inc(sm_ev, 1)
                @blk.sync
                def _(sy):
                    for oz in range(4):
                        for ti in range(8):
                            j = oz * 8 + ti
                            sy.wait_ge(sm_ev, ev0 + j + 1)
                            sy.dma_start(dz.ap()[t0g + ti * 128:t0g + (ti + 1) * 128,
                                                 oz * 512:(oz + 1) * 512],
                                         zst[j % 2][:]).then_inc(st_sem, 16)
                    sy.wait_ge(st_sem, st0 + 16 * 32)
            c.semc["mm"] = mm0 + 32; c.semc["ev"] = ev0 + 32
            c.semc["wl"] = wl0 + 16 * 4; c.semc["st"] = st0 + 16 * 32

        v0, as0 = c.semc["vv"], c.semc["as"]
        with nc.Block() as blk:
            @blk.vector
            def _(v):
                for ct in range(18):
                    ps = psa if ct % 2 == 0 else psb
                    if ct >= 2:
                        v.wait_ge(sm_as, as0 + ct - 1)
                    v.tensor_scalar_mul(ps[:], xbc[:, ct * 1027 + 0: ct * 1027 + 1024],
                                        convw[:, ct * 4:ct * 4 + 1])
                    for k in range(1, 4):
                        ins = v.scalar_tensor_tensor(ps[:], xbc[:, ct * 1027 + k: ct * 1027 + k + 1024],
                                                     convw[:, ct * 4 + k:ct * 4 + k + 1],
                                                     ps[:], ALU.mult, ALU.add)
                    ins.then_inc(sm_v, 1)
            @blk.scalar
            def _(s):
                for ct in range(18):
                    ps = psa if ct % 2 == 0 else psb
                    s.wait_ge(sm_v, v0 + ct + 1)
                    if ct < 16:
                        dst = x4b[:, ct * 1024:(ct + 1) * 1024]
                    else:
                        dst = BCh[:, (ct - 16) * 1024:(ct - 15) * 1024]
                    s.activation(dst, ps[:], AF.Silu,
                                 bias=convb[:, ct:ct + 1]).then_inc(sm_as, 1)
        c.semc["vv"] = v0 + 18; c.semc["as"] = as0 + 18
        if dbg_outs:
            loadblk(c, [(dbg_outs["xnh"].ap(), xnh[:]), (dbg_outs["xbc"].ap(), xbc[:])])
        c.pop_scope()

        c.push_scope()
        E = c.sb(f"p2_E{half}", [128, 4096], BF16)
        Mp = c.sb(f"p2_Mp{half}", [128, 4096], BF16)
        xT = c.sb(f"p2_xT{half}", [128, 2048], BF16)
        xtf = c.sb(f"p2_xtf{half}", [128, 2048], BF16)
        Rmat = c.sb(f"p2_Rmat{half}", [33, 4096])
        IDall = c.sb(f"p2_IDall{half}", [128, 4096], BF16)
        ytmp = c.sb(f"p2_ytmp{half}", [128, 1024])
        ysb = c.sb(f"p2_ysb{half}", [128, 2048])
        sz = c.sb(f"p2_sz{half}", [128, 2048])
        yn = c.sb(f"p2_yn{half}", [128, 2048], BF16)
        ynT = c.sb(f"p2_ynT{half}", [128, 2048], BF16)
        zch = c.sb(f"p2_zch{half}", [128, 2048], BF16)
        loadblk(c, [(Rmat[0:32, :], dW["mb_oh32"].ap())])
        if not front:
            with nc.Block() as blk:
                @blk.vector
                def _(v):
                    for h in range(32):
                        v.tensor_tensor(IDall[:, h * 128:(h + 1) * 128], idb[:],
                                        bcol(Drep[:, h:h + 1], 128), ALU.mult)
        with nc.Block() as blk:
            @blk.vector
            def _(v):
                v.tensor_scalar(CD2[0:32, :], CD2[0:32, :], dtb[:], None, ALU.add)
        with nc.Block() as blk:
            @blk.scalar
            def _(s):
                s.activation(ytmp[0:32, 0:1024], CD2[0:32, :], AF.Abs)
                s.activation(CD2[32:64, :], ytmp[0:32, 0:1024], AF.Exp, scale=f32(-1.0))
                s.activation(ytmp[0:32, 0:1024], CD2[32:64, :], AF.Ln, bias=f32(1.0))
        with nc.Block() as blk:
            @blk.vector
            def _(v):
                v.tensor_scalar(ysb[0:32, 0:1024], CD2[0:32, :], 0.0, None, ALU.max)
                v.tensor_add(CD2[0:32, :], ysb[0:32, 0:1024], ytmp[0:32, 0:1024])
        with nc.Block() as blk:
            @blk.vector
            def _(v):
                for ci in range(8):
                    v.tensor_scalar_mul(da32[:], CD2[0:32, ci * 128:(ci + 1) * 128], Acol[:])
                    v.tensor_tensor_scan(CD2[32:64, ci * 128:(ci + 1) * 128],
                                         da32[:], ones32[:], 0.0, ALU.add, ALU.mult)
        with nc.Block() as blk:
            @blk.scalar
            def _(s):
                s.activation(cdl[:], restride(CD2[32:64, 0:1024], 127, 128, 8), AF.Exp)
        dma0 = c.semc["dma"]
        with nc.Block() as blk:
            @blk.gpsimd
            def _(g):
                g.dma_start(cdflat[:].rearrange("p (a b) -> p a b", a=32), cdl[:]).then_inc(dma, 16)
                g.wait_ge(dma, dma0 + 16)
        c.semc["dma"] = dma0 + 16
        mm0 = c.semc["mm"]
        with nc.Block() as blk:
            @blk.tensor
            def _(t):
                t.matmul(psa[:, 0:256], onesrow_f[:], cdflat[:],
                         start=True, stop=True).then_inc(sm_mm, 1)
            @blk.vector
            def _(v):
                v.wait_ge(sm_mm, mm0 + 1)
                v.tensor_copy(CDB[:], psa[:, 0:256])
        c.semc["mm"] = mm0 + 1

        for ci in range(8):
            g0 = t0g + ci * 128
            csl = slice(ci * 128, (ci + 1) * 128)
            mm0, v0 = c.semc["mm"], c.semc["vv"]
            with nc.Block() as blk:
                @blk.tensor
                def _(t):
                    t.transpose(pp[0][:, 0:64], CD2[:, csl], idf[0:64, 0:64]).then_inc(sm_mm, 1)
                    for b in range(17):
                        if b >= 2:
                            t.wait_ge(sm_v, v0 + b - 1)
                        src = (x4b[:, b * 1024 + csl.start:b * 1024 + csl.stop] if b < 16
                               else BCh[:, csl])
                        t.transpose(pbf[:, (b % 2) * 1024:(b % 2) * 1024 + 128],
                                    src, idb[:]).then_inc(sm_mm, 1)
                @blk.vector
                def _(v):
                    v.wait_ge(sm_mm, mm0 + 1)
                    v.tensor_copy(TD[:], pp[0][:, 0:64])
                    for b in range(17):
                        v.wait_ge(sm_mm, mm0 + b + 2)
                        dst = xT[:, b * 128:(b + 1) * 128] if b < 16 else BT[:]
                        v.tensor_copy(dst, pbf[:, (b % 2) * 1024:(b % 2) * 1024 + 128]
                                      ).then_inc(sm_v, 1)
            c.semc["mm"] = mm0 + 18; c.semc["vv"] = v0 + 17

            dma0 = c.semc["dma"]
            with nc.Block() as blk:
                @blk.vector
                def _(v):
                    v.tensor_single_scalar(Lm[0:32, :], CD2[32:64, csl], -1.0, ALU.mult)
                @blk.gpsimd
                def _(g):
                    g.dma_start(Rmat[32:33, :].rearrange("p (a b) -> p a b", a=32),
                                CD2[32:64, csl]).then_inc(dma, 16)
                    g.wait_ge(dma, dma0 + 16)
                @blk.scalar
                def _(s):
                    s.activation(ETexp[:], TD[:, 32:64], AF.Exp)
            c.semc["dma"] = dma0 + 16

            mm0, v0, as0 = c.semc["mm"], c.semc["vv"], c.semc["as"]
            with nc.Block() as blk:
                @blk.tensor
                def _(t):
                    for j in range(8):
                        if j >= 4:
                            t.wait_ge(sm_v, v0 + j - 3)
                        ds = (psa if j % 4 < 2 else psb)[:, (j % 2) * 512:(j % 2) * 512 + 512]
                        t.matmul(ds, Lm[:], Rmat[:, j * 512:(j + 1) * 512],
                                 start=True, stop=True).then_inc(sm_mm, 1)
                @blk.vector
                def _(v):
                    for j in range(8):
                        v.wait_ge(sm_mm, mm0 + j + 1)
                        if j >= 2:
                            v.wait_ge(sm_as, as0 + j - 1)
                        ds = (psa if j % 4 < 2 else psb)[:, (j % 2) * 512:(j % 2) * 512 + 512]
                        v.tensor_scalar(ytmp[:, (j % 2) * 512:(j % 2) * 512 + 512], ds,
                                        0.0, None, ALU.min).then_inc(sm_v, 1)
                @blk.scalar
                def _(s):
                    for j in range(8):
                        s.wait_ge(sm_v, v0 + j + 1)
                        s.activation(E[:, j * 512:(j + 1) * 512],
                                     ytmp[:, (j % 2) * 512:(j % 2) * 512 + 512],
                                     AF.Exp).then_inc(sm_as, 1)
            c.semc["mm"] = mm0 + 8; c.semc["vv"] = v0 + 8; c.semc["as"] = as0 + 8

            mm0, v0, f0 = c.semc["mm"], c.semc["vv"], c.semc["fv"]
            with nc.Block() as blk:
                if not front:
                    @blk.tensor
                    def _(t):
                        t.matmul(pp[0][:, 0:128], BCh[:, csl], BCh[:, 1024 + csl.start:1024 + csl.stop],
                                 start=True, stop=True).then_inc(sm_mm, 1)
                @blk.vector
                def _(v):
                    if not front:
                        v.wait_ge(sm_mm, mm0 + 1)
                        v.tensor_tensor(GmT[:], pp[0][:, 0:128], triu[:], ALU.mult)
                        v.tensor_tensor(Mp[:], E[:],
                                        bass.AP(GmT[:].tensor, GmT[:].offset,
                                                [list(GmT[:].ap[0]), [0, 32], [1, 128]]),
                                        ALU.mult)
                        v.tensor_tensor(Mp[:], Mp[:],
                                        bass.AP(TD[:].tensor, TD[:].offset,
                                                [list(TD[:].ap[0]), [1, 32], [0, 128]]),
                                        ALU.mult)
                        v.tensor_tensor(Mp[:], Mp[:], IDall[:], ALU.add)
                    v.tensor_tensor(ddT[:], restride(E[:], 127, 128, 32), TD[:, 0:32],
                                    ALU.mult).then_inc(smf, 1)
                    v.wait_ge(smf, f0 + 1)
                    v.tensor_tensor(xtf[:], xT[:],
                                    bass.AP(ddT[:].tensor, ddT[:].offset,
                                            [list(ddT[:].ap[0]), [1, 32], [0, 64]]),
                                    ALU.mult).then_inc(sm_v, 1)
            c.semc["mm"] = mm0 + (0 if front else 1)
            c.semc["vv"] = v0 + 1; c.semc["fv"] = f0 + 1

            mm0, v0 = c.semc["mm"], c.semc["vv"]
            with nc.Block() as blk:
                @blk.tensor
                def _(t):
                    for j in range(4):
                        ds = (psa if j < 2 else psb)[:, (j % 2) * 512:(j % 2) * 512 + 512]
                        t.matmul(ds, BT[:], xtf[:, j * 512:(j + 1) * 512],
                                 start=True, stop=True).then_inc(sm_mm, 1)
                @blk.vector
                def _(v):
                    v.wait_ge(sm_mm, mm0 + 4)
                    v.tensor_copy(prevb[:], carry[:])
                    v.tensor_tensor(sz[:], carry[:],
                                    bass.AP(CDB[:].tensor, CDB[:].offset + ci,
                                            [list(CDB[:].ap[0]), [8, 32], [0, 64]]),
                                    ALU.mult)
                    v.tensor_add(carry[:, 0:1024], sz[:, 0:1024], psa[:])
                    v.tensor_add(carry[:, 1024:2048], sz[:, 1024:2048], psb[:])
            c.semc["mm"] = mm0 + 4

            if not front:
                mm0, v0 = c.semc["mm"], c.semc["vv"]
                with nc.Block() as blk:
                    @blk.tensor
                    def _(t):
                        for h in range(32):
                            ds = (psa if h < 16 else psb)[:, (h % 16) * 64:(h % 16) * 64 + 64]
                            mmi = t.matmul(ds, Mp[:, h * 128:(h + 1) * 128],
                                           xT[:, h * 64:(h + 1) * 64],
                                           start=True, stop=True)
                        mmi.then_inc(sm_mm, 1)
                        for j in range(2):
                            mmi = t.matmul(pp[j][:], BCh[:, 1024 + csl.start:1024 + csl.stop],
                                           prevb[:, j * 512:(j + 1) * 512],
                                           start=True, stop=True)
                        mmi.then_inc(sm_mm, 1)
                        t.wait_ge(sm_v, v0 + 1)
                        for j in range(2):
                            mmi = t.matmul(pp[j][:], BCh[:, 1024 + csl.start:1024 + csl.stop],
                                           prevb[:, 1024 + j * 512:1024 + (j + 1) * 512],
                                           start=True, stop=True)
                        mmi.then_inc(sm_mm, 1)
                    @blk.vector
                    def _(v):
                        v.wait_ge(sm_mm, mm0 + 2)
                        for j in range(2):
                            v.tensor_tensor(ytmp[:, j * 512:(j + 1) * 512], pp[j][:],
                                            bass.AP(ETexp[:].tensor, ETexp[:].offset + j * 8,
                                                    [list(ETexp[:].ap[0]), [1, 8], [0, 64]]),
                                            ALU.mult)
                        v.tensor_add(ysb[:, 0:1024], ytmp[:], psa[:]).then_inc(sm_v, 1)
                        v.wait_ge(sm_mm, mm0 + 3)
                        for j in range(2):
                            v.tensor_tensor(ytmp[:, j * 512:(j + 1) * 512], pp[j][:],
                                            bass.AP(ETexp[:].tensor, ETexp[:].offset + 16 + j * 8,
                                                    [list(ETexp[:].ap[0]), [1, 8], [0, 64]]),
                                            ALU.mult)
                        v.tensor_add(ysb[:, 1024:2048], ytmp[:], psb[:])
                c.semc["mm"] = mm0 + 3; c.semc["vv"] = v0 + 1

                dma0, as0, v0, f0 = (c.semc[s] for s in ("dma", "as", "vv", "fv"))
                with nc.Block() as blk:
                    @blk.gpsimd
                    def _(g):
                        g.dma_start(zch[:], dz.ap()[g0:g0 + 128, :]).then_inc(dma, 16)
                    @blk.scalar
                    def _(s):
                        s.wait_ge(dma, dma0 + 16)
                        s.activation(sz[:], zch[:], AF.Silu).then_inc(sm_as, 1)
                        s.wait_ge(sm_v, v0 + 1)
                        s.activation(rms1[:, 2:3], rms1[:, 1:2], AF.Ln, bias=EPS)
                        s.activation(rms1[:, 3:4], rms1[:, 2:3], AF.Exp,
                                     scale=f32(-0.5)).then_inc(sm_as, 1)
                    @blk.vector
                    def _(v):
                        v.wait_ge(sm_as, as0 + 1)
                        v.tensor_mul(ysb[:], ysb[:], sz[:])
                        v.tensor_mul(sz[:], ysb[:], ysb[:])
                        ins = v.reduce_sum(rms1[:, 0:1], sz[:], axis=AX)
                        ins.then_inc(smf, 1)
                        v.wait_ge(smf, f0 + 1)
                        v.tensor_single_scalar(rms1[:, 1:2], rms1[:, 0:1],
                                               1.0 / 2048.0, ALU.mult).then_inc(sm_v, 1)
                        v.wait_ge(sm_as, as0 + 2)
                        v.tensor_tensor(yn[:], ysb[:], bcol(rms1[:, 3:4], 2048), ALU.mult)
                        v.nop().then_inc(sm_v, 1)
                c.semc["dma"] = dma0 + 16; c.semc["as"] = as0 + 2
                c.semc["vv"] = v0 + 2; c.semc["fv"] = f0 + 1

                mm0, v0 = c.semc["mm"], c.semc["vv"]
                with nc.Block() as blk:
                    @blk.tensor
                    def _(t):
                        for b in range(16):
                            if b >= 2:
                                t.wait_ge(sm_v, v0 + b - 1)
                            t.transpose(pbf[:, (b % 2) * 1024:(b % 2) * 1024 + 128],
                                        yn[:, b * 128:(b + 1) * 128], idb[:]).then_inc(sm_mm, 1)
                    @blk.vector
                    def _(v):
                        for b in range(16):
                            v.wait_ge(sm_mm, mm0 + b + 1)
                            v.tensor_scalar_mul(ynT[:, b * 128:(b + 1) * 128],
                                                pbf[:, (b % 2) * 1024:(b % 2) * 1024 + 128],
                                                normw[:, b:b + 1]).then_inc(sm_v, 1)
                c.semc["mm"] = mm0 + 16; c.semc["vv"] = v0 + 16

                mm0, ev0, dma0, st0 = (c.semc[s] for s in ("mm", "ev", "dma", "st"))
                with nc.Block() as blk:
                    @blk.gpsimd
                    def _(g):
                        g.dma_start(xres[:].rearrange("p (kc t) -> p kc t", kc=8),
                                    fview(dxin, 3 + g0, 128)).then_inc(dma, 16)
                    @blk.tensor
                    def _(t):
                        for o in range(8):
                            if o >= 2:
                                t.wait_ge(sm_ev, ev0 + o - 1)
                            for kc in range(16):
                                mmi = t.matmul(pp[o % 2][:, 0:128],
                                               woutr[:, kc * 1024 + o * 128: kc * 1024 + (o + 1) * 128],
                                               ynT[:, kc * 128:(kc + 1) * 128],
                                               start=(kc == 0), stop=(kc == 15))
                            mmi.then_inc(sm_mm, 1)
                    @blk.vector
                    def _(v):
                        for o in range(8):
                            v.wait_ge(sm_mm, mm0 + o + 1)
                            v.wait_ge(dma, dma0 + 16)
                            if o >= 2:
                                v.wait_ge(st_sem, st0 + 16 * (o - 1))
                            v.tensor_add(ost[o % 2][:, 0:128], pp[o % 2][:, 0:128],
                                         xres[:, o * 128:(o + 1) * 128]).then_inc(sm_ev, 1)
                    @blk.sync
                    def _(sy):
                        for o in range(8):
                            sy.wait_ge(sm_ev, ev0 + o + 1)
                            sy.dma_start(dout_x1.ap()[o * 128:(o + 1) * 128, g0:g0 + 128],
                                         ost[o % 2][:, 0:128]).then_inc(st_sem, 16)
                        sy.wait_ge(st_sem, st0 + 16 * 8)
                c.semc["mm"] = mm0 + 8; c.semc["ev"] = ev0 + 8
                c.semc["dma"] = dma0 + 16; c.semc["st"] = st0 + 16 * 8
        if dbg:
            c.dbg_handles = dict(x4b=x4b, BCh=BCh, CD2=CD2, CDB=CDB, E=E, Mp=Mp,
                                 xT=xT, xtf=xtf, ysb=ysb, yn=yn, ynT=ynT, TD=TD,
                                 ETexp=ETexp, ddT=ddT, GmT=GmT, BT=BT, carry=carry,
                                 prevb=prevb, IDall=IDall, zch=zch, rms1=rms1)
            break
        c.pop_scope()
        c.pop_scope()

    st0 = c.semc["st"]
    with nc.Block() as blk:
        @blk.sync
        def _(sy):
            sy.dma_start(dstate.ap(), carry[:]).then_inc(st_sem, 16)
            sy.wait_ge(st_sem, st0 + 16)
    c.semc["st"] = st0 + 16


def _percol(v):
    return np.ascontiguousarray(v.reshape(-1, 128).T)


def prep_weights(I):
    W = {}
    W["ones_bf"] = np.ones((128, 128), dtype=bfd)
    W["ident_bf"] = np.eye(128, dtype=np.float32).astype(bfd)
    W["ident_f32"] = np.eye(128, dtype=np.float32)
    W["l1f_w1T"] = np.ascontiguousarray(I["L1_ffn_w1_w"].T)
    W["l1f_w2T"] = np.ascontiguousarray(I["L1_ffn_w2_w"].T)
    W["l1f_h1T"] = np.ascontiguousarray(I["L1_ffn_h1_w"].T)
    W["l1f_h2T"] = np.ascontiguousarray(I["L1_ffn_h2_w"].T)
    W["l1f_w3T"] = np.ascontiguousarray(I["L1_ffn_w3_w"].T)
    W["l1f_s1"] = _percol(I["L1_ffn_w1_s"])
    W["l1f_s2"] = _percol(I["L1_ffn_w2_s"])
    W["l1f_h1s"] = _percol(I["L1_ffn_h1_s"])
    W["l1f_h2s"] = _percol(I["L1_ffn_h2_s"])
    W["l1f_w3s"] = _percol(I["L1_ffn_w3_s"])
    W["onescol"] = np.ones((128, 1), dtype=np.float32)
    W["moe_gw"] = np.ascontiguousarray(I["L0_ffn_gate_W"])
    oh8 = np.zeros((8, 8 * 128), dtype=np.float32)
    for e in range(8):
        oh8[e, e * 128:(e + 1) * 128] = 1.0
    W["moe_oh8"] = oh8
    W["moe_b1c"] = np.concatenate([_percol(I["L0_ffn_b1"][e]) for e in range(8)], axis=1)
    W["moe_b2"] = np.ascontiguousarray(I["L0_ffn_b2"])
    W["moe_W1f"] = np.ascontiguousarray(I["L0_ffn_W1"].reshape(8192, 2048))
    W["moe_W2f"] = np.ascontiguousarray(I["L0_ffn_W2"].reshape(16384, 1024))
    W["onesrow_bf"] = np.ones((1, 128), dtype=bfd)
    wqkvT = np.ascontiguousarray(I["L1_attn_Wqkv"].T)
    W["l1a_wqT"] = np.ascontiguousarray(wqkvT[:, 0:1024])
    W["l1a_wkT"] = np.ascontiguousarray(wqkvT[:, 1024:2048])
    W["l1a_wvT"] = np.ascontiguousarray(wqkvT[:, 2048:3072])
    W["l1a_woutT"] = np.ascontiguousarray(I["L1_attn_Wout"].T)
    W["l1a_bq"] = _percol(I["L1_attn_bqkv"][0:1024])
    W["l1a_bk"] = _percol(I["L1_attn_bqkv"][1024:2048])
    W["l1a_bv"] = I["L1_attn_bqkv"][2048:3072].reshape(1, 1024).astype(bfd)
    W["l1a_bout"] = _percol(I["L1_attn_bout"])
    ll = np.arange(128)[:, None]; jj = np.arange(256)[None, :]
    mR = np.where((jj >= ll + 65) & (jj <= ll + 128), 0.0, -1e9).astype(np.float32)
    m0 = np.where((jj >= ll + 65) & (jj <= ll + 128) & (jj >= 128), 0.0, -1e9).astype(np.float32)
    W["l1a_maskR"] = mR
    W["l1a_mask0"] = m0
    W["onesrow_f"] = np.ones((1, 128), dtype=np.float32)
    W["ones32"] = np.ones((32, 128), dtype=np.float32)
    W["mb_winT"] = np.ascontiguousarray(I["L0_attn_W_in"].T)
    cw4 = I["L0_attn_conv_w"]
    W["mb_convw"] = np.ascontiguousarray(
        np.concatenate([cw4[ct * 128:(ct + 1) * 128, :] for ct in range(18)], axis=1))
    W["mb_convb"] = _percol(I["L0_attn_conv_b"])
    W["mb_dtb"] = np.ascontiguousarray(I["L0_attn_dt_bias"][:, None])
    W["mb_alog"] = np.ascontiguousarray(I["L0_attn_A_log"][:, None])
    W["mb_drep"] = np.ascontiguousarray(np.tile(I["L0_attn_D"][None, :], (128, 1)))
    W["mb_normw"] = _percol(I["L0_attn_norm_w"])
    W["mb_woutT"] = np.ascontiguousarray(I["L0_attn_W_out"].T)
    W["mb_oh32"] = np.ascontiguousarray(
        np.repeat(np.eye(32, dtype=np.float32), 128, axis=1))
    W["mb_triu"] = np.triu(np.ones((128, 128), dtype=np.float32))
    return W


def _declare(c, W, names):
    d = {}
    for nm in names:
        a = W[nm]
        dt = BF16 if a.dtype == bfd else F32
        d[nm] = c.din(nm, a.shape, dt)
    return d


MOE_W = ["ones_bf", "onescol", "ident_f32", "moe_gw", "moe_oh8", "moe_b1c",
         "moe_b2", "moe_W1f", "moe_W2f"]


def launch_l0ffn(x1_shards, W):
    c = Ctx()
    dx1 = c.din("x1", [D, T])
    dW = _declare(c, W, MOE_W)
    do = c.dout("x2", [D, T])
    emit_l0ffn(c, dx1, dW, do)
    c.nc.finalize()
    in_maps = [dict({nm: W[nm] for nm in MOE_W}, x1=x1_shards[i]) for i in range(N_CORES)]
    res = run_bass_kernel_spmd(c.nc, in_maps, core_ids=list(range(N_CORES)))
    return [r["x2"] for r in res.results]


MAMBA_W = ["ones_bf", "onesrow_f", "ident_f32", "ident_bf", "ones32",
           "mb_winT", "mb_convw", "mb_convb", "mb_dtb", "mb_alog", "mb_drep",
           "mb_normw", "mb_woutT", "mb_oh32", "mb_triu"]


def shard_xin(x_full):
    out = []
    for cid in range(N_CORES):
        bi, hi = cid // 2, cid % 2
        if hi == 0:
            rows = np.concatenate([np.zeros((3, D), np.float32),
                                   x_full[bi, 0:T, :]], axis=0)
        else:
            rows = x_full[bi, T - 3:2 * T, :]
        out.append(np.ascontiguousarray(rows.T))
    return out


def launch_l0attn(xin_shards, S0_list, W):
    c = Ctx()
    dxin = c.din("xin", [D, 2051])
    dS0 = c.din("S0", [128, 2048])
    dW = _declare(c, W, MAMBA_W)
    do = c.dout("x1", [D, T])
    dst = c.dout("fstate", [128, 2048])
    emit_l0attn(c, dxin, dS0.ap(), dW, do, dst)
    c.nc.finalize()
    in_maps = [dict({nm: W[nm] for nm in MAMBA_W}, xin=xin_shards[i], S0=S0_list[i])
               for i in range(N_CORES)]
    res = run_bass_kernel_spmd(c.nc, in_maps, core_ids=list(range(N_CORES)))
    return [r["x1"] for r in res.results], [r["fstate"] for r in res.results]


ATTN_W = ["ones_bf", "onesrow_bf", "ident_bf", "l1a_wqT", "l1a_wkT", "l1a_wvT",
          "l1a_woutT", "l1a_bq", "l1a_bk", "l1a_bv", "l1a_bout", "l1a_maskR",
          "l1a_mask0"]


def launch_l1attn(x2_shards, W):
    c = Ctx()
    dx2 = c.din("x2", [D, T])
    dxh = c.din("xh", [D, 63])
    dW = _declare(c, W, ATTN_W)
    do = c.dout("x3", [D, T])
    emit_l1attn(c, dx2, dxh.ap(), dW, do)
    c.nc.finalize()
    in_maps = []
    for i in range(N_CORES):
        xh = np.zeros((D, 63), dtype=np.float32)
        m = dict({nm: W[nm] for nm in ATTN_W}, x2=x2_shards[i], xh=xh)
        if i % 2 == 1:
            m["xh"] = np.ascontiguousarray(x2_shards[i - 1][:, 2048 - 63:2048])
            m["l1a_mask0"] = W["l1a_maskR"]
        in_maps.append(m)
    res = run_bass_kernel_spmd(c.nc, in_maps, core_ids=list(range(N_CORES)))
    return [r["x3"] for r in res.results]


L1FFN_W = ["ones_bf", "l1f_w1T", "l1f_w2T", "l1f_h1T", "l1f_h2T", "l1f_w3T",
           "l1f_s1", "l1f_s2", "l1f_h1s", "l1f_h2s", "l1f_w3s"]


def launch_l1ffn(x3_shards, W):
    c = Ctx()
    dx3 = c.din("x3", [D, T])
    dW = _declare(c, W, L1FFN_W)
    do = c.dout("x4", [D, T])
    emit_l1ffn(c, dx3, dW, do)
    c.nc.finalize()
    in_maps = [dict({nm: W[nm] for nm in L1FFN_W}, x3=x3_shards[i]) for i in range(N_CORES)]
    res = run_bass_kernel_spmd(c.nc, in_maps, core_ids=list(range(N_CORES)))
    return [r["x4"] for r in res.results]


def shard_rows(full):
    out = []
    for cid in range(N_CORES):
        bi, hi = cid // 2, cid % 2
        out.append(np.ascontiguousarray(full[bi, hi * T:(hi + 1) * T, :].T))
    return out


def unshard_rows(shards):
    full = np.empty((4, 4096, D), dtype=np.float32)
    for cid in range(N_CORES):
        bi, hi = cid // 2, cid % 2
        full[bi, hi * T:(hi + 1) * T, :] = np.asarray(shards[cid]).T
    return full


ALL_W = None


def build_fused(W):
    global ALL_W
    ALL_W = sorted(set(MAMBA_W) | set(MOE_W) | set(ATTN_W) | set(L1FFN_W))
    c = Ctx()
    nc = c.nc
    dxin = c.din("xin", [D, 2051])
    dmask = c.din("smask", [128, 1])
    dW = _declare(c, W, ALL_W)
    dout = c.dout("out", [D, T])
    dx1 = c.dtmp("f_x1", [D, T])
    dx2 = c.dtmp("f_x2", [D, T])
    dx3 = c.dtmp("f_x3", [D, T])
    dsend = c.dtmp("f_send", [128, 2048])
    drecv = c.dtmp("f_recv", [256, 2048])
    dsend2 = c.dtmp("f_send2", [D, 63])
    drecv2 = c.dtmp("f_recv2", [2 * D, 63])
    ddum = c.dtmp("f_dumstate", [128, 2048])
    cc_sem = c.sem("cc")

    c.push_scope()
    with nc.named_scope("mamba_front"):
        emit_l0attn(c, dxin, None, dW, dx1, dsend, front=True)
    c.pop_scope()
    with nc.Block() as blk:
        @blk.gpsimd
        def _(g):
            g.collective_compute(
                "AllGather", ALU.bypass,
                replica_groups=[[0, 1], [2, 3], [4, 5], [6, 7]],
                ins=[dsend.ap().opt()], outs=[drecv.ap().opt()],
            ).then_inc(cc_sem)
            g.wait_ge(cc_sem, 1)
    c.semc["cc"] = 1
    c.push_scope()
    with nc.named_scope("mamba_full"):
        emit_l0attn(c, dxin, drecv.ap()[0:128, :], dW, dx1, ddum, dmask=dmask)
    c.pop_scope()
    c.push_scope()
    with nc.named_scope("moe"):
        emit_l0ffn(c, dx1, dW, dx2)
    c.pop_scope()
    dma, st_sem = c.sem("dma"), c.sem("st")
    dma0 = c.semc["dma"]
    with nc.Block() as blk:
        @blk.gpsimd
        def _(g):
            g.dma_start(dsend2.ap(), dx2.ap()[:, T - 63:T]).then_inc(dma, 16)
            g.wait_ge(dma, dma0 + 16)
            g.collective_compute(
                "AllGather", ALU.bypass,
                replica_groups=[[0, 1], [2, 3], [4, 5], [6, 7]],
                ins=[dsend2.ap().opt()], outs=[drecv2.ap().opt()],
            ).then_inc(cc_sem)
            g.wait_ge(cc_sem, 2)
    c.semc["dma"] = dma0 + 16
    c.semc["cc"] = 2
    c.push_scope()
    with nc.named_scope("attn"):
        emit_l1attn(c, dx2, drecv2.ap()[0:D, :], dW, dx3)
    c.pop_scope()
    c.push_scope()
    with nc.named_scope("l1ffn"):
        emit_l1ffn(c, dx3, dW, dout)
    c.pop_scope()
    return c


def kernel_fused(**inputs):
    W = prep_weights(inputs)
    c = build_fused(W)
    c.nc.finalize()
    xin_shards = shard_xin(inputs["L0_attn_inputs"])
    in_maps = []
    for i in range(N_CORES):
        m = {nm: W[nm] for nm in ALL_W}
        m["xin"] = xin_shards[i]
        m["smask"] = np.full((128, 1), 1.0 if i % 2 == 1 else 0.0, np.float32)
        if i % 2 == 1:
            m["l1a_mask0"] = W["l1a_maskR"]
        in_maps.append(m)
    res = run_bass_kernel_spmd(c.nc, in_maps, core_ids=list(range(N_CORES)))
    return unshard_rows([r["out"] for r in res.results])


def kernel_unfused(**inputs):
    W = prep_weights(inputs)
    xin_shards = shard_xin(inputs["L0_attn_inputs"])
    zeros = [np.zeros((128, 2048), np.float32) for _ in range(N_CORES)]
    _, F = launch_l0attn(xin_shards, zeros, W)
    S0 = [zeros[i] if i % 2 == 0 else np.ascontiguousarray(np.asarray(F[i - 1]))
          for i in range(N_CORES)]
    x1_shards, _ = launch_l0attn(xin_shards, S0, W)
    x1_shards = [np.ascontiguousarray(np.asarray(a)) for a in x1_shards]
    x2_shards = [np.ascontiguousarray(np.asarray(a)) for a in launch_l0ffn(x1_shards, W)]
    x3_shards = [np.ascontiguousarray(np.asarray(a)) for a in launch_l1attn(x2_shards, W)]
    x4_shards = launch_l1ffn(x3_shards, W)
    return unshard_rows(x4_shards)


def kernel(**inputs):
    return kernel_fused(**inputs)


# version 05
# speedup vs baseline: 1.0521x; 1.0521x over previous
import numpy as np
from contextlib import ExitStack
import concourse.bass as bass
import concourse.mybir as mybir
from concourse.bass_utils import run_bass_kernel_spmd
import ml_dtypes

F32 = mybir.dt.float32
BF16 = mybir.dt.bfloat16
AF = mybir.ActivationFunctionType
ALU = mybir.AluOpType
AX = mybir.AxisListType.X
bfd = ml_dtypes.bfloat16
f32 = np.float32

T = 2048
D = 1024
N_CORES = 8
EPS = f32(1e-5)


def ap0(ap, extra_dims):
    dims = [list(d) for d in ap.ap] + [list(d) for d in extra_dims]
    return bass.AP(ap.tensor, ap.offset, dims)


class Ctx:
    def __init__(self):
        self.nc = bass.Bass("TRN2", target_bir_lowering=False)
        self.es = ExitStack()
        self.semc = {}
        self._sems = {}

    def sem(self, name):
        if name not in self._sems:
            self._sems[name] = self.es.enter_context(self.nc.semaphore(name))
            self.semc[name] = 0
        return self._sems[name]

    def _uniq(self, name):
        if not hasattr(self, "_names"):
            self._names = set()
        base, n = name, 1
        while name in self._names:
            n += 1
            name = f"{base}_{n}"
        self._names.add(name)
        return name

    def sb(self, name, shape, dt=F32):
        es = self.scopes[-1] if getattr(self, "scopes", None) else self.es
        return es.enter_context(self.nc.sbuf_tensor(self._uniq(name), shape, dt))

    def push_scope(self):
        if not hasattr(self, "scopes"):
            self.scopes = []
        self.scopes.append(ExitStack())

    def pop_scope(self):
        self.scopes.pop().close()

    def ps(self, name, shape, dt=F32):
        return self.es.enter_context(self.nc.psum_tensor(name, shape, dt))

    def dtmp(self, name, shape, dt=F32):
        return self.nc.dram_tensor(self._uniq(name), list(shape), dt)

    def din(self, name, shape, dt=F32):
        return self.nc.dram_tensor(name, list(shape), dt, kind="ExternalInput")

    def dout(self, name, shape, dt=F32):
        return self.nc.dram_tensor(name, list(shape), dt, kind="ExternalOutput")


def get_pool(c):
    if not hasattr(c, "_pool"):
        psa = c.ps("P_psa", [128, 1024])
        psb = c.ps("P_psb", [128, 1024])
        pp0 = c.ps("P_pp0", [128, 512])
        pp1 = c.ps("P_pp1", [128, 512])
        pbf = c.ps("P_pbf", [128, 2048], BF16)
        c._pool = dict(psa=psa, psb=psb, pbf=pbf,
                       pp=[pp0[:], pp1[:], psb[:, 0:512], psb[:, 512:1024]])
    return c._pool


def loadblk(c, pairs):
    nc, sem = c.nc, c.sem("dma")
    n0 = c.semc["dma"]
    with nc.Block() as blk:
        @blk.gpsimd
        def _(g):
            n = n0
            for dst, src in pairs:
                g.dma_start(dst, src).then_inc(sem, 16)
                n += 16
            g.wait_ge(sem, n)
    c.semc["dma"] = n0 + 16 * len(pairs)


def fview(d, t0, Tt):
    return d.ap()[:, t0:t0 + Tt].rearrange("(kc p) t -> p kc t", p=128)


def wslice(dw, o0, ow):
    return dw.ap()[:, o0:o0 + ow].rearrange("(kc p) o -> p kc o", p=128)


def emit_ln(c, xnsl, ones_bf, S, Tt, kcs=8, cast_from=None):
    nc = c.nc
    sq = S["sq"]
    if cast_from is not None:
        with nc.Block() as blk:
            @blk.vector
            def _(v):
                for kc in range(kcs):
                    v.tensor_copy(xnsl(kc), cast_from(kc))
    ps_a, ps_b = S["ps_a"], S["ps_b"]
    mu, va, rstd = S["mu"], S["va"], S["rstd"]
    rdim = 1.0 / (kcs * 128)
    with nc.Block() as blk:
        @blk.scalar
        def _(s):
            for kc in range(kcs):
                s.activation(sq[:, kc * Tt:(kc + 1) * Tt], xnsl(kc), AF.Square)
    with nc.Block() as blk:
        @blk.tensor
        def _(t):
            ntt = (Tt + 511) // 512
            for tt in range(ntt):
                w = min(512, Tt - tt * 512)
                for kc in range(kcs):
                    t.matmul(ps_a[:, tt * 512:tt * 512 + w], ones_bf[:],
                             xnsl(kc)[:, tt * 512:tt * 512 + w],
                             start=(kc == 0), stop=(kc == kcs - 1))
                for kc in range(kcs):
                    t.matmul(ps_b[:, tt * 512:tt * 512 + w], ones_bf[:],
                             sq[:, kc * Tt + tt * 512:kc * Tt + tt * 512 + w],
                             start=(kc == 0), stop=(kc == kcs - 1))
    smf = c.sem("fv")
    with nc.Block() as blk:
        @blk.vector
        def _(v):
            f0 = c.semc["fv"]
            v.tensor_single_scalar(mu[:, 0:Tt], ps_a[:, 0:Tt], rdim, ALU.mult)
            v.tensor_single_scalar(va[:, 0:Tt], ps_b[:, 0:Tt], rdim,
                                   ALU.mult).then_inc(smf, 1)
            v.wait_ge(smf, f0 + 1)
            v.tensor_mul(rstd[:, 0:Tt], mu[:, 0:Tt], mu[:, 0:Tt]).then_inc(smf, 1)
            v.wait_ge(smf, f0 + 2)
            v.tensor_sub(va[:, 0:Tt], va[:, 0:Tt], rstd[:, 0:Tt])
            c.semc["fv"] = f0 + 2
    with nc.Block() as blk:
        @blk.scalar
        def _(s):
            f0 = c.semc["fv"]
            s.activation(va[:, 0:Tt], va[:, 0:Tt], AF.Ln, bias=EPS).then_inc(smf, 1)
            s.wait_ge(smf, f0 + 1)
            s.activation(rstd[:, 0:Tt], va[:, 0:Tt], AF.Exp, scale=f32(-0.5))
            c.semc["fv"] = f0 + 1
    with nc.Block() as blk:
        @blk.vector
        def _(v):
            st = S["st"]
            f0 = c.semc["fv"]
            for kc in range(kcs):
                ntt = (Tt + 511) // 512
                for tt in range(ntt):
                    w = min(512, Tt - tt * 512)
                    v.tensor_sub(st[:, 0:w], xnsl(kc)[:, tt * 512:tt * 512 + w],
                                 mu[:, tt * 512:tt * 512 + w]).then_inc(smf, 1)
                    f0 += 1
                    v.wait_ge(smf, f0)
                    v.tensor_mul(xnsl(kc)[:, tt * 512:tt * 512 + w], st[:, 0:w],
                                 rstd[:, tt * 512:tt * 512 + w])
            c.semc["fv"] = f0


def emit_l1ffn(c, dx3, dW, dout_x4):
    nc = c.nc
    TH = 1024
    ones_bf = c.sb("l4_ones", [128, 128], BF16)
    s1 = c.sb("l4_s1", [128, 32]); s2 = c.sb("l4_s2", [128, 32])
    h1s = c.sb("l4_h1s", [128, 32]); h2s = c.sb("l4_h2s", [128, 32])
    w3s = c.sb("l4_w3s", [128, 8])
    loadblk(c, [(ones_bf[:], dW["ones_bf"].ap()),
                (s1[:], dW["l1f_s1"].ap()), (s2[:], dW["l1f_s2"].ap()),
                (h1s[:], dW["l1f_h1s"].ap()), (h2s[:], dW["l1f_h2s"].ap()),
                (w3s[:], dW["l1f_w3s"].ap())])

    xn = c.sb("l4_xn", [128, 8 * TH], BF16)
    hh = c.sb("l4_h", [128, 32 * TH], BF16)
    h2 = c.sb("l4_h2", [128, 32 * TH], BF16)
    P = get_pool(c)
    S = dict(sq=c.sb("l4_sq", [128, 8 * TH], BF16),
             ps_a=P["psa"], ps_b=P["psb"],
             mu=c.sb("l4_mu", [128, TH]), va=c.sb("l4_va", [128, TH]),
             rstd=c.sb("l4_rstd", [128, TH]), st=None)
    wb1 = [c.sb(f"l4_wb1_{i}", [128, 8 * 128], BF16) for i in range(2)]
    wb2 = [c.sb(f"l4_wb2_{i}", [128, 8 * 128], BF16) for i in range(2)]
    wbig = [c.sb(f"l4_wbg{i}", [128, 32 * 128], BF16) for i in range(2)]
    pp = P["pp"]
    atmp = [c.sb(f"l4_at{i}", [128, 512], BF16) for i in range(2)]
    ost = [c.sb(f"l4_ost{i}", [128, 512]) for i in range(2)]
    xt = [c.sb(f"l4_xt{i}", [128, 512]) for i in range(2)]
    S["st"] = ost[0]
    sm_mm, sm_ev, sm_as, sm_wl = (c.sem("mm"), c.sem("ev"), c.sem("as"), c.sem("wl"))
    dma = c.sem("dma")

    for half in range(2):
        t0 = half * TH
        loadblk(c, [(xn[:].rearrange("p (kc t) -> p kc t", kc=8), fview(dx3, t0, TH))])
        emit_ln(c, lambda kc: xn[:, kc * TH:(kc + 1) * TH], ones_bf, S, TH)

        mm0, ev0, as0, wl0 = (c.semc[s] for s in ("mm", "ev", "as", "wl"))
        with nc.Block() as blk:
            @blk.gpsimd
            def _(g):
                for o in range(32):
                    if o >= 2:
                        g.wait_ge(sm_mm, mm0 + 2 * (o - 1))
                    g.dma_start(wb1[o % 2][:], wslice(dW["l1f_w1T"], o * 128, 128)).then_inc(sm_wl, 16)
                    g.dma_start(wb2[o % 2][:], wslice(dW["l1f_w2T"], o * 128, 128)).then_inc(sm_wl, 16)
            @blk.tensor
            def _(t):
                for o in range(32):
                    t.wait_ge(sm_wl, wl0 + 32 * (o + 1))
                    for tt in range(2):
                        j = o * 2 + tt
                        if j >= 2:
                            t.wait_ge(sm_ev, ev0 + j - 1)
                            t.wait_ge(sm_as, as0 + j - 1)
                        p1, p2 = pp[2 * (j % 2)], pp[2 * (j % 2) + 1]
                        for kc in range(8):
                            t.matmul(p1[:], wb1[o % 2][:, kc * 128:(kc + 1) * 128],
                                     xn[:, kc * TH + tt * 512: kc * TH + (tt + 1) * 512],
                                     start=(kc == 0), stop=(kc == 7))
                        for kc in range(8):
                            mmi = t.matmul(p2[:], wb2[o % 2][:, kc * 128:(kc + 1) * 128],
                                           xn[:, kc * TH + tt * 512: kc * TH + (tt + 1) * 512],
                                           start=(kc == 0), stop=(kc == 7))
                        mmi.then_inc(sm_mm, 1)
            @blk.scalar
            def _(s):
                for o in range(32):
                    for tt in range(2):
                        j = o * 2 + tt
                        s.wait_ge(sm_mm, mm0 + j + 1)
                        if j >= 2:
                            s.wait_ge(sm_ev, ev0 + j - 1)
                        s.activation(atmp[j % 2][:], pp[2 * (j % 2)][:], AF.Silu,
                                     scale=s1[:, o:o + 1]).then_inc(sm_as, 1)
            @blk.vector
            def _(v):
                for o in range(32):
                    for tt in range(2):
                        j = o * 2 + tt
                        v.wait_ge(sm_mm, mm0 + j + 1)
                        v.tensor_scalar_mul(ost[j % 2][:], pp[2 * (j % 2) + 1][:], s2[:, o:o + 1])
                        v.wait_ge(sm_as, as0 + j + 1)
                        v.tensor_mul(hh[:, o * TH + tt * 512: o * TH + (tt + 1) * 512],
                                     atmp[j % 2][:], ost[j % 2][:]).then_inc(sm_ev, 1)
        c.semc["mm"] = mm0 + 64; c.semc["ev"] = ev0 + 64
        c.semc["as"] = as0 + 64; c.semc["wl"] = wl0 + 32 * 32

        for (dwname, src_t, dst_t, scol) in (("l1f_h1T", hh, h2, h1s),
                                             ("l1f_h2T", h2, hh, h2s)):
            mm0, ev0, wl0 = c.semc["mm"], c.semc["ev"], c.semc["wl"]
            dw = dW[dwname]
            with nc.Block() as blk:
                @blk.gpsimd
                def _(g):
                    for o in range(32):
                        if o >= 2:
                            g.wait_ge(sm_mm, mm0 + 2 * (o - 1))
                        g.dma_start(wbig[o % 2][:], wslice(dw, o * 128, 128)).then_inc(sm_wl, 16)
                @blk.tensor
                def _(t):
                    for o in range(32):
                        t.wait_ge(sm_wl, wl0 + 16 * (o + 1))
                        for tt in range(2):
                            j = o * 2 + tt
                            if j >= 2:
                                t.wait_ge(sm_ev, ev0 + j - 1)
                            for kc in range(32):
                                mmi = t.matmul(pp[j % 2][:],
                                               wbig[o % 2][:, kc * 128:(kc + 1) * 128],
                                               src_t[:, kc * TH + tt * 512: kc * TH + (tt + 1) * 512],
                                               start=(kc == 0), stop=(kc == 31))
                            mmi.then_inc(sm_mm, 1)
                @blk.scalar
                def _(s):
                    for o in range(32):
                        for tt in range(2):
                            j = o * 2 + tt
                            s.wait_ge(sm_mm, mm0 + j + 1)
                            s.activation(dst_t[:, o * TH + tt * 512: o * TH + (tt + 1) * 512],
                                         pp[j % 2][:], AF.Gelu,
                                         scale=scol[:, o:o + 1]).then_inc(sm_ev, 1)
            c.semc["mm"] = mm0 + 64; c.semc["ev"] = ev0 + 64
            c.semc["wl"] = wl0 + 16 * 32

        st_sem, st0 = c.sem("st"), c.semc["st"]
        mm0, ev0, wl0, dma0 = c.semc["mm"], c.semc["ev"], c.semc["wl"], c.semc["dma"]
        with nc.Block() as blk:
            @blk.gpsimd
            def _(g):
                for o in range(8):
                    if o >= 2:
                        g.wait_ge(sm_mm, mm0 + 2 * (o - 1))
                    g.dma_start(wbig[o % 2][:], wslice(dW["l1f_w3T"], o * 128, 128)).then_inc(sm_wl, 16)
                    for tt in range(2):
                        j = o * 2 + tt
                        if j >= 2:
                            g.wait_ge(sm_ev, ev0 + j - 1)
                        g.dma_start(xt[j % 2][:],
                                    dx3.ap()[o * 128:(o + 1) * 128,
                                             t0 + tt * 512:t0 + (tt + 1) * 512]).then_inc(dma, 16)
            @blk.tensor
            def _(t):
                for o in range(8):
                    t.wait_ge(sm_wl, wl0 + 16 * (o + 1))
                    for tt in range(2):
                        j = o * 2 + tt
                        if j >= 2:
                            t.wait_ge(sm_ev, ev0 + j - 1)
                        for kc in range(32):
                            mmi = t.matmul(pp[j % 2][:],
                                           wbig[o % 2][:, kc * 128:(kc + 1) * 128],
                                           hh[:, kc * TH + tt * 512: kc * TH + (tt + 1) * 512],
                                           start=(kc == 0), stop=(kc == 31))
                        mmi.then_inc(sm_mm, 1)
            @blk.vector
            def _(v):
                for o in range(8):
                    for tt in range(2):
                        j = o * 2 + tt
                        v.wait_ge(sm_mm, mm0 + j + 1)
                        v.wait_ge(dma, dma0 + 16 * (j + 1))
                        if j >= 2:
                            v.wait_ge(st_sem, st0 + 16 * (j - 1))
                        v.scalar_tensor_tensor(ost[j % 2][:], pp[j % 2][:], w3s[:, o:o + 1],
                                               xt[j % 2][:], ALU.mult, ALU.add).then_inc(sm_ev, 1)
            @blk.sync
            def _(sy):
                for o in range(8):
                    for tt in range(2):
                        j = o * 2 + tt
                        sy.wait_ge(sm_ev, ev0 + j + 1)
                        sy.dma_start(dout_x4.ap()[o * 128:(o + 1) * 128,
                                                  t0 + tt * 512:t0 + (tt + 1) * 512],
                                     ost[j % 2][:]).then_inc(st_sem, 16)
                sy.wait_ge(st_sem, st0 + 16 * 16)
        c.semc["mm"] = mm0 + 16; c.semc["ev"] = ev0 + 16
        c.semc["wl"] = wl0 + 16 * 8; c.semc["dma"] = dma0 + 16 * 16
        c.semc["st"] = st0 + 16 * 16



def bcol(ap_col, n):
    return bass.AP(ap_col.tensor, ap_col.offset, [list(ap_col.ap[0]), [0, n]])


def brep(ap2d, rep, inner):
    return bass.AP(ap2d.tensor, ap2d.offset,
                   [list(ap2d.ap[0]), [0, rep], list(ap2d.ap[1])[:1] + [inner]])


def emit_l0ffn(c, dx1, dW, dout_x2):
    nc = c.nc
    TH = 1024
    ones_bf = c.sb("m_ones", [128, 128], BF16)
    onescol = c.sb("m_onec", [128, 1])
    gw = c.sb("m_gw", [128, 8 * 8])
    oh8 = c.sb("m_oh8", [8, 8 * 128])
    b1c = c.sb("m_b1c", [128, 8 * 16])
    b2sb = c.sb("m_b2", [8, 1024])
    idf = c.sb("m_idf", [128, 128])
    loadblk(c, [(ones_bf[:], dW["ones_bf"].ap()), (onescol[:], dW["onescol"].ap()),
                (gw[:].rearrange("p (kc e) -> p kc e", kc=8),
                 dW["moe_gw"].ap().rearrange("(kc p) e -> p kc e", p=128)),
                (oh8[:], dW["moe_oh8"].ap()), (b1c[:], dW["moe_b1c"].ap()),
                (b2sb[:], dW["moe_b2"].ap()), (idf[:], dW["ident_f32"].ap())])

    xn = c.sb("m_xn", [128, 8 * TH], BF16)
    hsb = c.sb("m_h", [128, 16 * 512], BF16)
    acc = c.sb("m_acc", [128, 8 * TH])
    w1e = c.sb("m_w1e", [128, 8 * 2048], BF16)
    w2e = c.sb("m_w2e", [128, 16 * 1024], BF16)
    cwb = c.sb("m_cwb", [128, TH], BF16)
    cwT = c.sb("m_cwT", [8, TH])
    cw_all = c.sb("m_cwall", [128, 8 * 8])
    negcs = c.sb("m_negcs", [1, 8])
    ga = c.sb("m_ga", [128, 64]); gb = c.sb("m_gb", [128, 64])
    gsc = c.sb("m_gsc", [128, 8]); gsm = c.sb("m_gsm", [128, 8])
    gss = c.sb("m_gss", [128, 8])

    gx = [c.sb(f"m_gx{i}", [128, 8 * 128]) for i in range(2)]
    P = get_pool(c)
    S = dict(sq=acc[:, 3072:7168].bitcast(BF16),
             ps_a=P["psa"], ps_b=P["psb"],
             mu=acc[:, 0:1024], va=acc[:, 1024:2048],
             rstd=acc[:, 2048:3072], st=None)
    xt = [c.sb(f"m_xt{i}", [128, 512]) for i in range(2)]
    ost = [c.sb(f"m_ost{i}", [128, 512]) for i in range(2)]
    S["st"] = ost[0]
    pp = P["pp"]
    psg = S["ps_a"]
    sm_mm, sm_ev, sm_as, sm_wl = (c.sem("mm"), c.sem("ev"), c.sem("as"), c.sem("wl"))
    sm_v = c.sem("vv")
    dma = c.sem("dma")
    st_sem = c.sem("st")

    with nc.Block() as blk:
        @blk.tensor
        def _(t):
            for kc in range(8):
                mmi = t.matmul(psg[0:1, 0:8], onescol[:], gw[:, kc * 8:(kc + 1) * 8],
                               start=(kc == 0), stop=(kc == 7))
            mmi.then_inc(sm_mm, 1)
        @blk.vector
        def _(v):
            v.wait_ge(sm_mm, c.semc["mm"] + 1)
            v.tensor_single_scalar(negcs[:], psg[0:1, 0:8], -1.0, ALU.mult)
    c.semc["mm"] += 1

    for half in range(2):
        t0 = half * TH

        loadblk(c, [(xn[:].rearrange("p (kc t) -> p kc t", kc=8), fview(dx1, t0, TH))])
        emit_ln(c, lambda kc: xn[:, kc * TH:(kc + 1) * TH], ones_bf, S, TH)

        mm0, ev0, dma0 = (c.semc[s] for s in ("mm", "ev", "dma"))
        mu, rstd = S["mu"], S["rstd"]
        with nc.Block() as blk:
            @blk.gpsimd
            def _(g):
                for ti in range(8):
                    if ti >= 2:
                        g.wait_ge(sm_mm, mm0 + ti - 1)
                    g.dma_start(gx[ti % 2][:].rearrange("p (kc t) -> p kc t", kc=8),
                                dx1.ap()[:, t0 + ti * 128:t0 + (ti + 1) * 128]
                                .rearrange("(kc p) t -> p kc t", p=128)).then_inc(dma, 16)
            @blk.tensor
            def _(t):
                for ti in range(8):
                    t.wait_ge(dma, dma0 + 16 * (ti + 1))
                    po = ti * 16
                    for kc in range(8):
                        t.matmul(psg[:, po:po + 8], gx[ti % 2][:, kc * 128:(kc + 1) * 128],
                                 gw[:, kc * 8:(kc + 1) * 8], start=(kc == 0), stop=False)
                    t.matmul(psg[:, po:po + 8], mu[0:1, ti * 128:(ti + 1) * 128],
                             negcs[:], start=False, stop=True)
                    t.matmul(psg[:, po + 8:po + 9], rstd[0:1, ti * 128:(ti + 1) * 128],
                             onescol[0:1, 0:1], start=True, stop=True).then_inc(sm_mm, 1)
        c.semc["mm"] = mm0 + 8; c.semc["dma"] = dma0 + 16 * 8
        smf = c.sem("fv"); f0 = c.semc["fv"]
        with nc.Block() as blk:
            @blk.vector
            def _(v):
                fc = f0
                for ti in range(8):
                    ins = v.tensor_copy(gsc[:, ti:ti + 1], psg[:, ti * 16 + 8:ti * 16 + 9])
                ins.then_inc(smf, 1); fc += 1
                v.wait_ge(smf, fc)
                for ti in range(8):
                    ins = v.tensor_tensor(ga[:, ti * 8:(ti + 1) * 8], psg[:, ti * 16:ti * 16 + 8],
                                          bcol(gsc[:, ti:ti + 1], 8), ALU.mult)
                for ti in range(8):
                    ins = v.reduce_max(gsm[:, ti:ti + 1], ga[:, ti * 8:(ti + 1) * 8], axis=AX)
                ins.then_inc(smf, 1); fc += 1
                v.wait_ge(smf, fc)
                for ti in range(8):
                    v.tensor_tensor(ga[:, ti * 8:(ti + 1) * 8], ga[:, ti * 8:(ti + 1) * 8],
                                    bcol(gsm[:, ti:ti + 1], 8), ALU.subtract)
        c.semc["fv"] = f0 + 2
        with nc.Block() as blk:
            @blk.scalar
            def _(s):
                for ti in range(8):
                    s.activation(gb[:, ti * 8:(ti + 1) * 8], ga[:, ti * 8:(ti + 1) * 8],
                                 AF.Exp)
        f0 = c.semc["fv"]
        with nc.Block() as blk:
            @blk.vector
            def _(v):
                fc = f0
                def fence(ins):
                    nonlocal fc
                    ins.then_inc(smf, 1); fc += 1
                    v.wait_ge(smf, fc)
                for ti in range(8):
                    ins = v.reduce_sum(gss[:, ti:ti + 1], gb[:, ti * 8:(ti + 1) * 8], axis=AX)
                for ti in range(8):
                    ins = v.reciprocal(gsc[:, ti:ti + 1], gss[:, ti:ti + 1])
                fence(ins)
                for ti in range(8):
                    sl = slice(ti * 8, (ti + 1) * 8)
                    v.tensor_tensor(ga[:, sl], gb[:, sl], bcol(gsc[:, ti:ti + 1], 8),
                                    ALU.mult)
                for ti in range(8):
                    ins = v.reduce_max(gsm[:, ti:ti + 1], ga[:, ti * 8:(ti + 1) * 8], axis=AX)
                fence(ins)
                for ti in range(8):
                    sl = slice(ti * 8, (ti + 1) * 8)
                    v.tensor_tensor(gb[:, sl], ga[:, sl], bcol(gsm[:, ti:ti + 1], 8),
                                    ALU.is_ge)
                for ti in range(8):
                    sl = slice(ti * 8, (ti + 1) * 8)
                    v.scalar_tensor_tensor(gb[:, sl], gb[:, sl], -1e9, ga[:, sl],
                                           ALU.mult, ALU.add)
                for ti in range(8):
                    ins = v.reduce_max(gsm[:, ti:ti + 1], gb[:, ti * 8:(ti + 1) * 8], axis=AX)
                fence(ins)
                for ti in range(8):
                    sl = slice(ti * 8, (ti + 1) * 8)
                    v.tensor_tensor(gb[:, sl], ga[:, sl], bcol(gsm[:, ti:ti + 1], 8),
                                    ALU.is_ge)
                for ti in range(8):
                    sl = slice(ti * 8, (ti + 1) * 8)
                    v.tensor_mul(gb[:, sl], ga[:, sl], gb[:, sl])
                for ti in range(8):
                    v.reduce_sum(gss[:, ti:ti + 1], gb[:, ti * 8:(ti + 1) * 8], axis=AX)
                for ti in range(8):
                    ins = v.tensor_single_scalar(gss[:, ti:ti + 1], gss[:, ti:ti + 1],
                                                 1e-8, ALU.add)
                for ti in range(8):
                    ins = v.reciprocal(gsc[:, ti:ti + 1], gss[:, ti:ti + 1])
                fence(ins)
                for ti in range(8):
                    sl = slice(ti * 8, (ti + 1) * 8)
                    v.tensor_tensor(cw_all[:, sl], gb[:, sl],
                                    bcol(gsc[:, ti:ti + 1], 8), ALU.mult)
        c.semc["fv"] = fc if False else c.semc["fv"] + 4
        mm0, ev0 = c.semc["mm"], c.semc["ev"]
        with nc.Block() as blk:
            @blk.tensor
            def _(t):
                for ti in range(8):
                    if ti >= 2:
                        t.wait_ge(sm_ev, ev0 + ti - 1)
                    t.transpose(pp[ti % 2][0:8, 0:128], cw_all[:, ti * 8:(ti + 1) * 8],
                                idf[:]).then_inc(sm_mm, 1)
            @blk.vector
            def _(v):
                for ti in range(8):
                    v.wait_ge(sm_mm, mm0 + ti + 1)
                    v.tensor_copy(cwT[:, ti * 128:(ti + 1) * 128],
                                  pp[ti % 2][0:8, 0:128]).then_inc(sm_ev, 1)
        c.semc["mm"] = mm0 + 8; c.semc["ev"] = ev0 + 8

        for e in range(8):
            loadblk(c, [(w1e[:].rearrange("p (kc o) -> p kc o", kc=8),
                         dW["moe_W1f"].ap()[e * 1024:(e + 1) * 1024, :]
                         .rearrange("(kc p) o -> p kc o", p=128)),
                        (w2e[:].rearrange("p (kc o) -> p kc o", kc=16),
                         dW["moe_W2f"].ap()[e * 2048:(e + 1) * 2048, :]
                         .rearrange("(kc p) o -> p kc o", p=128))])
            mm0, ev0 = c.semc["mm"], c.semc["ev"]
            with nc.Block() as blk:
                @blk.tensor
                def _(t):
                    for tt in range(2):
                        t.matmul(pp[tt][:], oh8[:, e * 128:(e + 1) * 128],
                                 cwT[:, tt * 512:(tt + 1) * 512],
                                 start=True, stop=True).then_inc(sm_mm, 1)
                @blk.vector
                def _(v):
                    for tt in range(2):
                        v.wait_ge(sm_mm, mm0 + tt + 1)
                        v.tensor_copy(cwb[:, tt * 512:(tt + 1) * 512], pp[tt][:])
            c.semc["mm"] = mm0 + 2; c.semc["ev"] = ev0

            mm0, ev0, as0, v0 = (c.semc[s] for s in ("mm", "ev", "as", "vv"))
            with nc.Block() as blk:
                @blk.tensor
                def _(t):
                    nmm = mm0
                    for tt in range(2):
                        for ho in range(16):
                            jh = tt * 16 + ho
                            if jh >= 2:
                                t.wait_ge(sm_as, as0 + jh - 1)
                            for kc in range(8):
                                mmi = t.matmul(pp[2 + jh % 2][:],
                                               w1e[:, kc * 2048 + ho * 128: kc * 2048 + (ho + 1) * 128],
                                               xn[:, kc * TH + tt * 512: kc * TH + (tt + 1) * 512],
                                               start=(kc == 0), stop=(kc == 7))
                            mmi.then_inc(sm_mm, 1)
                            nmm += 1
                        t.wait_ge(sm_v, v0 + tt + 1)
                        for oo in range(8):
                            jw = tt * 8 + oo
                            if jw >= 2:
                                t.wait_ge(sm_ev, ev0 + jw - 1)
                            for kc in range(16):
                                mmi = t.matmul(pp[jw % 2][:],
                                               w2e[:, kc * 1024 + oo * 128: kc * 1024 + (oo + 1) * 128],
                                               hsb[:, kc * 512:(kc + 1) * 512],
                                               start=(kc == 0), stop=(kc == 15),
                                               skip_group_check=True)
                            mmi.then_inc(sm_mm, 1)
                            nmm += 1
                @blk.scalar
                def _(s):
                    for tt in range(2):
                        for ho in range(16):
                            jh = tt * 16 + ho
                            s.wait_ge(sm_mm, mm0 + tt * 24 + ho + 1)
                            s.activation(hsb[:, ho * 512:(ho + 1) * 512], pp[2 + jh % 2][:],
                                         AF.Relu, bias=b1c[:, e * 16 + ho:e * 16 + ho + 1]
                                         ).then_inc(sm_as, 1)
                @blk.vector
                def _(v):
                    for tt in range(2):
                        v.wait_ge(sm_as, as0 + 16 * (tt + 1))
                        v.tensor_mul(hsb[:], hsb[:],
                                     brep(cwb[:, tt * 512:(tt + 1) * 512], 16, 512)
                                     ).then_inc(sm_v, 1)
                        for oo in range(8):
                            jw = tt * 8 + oo
                            v.wait_ge(sm_mm, mm0 + tt * 24 + 16 + oo + 1)
                            dsl = acc[:, oo * TH + tt * 512: oo * TH + (tt + 1) * 512]
                            if e == 0:
                                v.tensor_copy(dsl, pp[jw % 2][:]).then_inc(sm_ev, 1)
                            else:
                                v.tensor_add(dsl, dsl, pp[jw % 2][:]).then_inc(sm_ev, 1)
            c.semc["mm"] = mm0 + 48; c.semc["ev"] = ev0 + 16
            c.semc["as"] = as0 + 32; c.semc["vv"] = v0 + 2

        mm0, ev0 = c.semc["mm"], c.semc["ev"]
        with nc.Block() as blk:
            @blk.tensor
            def _(t):
                for j in range(16):
                    oo, tt = j // 2, j % 2
                    if j >= 2:
                        t.wait_ge(sm_ev, ev0 + j - 1)
                    t.matmul(pp[j % 2][:], b2sb[0:8, oo * 128:(oo + 1) * 128],
                             cwT[:, tt * 512:(tt + 1) * 512],
                             start=True, stop=True).then_inc(sm_mm, 1)
            @blk.vector
            def _(v):
                for j in range(16):
                    oo, tt = j // 2, j % 2
                    v.wait_ge(sm_mm, mm0 + j + 1)
                    dsl = acc[:, oo * TH + tt * 512: oo * TH + (tt + 1) * 512]
                    v.tensor_add(dsl, dsl, pp[j % 2][:]).then_inc(sm_ev, 1)
        c.semc["mm"] = mm0 + 16; c.semc["ev"] = ev0 + 16

        mm0, ev0, dma0, st0 = c.semc["mm"], c.semc["ev"], c.semc["dma"], c.semc["st"]
        with nc.Block() as blk:
            @blk.gpsimd
            def _(g):
                for j in range(16):
                    oo, tt = j // 2, j % 2
                    if j >= 2:
                        g.wait_ge(sm_ev, ev0 + j - 1)
                    g.dma_start(xt[j % 2][:],
                                dx1.ap()[oo * 128:(oo + 1) * 128,
                                         t0 + tt * 512:t0 + (tt + 1) * 512]).then_inc(dma, 16)
            @blk.vector
            def _(v):
                for j in range(16):
                    oo, tt = j // 2, j % 2
                    v.wait_ge(dma, dma0 + 16 * (j + 1))
                    if j >= 2:
                        v.wait_ge(st_sem, st0 + 16 * (j - 1))
                    asl = acc[:, oo * TH + tt * 512: oo * TH + (tt + 1) * 512]
                    v.tensor_add(ost[j % 2][:], asl, xt[j % 2][:])
                    v.tensor_add(ost[j % 2][:], ost[j % 2][:],
                                 xn[:, oo * TH + tt * 512: oo * TH + (tt + 1) * 512]
                                 ).then_inc(sm_ev, 1)
            @blk.sync
            def _(sy):
                for j in range(16):
                    oo, tt = j // 2, j % 2
                    sy.wait_ge(sm_ev, ev0 + j + 1)
                    sy.dma_start(dout_x2.ap()[oo * 128:(oo + 1) * 128,
                                              t0 + tt * 512:t0 + (tt + 1) * 512],
                                 ost[j % 2][:]).then_inc(st_sem, 16)
                sy.wait_ge(st_sem, st0 + 16 * 16)
        c.semc["ev"] = ev0 + 16; c.semc["dma"] = dma0 + 16 * 16
        c.semc["st"] = st0 + 16 * 16
    return dict(xn=xn, acc=acc, cw_all=cw_all, cwT=cwT, hsb=hsb,
                mu=S["mu"], rstd=S["rstd"], negcs=negcs, cwb=cwb,
                ga=ga, gb=gb, gsc=gsc, gsm=gsm, gss=gss)


def dumpblk(c, pairs):
    loadblk(c, pairs)



def emit_l1attn(c, dx2, dxh63, dW, dout_x3):
    nc = c.nc
    TF = 2176
    ones_bf = c.sb("a_ones", [128, 128], BF16)
    onesrow_bf = c.sb("a_onesr", [1, 128], BF16)
    bq = c.sb("a_bq", [128, 8]); bk = c.sb("a_bk", [128, 8])
    bout = c.sb("a_bout", [128, 8])
    bv = c.sb("a_bv", [1, 1024], BF16)
    maskR = c.sb("a_maskR", [128, 256]); mask0 = c.sb("a_mask0", [128, 256])
    idb = c.sb("a_idb", [128, 128], BF16)
    wv = c.sb("a_wv", [128, 8 * 1024], BF16)
    wout = c.sb("a_wout", [128, 8 * 1024], BF16)
    loadblk(c, [(ones_bf[:], dW["ones_bf"].ap()), (onesrow_bf[:], dW["onesrow_bf"].ap()),
                (bq[:], dW["l1a_bq"].ap()), (bk[:], dW["l1a_bk"].ap()),
                (bout[:], dW["l1a_bout"].ap()),
                (bv[:], dW["l1a_bv"].ap()),
                (maskR[:], dW["l1a_maskR"].ap()), (mask0[:], dW["l1a_mask0"].ap()),
                (idb[:], dW["ident_bf"].ap()),
                (wv[:].rearrange("p (kc o) -> p kc o", kc=8),
                 dW["l1a_wvT"].ap().rearrange("(kc p) o -> p kc o", p=128)),
                (wout[:].rearrange("p (kc o) -> p kc o", kc=8),
                 dW["l1a_woutT"].ap().rearrange("(kc p) o -> p kc o", p=128))])

    xn2 = c.sb("a_xn", [128, 8 * TF], BF16)
    q_sb = c.sb("a_q", [128, 8 * 2048], BF16)
    k_sb = c.sb("a_k", [128, 8 * TF], BF16)
    v_sb = c.sb("a_v", [128, 17 * 1024], BF16)
    o_sb = c.sb("a_o", [128, 8 * 128], BF16)
    x2t = c.sb("a_x2t", [128, 8 * 128])
    ssb_all = c.sb("a_ssba", [128, 16 * 256])

    ost = [c.sb(f"a_ost{i}", [128, 128]) for i in range(2)]
    wbq = [c.sb(f"a_wbq{i}", [128, 8 * 128], BF16) for i in range(2)]
    P = get_pool(c)
    S = dict(sq=q_sb,
             ps_a=P["psa"], ps_b=P["psb"],
             mu=ssb_all[:, 0:1024], va=ssb_all[:, 1024:2048],
             rstd=ssb_all[:, 2048:3072], st=None)
    S["st"] = x2t
    pbf = P["pbf"]
    pp = P["pp"]
    sm_mm, sm_ev, sm_as, sm_wl = (c.sem("mm"), c.sem("ev"), c.sem("as"), c.sem("wl"))
    sm_v = c.sem("vv"); smf = c.sem("fv")
    dma = c.sem("dma"); st_sem = c.sem("st")

    with nc.Block() as blk:
        @blk.gpsimd
        def _(g):
            g.memset(xn2[:].rearrange("p (kc t) -> p kc t", kc=8)[:, :, 0:65], 0.0)
    loadblk(c, [(xn2[:].rearrange("p (kc t) -> p kc t", kc=8)[:, :, 65:128],
                 dxh63.rearrange("(kc p) t -> p kc t", p=128)),
                (xn2[:].rearrange("p (kc t) -> p kc t", kc=8)[:, :, 128:TF],
                 fview(dx2, 0, 2048))])
    for (off, Tt) in ((0, 1024), (1024, 1024), (2048, 128)):
        emit_ln(c, lambda kc, off=off, Tt=Tt: xn2[:, kc * TF + off: kc * TF + off + Tt],
                ones_bf, S, Tt)

    for (wname, dst, bias, Tt, toff) in (("l1a_wqT", q_sb, bq, 2048, 128),
                                         ("l1a_wkT", k_sb, bk, TF, 0)):
        mm0, ev0, wl0 = c.semc["mm"], c.semc["ev"], c.semc["wl"]
        ntt = (Tt + 511) // 512
        with nc.Block() as blk:
            @blk.gpsimd
            def _(g):
                for o in range(8):
                    if o >= 2:
                        g.wait_ge(sm_mm, mm0 + ntt * (o - 1))
                    g.dma_start(wbq[o % 2][:], wslice(dW[wname], o * 128, 128)).then_inc(sm_wl, 16)
            @blk.tensor
            def _(t):
                for o in range(8):
                    t.wait_ge(sm_wl, wl0 + 16 * (o + 1))
                    for tt in range(ntt):
                        w = min(512, Tt - tt * 512)
                        j = o * ntt + tt
                        if j >= 2:
                            t.wait_ge(sm_ev, ev0 + j - 1)
                        for kc in range(8):
                            mmi = t.matmul(pp[j % 2][:, 0:w],
                                           wbq[o % 2][:, kc * 128:(kc + 1) * 128],
                                           xn2[:, kc * TF + toff + tt * 512: kc * TF + toff + tt * 512 + w],
                                           start=(kc == 0), stop=(kc == 7))
                        mmi.then_inc(sm_mm, 1)
            @blk.vector
            def _(v):
                for o in range(8):
                    for tt in range(ntt):
                        w = min(512, Tt - tt * 512)
                        j = o * ntt + tt
                        v.wait_ge(sm_mm, mm0 + j + 1)
                        v.tensor_scalar(dst[:, o * Tt + tt * 512: o * Tt + tt * 512 + w],
                                        pp[j % 2][:, 0:w], bias[:, o:o + 1], None,
                                        ALU.add).then_inc(sm_ev, 1)
        c.semc["mm"] = mm0 + 8 * ntt; c.semc["ev"] = ev0 + 8 * ntt
        c.semc["wl"] = wl0 + 16 * 8

    mm0, ev0 = c.semc["mm"], c.semc["ev"]
    with nc.Block() as blk:
        @blk.tensor
        def _(t):
            for ti in range(17):
                for oh in range(2):
                    j = ti * 2 + oh
                    if j >= 2:
                        t.wait_ge(sm_ev, ev0 + j - 1)
                    for kc in range(8):
                        t.matmul(pp[j % 2][:],
                                 xn2[:, kc * TF + ti * 128:kc * TF + (ti + 1) * 128],
                                 wv[:, kc * 1024 + oh * 512: kc * 1024 + (oh + 1) * 512],
                                 start=(kc == 0), stop=False)
                    mmi = t.matmul(pp[j % 2][:], onesrow_bf[:],
                                   bv[:, oh * 512:(oh + 1) * 512], start=False, stop=True)
                    mmi.then_inc(sm_mm, 1)
        @blk.vector
        def _(v):
            for ti in range(17):
                for oh in range(2):
                    j = ti * 2 + oh
                    v.wait_ge(sm_mm, mm0 + j + 1)
                    v.tensor_copy(v_sb[:, ti * 1024 + oh * 512: ti * 1024 + (oh + 1) * 512],
                                  pp[j % 2][:]).then_inc(sm_ev, 1)
    c.semc["mm"] = mm0 + 34; c.semc["ev"] = ev0 + 34

    SCALE = 0.125
    P_all = c.sb("a_Pall", [128, 16 * 256], BF16)
    PT_all = ssb_all[:].bitcast(BF16)[:, 0:4096]
    mc = c.sb("a_mc", [128, 48])

    def s_slot(h):
        ps = S["ps_a"] if h % 4 < 2 else S["ps_b"]
        return ps[:, (h % 2) * 512 + ((h // 4) % 2) * 256:
                  (h % 2) * 512 + ((h // 4) % 2) * 256 + 256]

    def t_slot(j):
        g = j // 4
        return pbf[:, (g % 2) * 1024 + (j % 4) * 128:(g % 2) * 1024 + (j % 4) * 128 + 128]

    def o_slot(h):
        ps = S["ps_a"] if h % 4 < 2 else S["ps_b"]
        return ps[0:64, (h % 2) * 512 + (h // 4) * 128:(h % 2) * 512 + (h // 4) * 128 + 128]

    for qb in range(16):
        mm0, ev0, as0, v0, dma0, st0 = (c.semc[s] for s in
                                        ("mm", "ev", "as", "vv", "dma", "st"))
        mask = mask0 if qb == 0 else maskR
        with nc.Block() as blk:
            @blk.gpsimd
            def _(g):
                g.dma_start(x2t[:].rearrange("p (kc t) -> p kc t", kc=8),
                            dx2.ap()[:, qb * 128:(qb + 1) * 128]
                            .rearrange("(kc p) t -> p kc t", p=128)).then_inc(dma, 16)
            @blk.tensor
            def _(t):
                for h in range(16):
                    kc, pb = h // 2, (h % 2) * 64
                    if h >= 4:
                        t.wait_ge(sm_ev, ev0 + h - 3)
                    t.matmul(s_slot(h),
                             q_sb[pb:pb + 64, kc * 2048 + qb * 128: kc * 2048 + (qb + 1) * 128],
                             k_sb[pb:pb + 64, kc * TF + qb * 128: kc * TF + qb * 128 + 256],
                             start=True, stop=True).then_inc(sm_mm, 1)
                t.wait_ge(sm_v, v0 + 2)
                for j in range(32):
                    h, ki = j // 2, j % 2
                    if j % 4 == 0 and j >= 8:
                        t.wait_ge(sm_ev, ev0 + 16 + 4 * (j // 4 - 1))
                    ins = t.transpose(t_slot(j),
                                      P_all[:, h * 256 + ki * 128: h * 256 + ki * 128 + 128],
                                      idb[:])
                    if j % 4 == 3:
                        ins.then_inc(sm_mm, 1)
                for h in range(16):
                    if h == 0:
                        t.wait_ge(sm_ev, ev0 + 48)
                    if h >= 4:
                        t.wait_ge(sm_ev, ev0 + 48 + h - 3)
                    for ki in range(2):
                        mmi = t.matmul(o_slot(h),
                                       v_sb[:, (qb + ki) * 1024 + h * 64: (qb + ki) * 1024 + (h + 1) * 64],
                                       PT_all[:, h * 256 + ki * 128: h * 256 + ki * 128 + 128],
                                       start=(ki == 0), stop=(ki == 1))
                    mmi.then_inc(sm_mm, 1)
                for o in range(8):
                    t.wait_ge(sm_ev, ev0 + 64 + (o - 1 if o >= 2 else 0))
                    for kc in range(8):
                        mmi = t.matmul(pp[o % 2][:, 0:128],
                                       wout[:, kc * 1024 + o * 128: kc * 1024 + (o + 1) * 128],
                                       o_sb[:, kc * 128:(kc + 1) * 128],
                                       start=(kc == 0), stop=(kc == 7))
                    mmi.then_inc(sm_mm, 1)
            @blk.vector
            def _(v):
                for h in range(16):
                    v.wait_ge(sm_mm, mm0 + h + 1)
                    v.scalar_tensor_tensor(ssb_all[:, h * 256:(h + 1) * 256], s_slot(h),
                                           SCALE, mask[:], ALU.mult,
                                           ALU.add).then_inc(sm_ev, 1)
                for h in range(16):
                    v.reduce_max(mc[:, h:h + 1], ssb_all[:, h * 256:(h + 1) * 256], axis=AX)
                for h in range(16):
                    ins = v.tensor_single_scalar(mc[:, 16 + h:17 + h], mc[:, h:h + 1],
                                                 -1.0, ALU.mult)
                ins.then_inc(sm_v, 1)
                v.wait_ge(sm_as, as0 + 1)
                for h in range(16):
                    v.reduce_sum(mc[:, 32 + h:33 + h], P_all[:, h * 256:(h + 1) * 256], axis=AX)
                for h in range(16):
                    v.reciprocal(mc[:, h:h + 1], mc[:, 32 + h:33 + h])
                for h in range(16):
                    ins = v.tensor_tensor(P_all[:, h * 256:(h + 1) * 256],
                                          P_all[:, h * 256:(h + 1) * 256],
                                          bcol(mc[:, h:h + 1], 256), ALU.mult)
                ins.then_inc(sm_v, 1)
                for j in range(32):
                    h, ki = j // 2, j % 2
                    v.wait_ge(sm_mm, mm0 + 16 + j // 4 + 1)
                    v.tensor_copy(PT_all[:, h * 256 + ki * 128: h * 256 + ki * 128 + 128],
                                  t_slot(j)).then_inc(sm_ev, 1)
                for h in range(16):
                    v.wait_ge(sm_mm, mm0 + 24 + h + 1)
                    v.tensor_copy(o_sb[(h % 2) * 64:(h % 2) * 64 + 64,
                                       (h // 2) * 128:(h // 2) * 128 + 128],
                                  o_slot(h)).then_inc(sm_ev, 1)
                for o in range(8):
                    v.wait_ge(sm_mm, mm0 + 40 + o + 1)
                    v.wait_ge(dma, dma0 + 16)
                    if o >= 2:
                        v.wait_ge(st_sem, st0 + 16 * (o - 1))
                    v.tensor_scalar(ost[o % 2][:], pp[o % 2][:, 0:128], bout[:, o:o + 1],
                                    None, ALU.add)
                    v.tensor_add(ost[o % 2][:], ost[o % 2][:],
                                 x2t[:, o * 128:(o + 1) * 128]).then_inc(sm_ev, 1)
            @blk.scalar
            def _(s):
                s.wait_ge(sm_v, v0 + 1)
                for h in range(16):
                    ins = s.activation(P_all[:, h * 256:(h + 1) * 256],
                                       ssb_all[:, h * 256:(h + 1) * 256], AF.Exp,
                                       bias=mc[:, 16 + h:17 + h])
                ins.then_inc(sm_as, 1)
            @blk.sync
            def _(sy):
                for o in range(8):
                    sy.wait_ge(sm_ev, ev0 + 64 + o + 1)
                    sy.dma_start(dout_x3.ap()[o * 128:(o + 1) * 128, qb * 128:(qb + 1) * 128],
                                 ost[o % 2][:]).then_inc(st_sem, 16)
                sy.wait_ge(st_sem, st0 + 16 * 8)
        c.semc["mm"] = mm0 + 48
        c.semc["ev"] = ev0 + 72
        c.semc["as"] = as0 + 1
        c.semc["vv"] = v0 + 2
        c.semc["dma"] = dma0 + 16
        c.semc["st"] = st0 + 16 * 8


def restride(ap2d, off, step, count):
    return bass.AP(ap2d.tensor, ap2d.offset + off, [list(ap2d.ap[0]), [step, count]])


def emit_l0attn(c, dxin, dS0, dW, dout_x1, dstate, dbg=False, dbg_outs=None,
                front=False, dmask=None):
    nc = c.nc
    TH = 1024
    ones_bf = c.sb("b_ones", [128, 128], BF16)
    onesrow_f = c.sb("b_onesrf", [1, 128])
    idf = c.sb("b_idf", [128, 128])
    idb = c.sb("b_idb", [128, 128], BF16)
    triu = c.sb("b_triu", [128, 128])
    convw = c.sb("b_convw", [128, 72])
    convb = c.sb("b_convb", [128, 18])
    dtb = c.sb("b_dtb", [32, 1])
    Acol = c.sb("b_acol", [32, 1])
    Drep = c.sb("b_drep", [128, 32])
    normw = c.sb("b_normw", [128, 16])
    ones32 = c.sb("b_ones32", [32, 128])
    woutr = c.sb("b_woutr", [128, 16 * 1024], BF16)
    carry = c.sb("b_carry", [128, 2048])
    prevb = c.sb("b_prevb", [128, 2048], BF16)
    da32 = c.sb("b_da32", [32, 128])
    cdl = c.sb("b_cdl", [32, 8])
    cdflat = c.sb("b_cdflat", [1, 256])
    Lm = c.sb("b_Lm", [33, 128])
    TD = c.sb("b_TD", [128, 64])
    ETexp = c.sb("b_ETexp", [128, 32])
    ddT = c.sb("b_ddT", [128, 32])
    GmT = c.sb("b_GmT", [128, 128], BF16)
    BT = c.sb("b_BT", [128, 128], BF16)
    rms1 = c.sb("b_rms1", [128, 4])
    ost = [c.sb(f"b_ost{i}", [128, 512]) for i in range(2)]
    xres = c.sb("b_xres", [128, 8 * 128])
    zst = [c.sb(f"b_zst{i}", [128, 512], BF16) for i in range(2)]
    winb = [c.sb(f"b_winb{i}", [128, 8 * 128], BF16) for i in range(2)]
    P = get_pool(c)
    psa = P["psa"]; psb = P["psb"]; pp = P["pp"]; pbf = P["pbf"]
    dz = c.dtmp("b_dz", [2048, 2048], BF16)
    sm_mm, sm_ev, sm_as, sm_wl = (c.sem("mm"), c.sem("ev"), c.sem("as"), c.sem("wl"))
    sm_v = c.sem("vv"); smf = c.sem("fv")
    dma = c.sem("dma"); st_sem = c.sem("st")

    loadblk(c, [(ones_bf[:], dW["ones_bf"].ap()), (onesrow_f[:], dW["onesrow_f"].ap()),
                (idf[:], dW["ident_f32"].ap()), (idb[:], dW["ident_bf"].ap()),
                (triu[:], dW["mb_triu"].ap()), (convw[:], dW["mb_convw"].ap()),
                (convb[:], dW["mb_convb"].ap()), (dtb[:], dW["mb_dtb"].ap()),
                (Acol[:], dW["mb_alog"].ap()), (Drep[:], dW["mb_drep"].ap()),
                (normw[:], dW["mb_normw"].ap()), (ones32[:], dW["ones32"].ap()),
                (woutr[:].rearrange("p (kc o) -> p kc o", kc=16),
                 dW["mb_woutT"].ap().rearrange("(kc p) o -> p kc o", p=128))]
            + ([(carry[:], dS0)] if dS0 is not None else []))
    if dS0 is None:
        with nc.Block() as blk:
            @blk.gpsimd
            def _(g):
                g.memset(carry[:], 0.0)
    if dmask is not None:
        mcol = c.sb("b_mcol", [128, 1])
        loadblk(c, [(mcol[:], dmask.ap())])
        with nc.Block() as blk:
            @blk.vector
            def _(v):
                v.tensor_mul(carry[:], carry[:], bcol(mcol[:], 2048))
    with nc.Block() as blk:
        @blk.scalar
        def _(s):
            s.activation(Acol[:], Acol[:], AF.Exp)
        @blk.gpsimd
        def _(g):
            g.memset(Lm[32:33, 0:128], 1.0)
    with nc.Block() as blk:
        @blk.vector
        def _(v):
            v.tensor_single_scalar(Acol[:], Acol[:], -1.0, ALU.mult)

    for half in range(1 if dbg else 2):
        t0g = half * TH
        c.push_scope()
        x4b = c.sb(f"h_x4b{half}", [128, 16 * 1024], BF16)
        BCh = c.sb(f"h_BC{half}", [128, 2 * 1024], BF16)
        CD2 = c.sb(f"h_CD2{half}", [64, 1024])
        CDB = c.sb(f"h_CDB{half}", [128, 256])

        c.push_scope()
        xnh = c.sb(f"p1_xn{half}", [128, 8 * 1027], BF16)
        xbc = c.sb(f"p1_xbc{half}", [128, 18 * 1027], BF16)
        wzb = [c.sb(f"p1_wzb{half}_{i}", [128, 8 * 512], BF16) for i in range(2)]
        S = dict(sq=xbc, ps_a=psa, ps_b=psb,
                 mu=c.sb(f"p1_mu{half}", [128, 512]), va=c.sb(f"p1_va{half}", [128, 512]),
                 rstd=c.sb(f"p1_rstd{half}", [128, 512]), st=ost[0])
        loadblk(c, [(xnh[:].rearrange("p (kc t) -> p kc t", kc=8),
                     fview(dxin, t0g, 1027))])
        for (off, Tt) in ((0, 512), (512, 512), (1024, 3)):
            emit_ln(c, lambda kc, off=off, Tt=Tt: xnh[:, kc * 1027 + off: kc * 1027 + off + Tt],
                    ones_bf, S, Tt)

        tjobs = ((0, 512), (512, 512), (1024, 3))
        mm0, ev0, wl0 = c.semc["mm"], c.semc["ev"], c.semc["wl"]
        with nc.Block() as blk:
            @blk.gpsimd
            def _(g):
                for o in range(19):
                    if o >= 2:
                        g.wait_ge(sm_mm, mm0 + 3 * (o - 1))
                    ow = 128 if o < 18 else 32
                    g.dma_start(winb[o % 2][:, 0:8 * ow],
                                wslice(dW["mb_winT"], 2048 + o * 128, ow)).then_inc(sm_wl, 16)
            @blk.tensor
            def _(t):
                for o in range(19):
                    ow = 128 if o < 18 else 32
                    t.wait_ge(sm_wl, wl0 + 16 * (o + 1))
                    for tj, (tt0, w) in enumerate(tjobs):
                        j = o * 3 + tj
                        if j >= 2:
                            t.wait_ge(sm_ev, ev0 + j - 1)
                        for kc in range(8):
                            mmi = t.matmul(pp[j % 2][0:ow, 0:w],
                                           winb[o % 2][:, kc * ow:(kc + 1) * ow],
                                           xnh[:, kc * 1027 + tt0: kc * 1027 + tt0 + w],
                                           start=(kc == 0), stop=(kc == 7))
                        mmi.then_inc(sm_mm, 1)
            @blk.vector
            def _(v):
                for o in range(19):
                    for tj, (tt0, w) in enumerate(tjobs):
                        j = o * 3 + tj
                        v.wait_ge(sm_mm, mm0 + j + 1)
                        if o < 18:
                            ins = v.tensor_copy(xbc[:, o * 1027 + tt0: o * 1027 + tt0 + w],
                                                pp[j % 2][:, 0:w])
                        else:
                            lo = max(tt0, 3); hi = tt0 + w
                            ins = v.tensor_copy(CD2[0:32, lo - 3:hi - 3],
                                                pp[j % 2][0:32, lo - tt0:hi - tt0])
                        ins.then_inc(sm_ev, 1)
        c.semc["mm"] = mm0 + 57; c.semc["ev"] = ev0 + 57; c.semc["wl"] = wl0 + 19 * 16

        if not front:
            mm0, ev0, wl0, st0 = (c.semc[s] for s in ("mm", "ev", "wl", "st"))
            with nc.Block() as blk:
                @blk.gpsimd
                def _(g):
                    for oz in range(4):
                        if oz >= 2:
                            g.wait_ge(sm_mm, mm0 + 8 * (oz - 1))
                        g.dma_start(wzb[oz % 2][:].rearrange("p (kc o) -> p kc o", kc=8),
                                    wslice(dW["mb_winT"], oz * 512, 512)).then_inc(sm_wl, 16)
                @blk.tensor
                def _(t):
                    for oz in range(4):
                        t.wait_ge(sm_wl, wl0 + 16 * (oz + 1))
                        for ti in range(8):
                            j = oz * 8 + ti
                            if j >= 2:
                                t.wait_ge(sm_ev, ev0 + j - 1)
                            for kc in range(8):
                                mmi = t.matmul(pp[j % 2][:],
                                               xnh[:, kc * 1027 + 3 + ti * 128: kc * 1027 + 3 + (ti + 1) * 128],
                                               wzb[oz % 2][:, kc * 512:(kc + 1) * 512],
                                               start=(kc == 0), stop=(kc == 7))
                            mmi.then_inc(sm_mm, 1)
                @blk.vector
                def _(v):
                    for oz in range(4):
                        for ti in range(8):
                            j = oz * 8 + ti
                            v.wait_ge(sm_mm, mm0 + j + 1)
                            if j >= 2:
                                v.wait_ge(st_sem, st0 + 16 * (j - 1))
                            v.tensor_copy(zst[j % 2][:], pp[j % 2][:]).then_inc(sm_ev, 1)
                @blk.sync
                def _(sy):
                    for oz in range(4):
                        for ti in range(8):
                            j = oz * 8 + ti
                            sy.wait_ge(sm_ev, ev0 + j + 1)
                            sy.dma_start(dz.ap()[t0g + ti * 128:t0g + (ti + 1) * 128,
                                                 oz * 512:(oz + 1) * 512],
                                         zst[j % 2][:]).then_inc(st_sem, 16)
                    sy.wait_ge(st_sem, st0 + 16 * 32)
            c.semc["mm"] = mm0 + 32; c.semc["ev"] = ev0 + 32
            c.semc["wl"] = wl0 + 16 * 4; c.semc["st"] = st0 + 16 * 32

        v0, as0 = c.semc["vv"], c.semc["as"]
        with nc.Block() as blk:
            @blk.vector
            def _(v):
                for ct in range(18):
                    ps = psa if ct % 2 == 0 else psb
                    if ct >= 2:
                        v.wait_ge(sm_as, as0 + ct - 1)
                    v.tensor_scalar_mul(ps[:], xbc[:, ct * 1027 + 0: ct * 1027 + 1024],
                                        convw[:, ct * 4:ct * 4 + 1])
                    for k in range(1, 4):
                        ins = v.scalar_tensor_tensor(ps[:], xbc[:, ct * 1027 + k: ct * 1027 + k + 1024],
                                                     convw[:, ct * 4 + k:ct * 4 + k + 1],
                                                     ps[:], ALU.mult, ALU.add)
                    ins.then_inc(sm_v, 1)
            @blk.scalar
            def _(s):
                for ct in range(18):
                    ps = psa if ct % 2 == 0 else psb
                    s.wait_ge(sm_v, v0 + ct + 1)
                    if ct < 16:
                        dst = x4b[:, ct * 1024:(ct + 1) * 1024]
                    else:
                        dst = BCh[:, (ct - 16) * 1024:(ct - 15) * 1024]
                    s.activation(dst, ps[:], AF.Silu,
                                 bias=convb[:, ct:ct + 1]).then_inc(sm_as, 1)
        c.semc["vv"] = v0 + 18; c.semc["as"] = as0 + 18
        if dbg_outs:
            loadblk(c, [(dbg_outs["xnh"].ap(), xnh[:]), (dbg_outs["xbc"].ap(), xbc[:])])
        c.pop_scope()

        c.push_scope()
        E = c.sb(f"p2_E{half}", [128, 4096], BF16)
        Mp = c.sb(f"p2_Mp{half}", [128, 4096], BF16)
        xT = c.sb(f"p2_xT{half}", [128, 2048], BF16)
        xtf = c.sb(f"p2_xtf{half}", [128, 2048], BF16)
        Rmat = c.sb(f"p2_Rmat{half}", [33, 4096])
        IDall = c.sb(f"p2_IDall{half}", [128, 4096], BF16)
        ytmp = c.sb(f"p2_ytmp{half}", [128, 1024])
        ysb = c.sb(f"p2_ysb{half}", [128, 2048])
        sz = c.sb(f"p2_sz{half}", [128, 2048])
        yn = c.sb(f"p2_yn{half}", [128, 2048], BF16)
        ynT = c.sb(f"p2_ynT{half}", [128, 2048], BF16)
        zch = c.sb(f"p2_zch{half}", [128, 2048], BF16)
        loadblk(c, [(Rmat[0:32, :], dW["mb_oh32"].ap())])
        if not front:
            with nc.Block() as blk:
                @blk.vector
                def _(v):
                    for h in range(32):
                        v.tensor_tensor(IDall[:, h * 128:(h + 1) * 128], idb[:],
                                        bcol(Drep[:, h:h + 1], 128), ALU.mult)
        with nc.Block() as blk:
            @blk.vector
            def _(v):
                v.tensor_scalar(CD2[0:32, :], CD2[0:32, :], dtb[:], None, ALU.add)
        with nc.Block() as blk:
            @blk.scalar
            def _(s):
                s.activation(ytmp[0:32, 0:1024], CD2[0:32, :], AF.Abs)
                s.activation(CD2[32:64, :], ytmp[0:32, 0:1024], AF.Exp, scale=f32(-1.0))
                s.activation(ytmp[0:32, 0:1024], CD2[32:64, :], AF.Ln, bias=f32(1.0))
        with nc.Block() as blk:
            @blk.vector
            def _(v):
                v.tensor_scalar(ysb[0:32, 0:1024], CD2[0:32, :], 0.0, None, ALU.max)
                v.tensor_add(CD2[0:32, :], ysb[0:32, 0:1024], ytmp[0:32, 0:1024])
        with nc.Block() as blk:
            @blk.vector
            def _(v):
                for ci in range(8):
                    v.tensor_scalar_mul(da32[:], CD2[0:32, ci * 128:(ci + 1) * 128], Acol[:])
                    v.tensor_tensor_scan(CD2[32:64, ci * 128:(ci + 1) * 128],
                                         da32[:], ones32[:], 0.0, ALU.add, ALU.mult)
        with nc.Block() as blk:
            @blk.scalar
            def _(s):
                s.activation(cdl[:], restride(CD2[32:64, 0:1024], 127, 128, 8), AF.Exp)
        dma0 = c.semc["dma"]
        with nc.Block() as blk:
            @blk.gpsimd
            def _(g):
                g.dma_start(cdflat[:].rearrange("p (a b) -> p a b", a=32), cdl[:]).then_inc(dma, 16)
                g.wait_ge(dma, dma0 + 16)
        c.semc["dma"] = dma0 + 16
        mm0 = c.semc["mm"]
        with nc.Block() as blk:
            @blk.tensor
            def _(t):
                t.matmul(psa[:, 0:256], onesrow_f[:], cdflat[:],
                         start=True, stop=True).then_inc(sm_mm, 1)
            @blk.vector
            def _(v):
                v.wait_ge(sm_mm, mm0 + 1)
                v.tensor_copy(CDB[:], psa[:, 0:256])
        c.semc["mm"] = mm0 + 1

        for ci in range(8):
            g0 = t0g + ci * 128
            csl = slice(ci * 128, (ci + 1) * 128)
            mm0, v0 = c.semc["mm"], c.semc["vv"]
            with nc.Block() as blk:
                @blk.tensor
                def _(t):
                    t.transpose(pp[0][:, 0:64], CD2[:, csl], idf[0:64, 0:64]).then_inc(sm_mm, 1)
                    for b in range(17):
                        if b >= 2:
                            t.wait_ge(sm_v, v0 + b - 1)
                        src = (x4b[:, b * 1024 + csl.start:b * 1024 + csl.stop] if b < 16
                               else BCh[:, csl])
                        t.transpose(pbf[:, (b % 2) * 1024:(b % 2) * 1024 + 128],
                                    src, idb[:]).then_inc(sm_mm, 1)
                @blk.vector
                def _(v):
                    v.wait_ge(sm_mm, mm0 + 1)
                    v.tensor_copy(TD[:], pp[0][:, 0:64])
                    for b in range(17):
                        v.wait_ge(sm_mm, mm0 + b + 2)
                        dst = xT[:, b * 128:(b + 1) * 128] if b < 16 else BT[:]
                        v.tensor_copy(dst, pbf[:, (b % 2) * 1024:(b % 2) * 1024 + 128]
                                      ).then_inc(sm_v, 1)
            c.semc["mm"] = mm0 + 18; c.semc["vv"] = v0 + 17

            dma0 = c.semc["dma"]
            with nc.Block() as blk:
                @blk.vector
                def _(v):
                    v.tensor_single_scalar(Lm[0:32, :], CD2[32:64, csl], -1.0, ALU.mult)
                @blk.gpsimd
                def _(g):
                    g.dma_start(Rmat[32:33, :].rearrange("p (a b) -> p a b", a=32),
                                CD2[32:64, csl]).then_inc(dma, 16)
                    g.wait_ge(dma, dma0 + 16)
                @blk.scalar
                def _(s):
                    s.activation(ETexp[:], TD[:, 32:64], AF.Exp)
            c.semc["dma"] = dma0 + 16

            mm0, v0, as0 = c.semc["mm"], c.semc["vv"], c.semc["as"]
            with nc.Block() as blk:
                @blk.tensor
                def _(t):
                    for j in range(8):
                        if j >= 4:
                            t.wait_ge(sm_v, v0 + j - 3)
                        ds = (psa if j % 4 < 2 else psb)[:, (j % 2) * 512:(j % 2) * 512 + 512]
                        t.matmul(ds, Lm[:], Rmat[:, j * 512:(j + 1) * 512],
                                 start=True, stop=True).then_inc(sm_mm, 1)
                @blk.vector
                def _(v):
                    for j in range(8):
                        v.wait_ge(sm_mm, mm0 + j + 1)
                        if j >= 2:
                            v.wait_ge(sm_as, as0 + j - 1)
                        ds = (psa if j % 4 < 2 else psb)[:, (j % 2) * 512:(j % 2) * 512 + 512]
                        v.tensor_scalar(ytmp[:, (j % 2) * 512:(j % 2) * 512 + 512], ds,
                                        0.0, None, ALU.min).then_inc(sm_v, 1)
                @blk.scalar
                def _(s):
                    for j in range(8):
                        s.wait_ge(sm_v, v0 + j + 1)
                        s.activation(E[:, j * 512:(j + 1) * 512],
                                     ytmp[:, (j % 2) * 512:(j % 2) * 512 + 512],
                                     AF.Exp).then_inc(sm_as, 1)
            c.semc["mm"] = mm0 + 8; c.semc["vv"] = v0 + 8; c.semc["as"] = as0 + 8

            mm0, v0, f0 = c.semc["mm"], c.semc["vv"], c.semc["fv"]
            with nc.Block() as blk:
                if not front:
                    @blk.tensor
                    def _(t):
                        t.matmul(pp[0][:, 0:128], BCh[:, csl], BCh[:, 1024 + csl.start:1024 + csl.stop],
                                 start=True, stop=True).then_inc(sm_mm, 1)
                @blk.vector
                def _(v):
                    if not front:
                        v.wait_ge(sm_mm, mm0 + 1)
                        v.tensor_tensor(GmT[:], pp[0][:, 0:128], triu[:], ALU.mult)
                        v.tensor_tensor(Mp[:], E[:],
                                        bass.AP(GmT[:].tensor, GmT[:].offset,
                                                [list(GmT[:].ap[0]), [0, 32], [1, 128]]),
                                        ALU.mult)
                        v.tensor_tensor(Mp[:], Mp[:],
                                        bass.AP(TD[:].tensor, TD[:].offset,
                                                [list(TD[:].ap[0]), [1, 32], [0, 128]]),
                                        ALU.mult)
                        v.tensor_tensor(Mp[:], Mp[:], IDall[:], ALU.add)
                    v.tensor_tensor(ddT[:], restride(E[:], 127, 128, 32), TD[:, 0:32],
                                    ALU.mult).then_inc(smf, 1)
                    v.wait_ge(smf, f0 + 1)
                    v.tensor_tensor(xtf[:], xT[:],
                                    bass.AP(ddT[:].tensor, ddT[:].offset,
                                            [list(ddT[:].ap[0]), [1, 32], [0, 64]]),
                                    ALU.mult).then_inc(sm_v, 1)
            c.semc["mm"] = mm0 + (0 if front else 1)
            c.semc["vv"] = v0 + 1; c.semc["fv"] = f0 + 1

            mm0, v0 = c.semc["mm"], c.semc["vv"]
            with nc.Block() as blk:
                @blk.tensor
                def _(t):
                    for j in range(4):
                        ds = (psa if j < 2 else psb)[:, (j % 2) * 512:(j % 2) * 512 + 512]
                        t.matmul(ds, BT[:], xtf[:, j * 512:(j + 1) * 512],
                                 start=True, stop=True).then_inc(sm_mm, 1)
                @blk.vector
                def _(v):
                    v.wait_ge(sm_mm, mm0 + 4)
                    v.tensor_copy(prevb[:], carry[:])
                    v.tensor_tensor(sz[:], carry[:],
                                    bass.AP(CDB[:].tensor, CDB[:].offset + ci,
                                            [list(CDB[:].ap[0]), [8, 32], [0, 64]]),
                                    ALU.mult)
                    v.tensor_add(carry[:, 0:1024], sz[:, 0:1024], psa[:])
                    v.tensor_add(carry[:, 1024:2048], sz[:, 1024:2048], psb[:])
            c.semc["mm"] = mm0 + 4

            if not front:
                mm0, v0 = c.semc["mm"], c.semc["vv"]
                with nc.Block() as blk:
                    @blk.tensor
                    def _(t):
                        for h in range(32):
                            ds = (psa if h < 16 else psb)[:, (h % 16) * 64:(h % 16) * 64 + 64]
                            mmi = t.matmul(ds, Mp[:, h * 128:(h + 1) * 128],
                                           xT[:, h * 64:(h + 1) * 64],
                                           start=True, stop=True)
                        mmi.then_inc(sm_mm, 1)
                        for j in range(2):
                            mmi = t.matmul(pp[j][:], BCh[:, 1024 + csl.start:1024 + csl.stop],
                                           prevb[:, j * 512:(j + 1) * 512],
                                           start=True, stop=True)
                        mmi.then_inc(sm_mm, 1)
                        t.wait_ge(sm_v, v0 + 1)
                        for j in range(2):
                            mmi = t.matmul(pp[j][:], BCh[:, 1024 + csl.start:1024 + csl.stop],
                                           prevb[:, 1024 + j * 512:1024 + (j + 1) * 512],
                                           start=True, stop=True)
                        mmi.then_inc(sm_mm, 1)
                    @blk.vector
                    def _(v):
                        v.wait_ge(sm_mm, mm0 + 2)
                        for j in range(2):
                            v.tensor_tensor(ytmp[:, j * 512:(j + 1) * 512], pp[j][:],
                                            bass.AP(ETexp[:].tensor, ETexp[:].offset + j * 8,
                                                    [list(ETexp[:].ap[0]), [1, 8], [0, 64]]),
                                            ALU.mult)
                        v.tensor_add(ysb[:, 0:1024], ytmp[:], psa[:]).then_inc(sm_v, 1)
                        v.wait_ge(sm_mm, mm0 + 3)
                        for j in range(2):
                            v.tensor_tensor(ytmp[:, j * 512:(j + 1) * 512], pp[j][:],
                                            bass.AP(ETexp[:].tensor, ETexp[:].offset + 16 + j * 8,
                                                    [list(ETexp[:].ap[0]), [1, 8], [0, 64]]),
                                            ALU.mult)
                        v.tensor_add(ysb[:, 1024:2048], ytmp[:], psb[:])
                c.semc["mm"] = mm0 + 3; c.semc["vv"] = v0 + 1

                dma0, as0, v0, f0 = (c.semc[s] for s in ("dma", "as", "vv", "fv"))
                with nc.Block() as blk:
                    @blk.gpsimd
                    def _(g):
                        g.dma_start(zch[:], dz.ap()[g0:g0 + 128, :]).then_inc(dma, 16)
                    @blk.scalar
                    def _(s):
                        s.wait_ge(dma, dma0 + 16)
                        s.activation(sz[:], zch[:], AF.Silu).then_inc(sm_as, 1)
                        s.wait_ge(sm_v, v0 + 1)
                        s.activation(rms1[:, 2:3], rms1[:, 1:2], AF.Ln, bias=EPS)
                        s.activation(rms1[:, 3:4], rms1[:, 2:3], AF.Exp,
                                     scale=f32(-0.5)).then_inc(sm_as, 1)
                    @blk.vector
                    def _(v):
                        v.wait_ge(sm_as, as0 + 1)
                        v.tensor_mul(ysb[:], ysb[:], sz[:])
                        v.tensor_mul(sz[:], ysb[:], ysb[:])
                        ins = v.reduce_sum(rms1[:, 0:1], sz[:], axis=AX)
                        ins.then_inc(smf, 1)
                        v.wait_ge(smf, f0 + 1)
                        v.tensor_single_scalar(rms1[:, 1:2], rms1[:, 0:1],
                                               1.0 / 2048.0, ALU.mult).then_inc(sm_v, 1)
                        v.wait_ge(sm_as, as0 + 2)
                        v.tensor_tensor(yn[:], ysb[:], bcol(rms1[:, 3:4], 2048), ALU.mult)
                        v.nop().then_inc(sm_v, 1)
                c.semc["dma"] = dma0 + 16; c.semc["as"] = as0 + 2
                c.semc["vv"] = v0 + 2; c.semc["fv"] = f0 + 1

                mm0, v0 = c.semc["mm"], c.semc["vv"]
                with nc.Block() as blk:
                    @blk.tensor
                    def _(t):
                        for b in range(16):
                            if b >= 2:
                                t.wait_ge(sm_v, v0 + b - 1)
                            t.transpose(pbf[:, (b % 2) * 1024:(b % 2) * 1024 + 128],
                                        yn[:, b * 128:(b + 1) * 128], idb[:]).then_inc(sm_mm, 1)
                    @blk.vector
                    def _(v):
                        for b in range(16):
                            v.wait_ge(sm_mm, mm0 + b + 1)
                            v.tensor_scalar_mul(ynT[:, b * 128:(b + 1) * 128],
                                                pbf[:, (b % 2) * 1024:(b % 2) * 1024 + 128],
                                                normw[:, b:b + 1]).then_inc(sm_v, 1)
                c.semc["mm"] = mm0 + 16; c.semc["vv"] = v0 + 16

                mm0, ev0, dma0, st0 = (c.semc[s] for s in ("mm", "ev", "dma", "st"))
                with nc.Block() as blk:
                    @blk.gpsimd
                    def _(g):
                        g.dma_start(xres[:].rearrange("p (kc t) -> p kc t", kc=8),
                                    fview(dxin, 3 + g0, 128)).then_inc(dma, 16)
                    @blk.tensor
                    def _(t):
                        for o in range(8):
                            if o >= 2:
                                t.wait_ge(sm_ev, ev0 + o - 1)
                            for kc in range(16):
                                mmi = t.matmul(pp[o % 2][:, 0:128],
                                               woutr[:, kc * 1024 + o * 128: kc * 1024 + (o + 1) * 128],
                                               ynT[:, kc * 128:(kc + 1) * 128],
                                               start=(kc == 0), stop=(kc == 15))
                            mmi.then_inc(sm_mm, 1)
                    @blk.vector
                    def _(v):
                        for o in range(8):
                            v.wait_ge(sm_mm, mm0 + o + 1)
                            v.wait_ge(dma, dma0 + 16)
                            if o >= 2:
                                v.wait_ge(st_sem, st0 + 16 * (o - 1))
                            v.tensor_add(ost[o % 2][:, 0:128], pp[o % 2][:, 0:128],
                                         xres[:, o * 128:(o + 1) * 128]).then_inc(sm_ev, 1)
                    @blk.sync
                    def _(sy):
                        for o in range(8):
                            sy.wait_ge(sm_ev, ev0 + o + 1)
                            sy.dma_start(dout_x1.ap()[o * 128:(o + 1) * 128, g0:g0 + 128],
                                         ost[o % 2][:, 0:128]).then_inc(st_sem, 16)
                        sy.wait_ge(st_sem, st0 + 16 * 8)
                c.semc["mm"] = mm0 + 8; c.semc["ev"] = ev0 + 8
                c.semc["dma"] = dma0 + 16; c.semc["st"] = st0 + 16 * 8
        if dbg:
            c.dbg_handles = dict(x4b=x4b, BCh=BCh, CD2=CD2, CDB=CDB, E=E, Mp=Mp,
                                 xT=xT, xtf=xtf, ysb=ysb, yn=yn, ynT=ynT, TD=TD,
                                 ETexp=ETexp, ddT=ddT, GmT=GmT, BT=BT, carry=carry,
                                 prevb=prevb, IDall=IDall, zch=zch, rms1=rms1)
            break
        c.pop_scope()
        c.pop_scope()

    st0 = c.semc["st"]
    with nc.Block() as blk:
        @blk.sync
        def _(sy):
            sy.dma_start(dstate.ap(), carry[:]).then_inc(st_sem, 16)
            sy.wait_ge(st_sem, st0 + 16)
    c.semc["st"] = st0 + 16


def _percol(v):
    return np.ascontiguousarray(v.reshape(-1, 128).T)


def prep_weights(I):
    W = {}
    W["ones_bf"] = np.ones((128, 128), dtype=bfd)
    W["ident_bf"] = np.eye(128, dtype=np.float32).astype(bfd)
    W["ident_f32"] = np.eye(128, dtype=np.float32)
    W["l1f_w1T"] = np.ascontiguousarray(I["L1_ffn_w1_w"].T)
    W["l1f_w2T"] = np.ascontiguousarray(I["L1_ffn_w2_w"].T)
    W["l1f_h1T"] = np.ascontiguousarray(I["L1_ffn_h1_w"].T)
    W["l1f_h2T"] = np.ascontiguousarray(I["L1_ffn_h2_w"].T)
    W["l1f_w3T"] = np.ascontiguousarray(I["L1_ffn_w3_w"].T)
    W["l1f_s1"] = _percol(I["L1_ffn_w1_s"])
    W["l1f_s2"] = _percol(I["L1_ffn_w2_s"])
    W["l1f_h1s"] = _percol(I["L1_ffn_h1_s"])
    W["l1f_h2s"] = _percol(I["L1_ffn_h2_s"])
    W["l1f_w3s"] = _percol(I["L1_ffn_w3_s"])
    W["onescol"] = np.ones((128, 1), dtype=np.float32)
    W["moe_gw"] = np.ascontiguousarray(I["L0_ffn_gate_W"])
    oh8 = np.zeros((8, 8 * 128), dtype=np.float32)
    for e in range(8):
        oh8[e, e * 128:(e + 1) * 128] = 1.0
    W["moe_oh8"] = oh8
    W["moe_b1c"] = np.concatenate([_percol(I["L0_ffn_b1"][e]) for e in range(8)], axis=1)
    W["moe_b2"] = np.ascontiguousarray(I["L0_ffn_b2"])
    W["moe_W1f"] = np.ascontiguousarray(I["L0_ffn_W1"].reshape(8192, 2048))
    W["moe_W2f"] = np.ascontiguousarray(I["L0_ffn_W2"].reshape(16384, 1024))
    W["onesrow_bf"] = np.ones((1, 128), dtype=bfd)
    wqkvT = np.ascontiguousarray(I["L1_attn_Wqkv"].T)
    W["l1a_wqT"] = np.ascontiguousarray(wqkvT[:, 0:1024])
    W["l1a_wkT"] = np.ascontiguousarray(wqkvT[:, 1024:2048])
    W["l1a_wvT"] = np.ascontiguousarray(wqkvT[:, 2048:3072])
    W["l1a_woutT"] = np.ascontiguousarray(I["L1_attn_Wout"].T)
    W["l1a_bq"] = _percol(I["L1_attn_bqkv"][0:1024])
    W["l1a_bk"] = _percol(I["L1_attn_bqkv"][1024:2048])
    W["l1a_bv"] = I["L1_attn_bqkv"][2048:3072].reshape(1, 1024).astype(bfd)
    W["l1a_bout"] = _percol(I["L1_attn_bout"])
    ll = np.arange(128)[:, None]; jj = np.arange(256)[None, :]
    mR = np.where((jj >= ll + 65) & (jj <= ll + 128), 0.0, -1e9).astype(np.float32)
    m0 = np.where((jj >= ll + 65) & (jj <= ll + 128) & (jj >= 128), 0.0, -1e9).astype(np.float32)
    W["l1a_maskR"] = mR
    W["l1a_mask0"] = m0
    W["onesrow_f"] = np.ones((1, 128), dtype=np.float32)
    W["ones32"] = np.ones((32, 128), dtype=np.float32)
    W["mb_winT"] = np.ascontiguousarray(I["L0_attn_W_in"].T)
    cw4 = I["L0_attn_conv_w"]
    W["mb_convw"] = np.ascontiguousarray(
        np.concatenate([cw4[ct * 128:(ct + 1) * 128, :] for ct in range(18)], axis=1))
    W["mb_convb"] = _percol(I["L0_attn_conv_b"])
    W["mb_dtb"] = np.ascontiguousarray(I["L0_attn_dt_bias"][:, None])
    W["mb_alog"] = np.ascontiguousarray(I["L0_attn_A_log"][:, None])
    W["mb_drep"] = np.ascontiguousarray(np.tile(I["L0_attn_D"][None, :], (128, 1)))
    W["mb_normw"] = _percol(I["L0_attn_norm_w"])
    W["mb_woutT"] = np.ascontiguousarray(I["L0_attn_W_out"].T)
    W["mb_oh32"] = np.ascontiguousarray(
        np.repeat(np.eye(32, dtype=np.float32), 128, axis=1))
    W["mb_triu"] = np.triu(np.ones((128, 128), dtype=np.float32))
    return W


def _declare(c, W, names):
    d = {}
    for nm in names:
        a = W[nm]
        dt = BF16 if a.dtype == bfd else F32
        d[nm] = c.din(nm, a.shape, dt)
    return d


MOE_W = ["ones_bf", "onescol", "ident_f32", "moe_gw", "moe_oh8", "moe_b1c",
         "moe_b2", "moe_W1f", "moe_W2f"]


def launch_l0ffn(x1_shards, W):
    c = Ctx()
    dx1 = c.din("x1", [D, T])
    dW = _declare(c, W, MOE_W)
    do = c.dout("x2", [D, T])
    emit_l0ffn(c, dx1, dW, do)
    c.nc.finalize()
    in_maps = [dict({nm: W[nm] for nm in MOE_W}, x1=x1_shards[i]) for i in range(N_CORES)]
    res = run_bass_kernel_spmd(c.nc, in_maps, core_ids=list(range(N_CORES)))
    return [r["x2"] for r in res.results]


MAMBA_W = ["ones_bf", "onesrow_f", "ident_f32", "ident_bf", "ones32",
           "mb_winT", "mb_convw", "mb_convb", "mb_dtb", "mb_alog", "mb_drep",
           "mb_normw", "mb_woutT", "mb_oh32", "mb_triu"]


def shard_xin(x_full):
    out = []
    for cid in range(N_CORES):
        bi, hi = cid // 2, cid % 2
        if hi == 0:
            rows = np.concatenate([np.zeros((3, D), np.float32),
                                   x_full[bi, 0:T, :]], axis=0)
        else:
            rows = x_full[bi, T - 3:2 * T, :]
        out.append(np.ascontiguousarray(rows.T))
    return out


def launch_l0attn(xin_shards, S0_list, W):
    c = Ctx()
    dxin = c.din("xin", [D, 2051])
    dS0 = c.din("S0", [128, 2048])
    dW = _declare(c, W, MAMBA_W)
    do = c.dout("x1", [D, T])
    dst = c.dout("fstate", [128, 2048])
    emit_l0attn(c, dxin, dS0.ap(), dW, do, dst)
    c.nc.finalize()
    in_maps = [dict({nm: W[nm] for nm in MAMBA_W}, xin=xin_shards[i], S0=S0_list[i])
               for i in range(N_CORES)]
    res = run_bass_kernel_spmd(c.nc, in_maps, core_ids=list(range(N_CORES)))
    return [r["x1"] for r in res.results], [r["fstate"] for r in res.results]


ATTN_W = ["ones_bf", "onesrow_bf", "ident_bf", "l1a_wqT", "l1a_wkT", "l1a_wvT",
          "l1a_woutT", "l1a_bq", "l1a_bk", "l1a_bv", "l1a_bout", "l1a_maskR",
          "l1a_mask0"]


def launch_l1attn(x2_shards, W):
    c = Ctx()
    dx2 = c.din("x2", [D, T])
    dxh = c.din("xh", [D, 63])
    dW = _declare(c, W, ATTN_W)
    do = c.dout("x3", [D, T])
    emit_l1attn(c, dx2, dxh.ap(), dW, do)
    c.nc.finalize()
    in_maps = []
    for i in range(N_CORES):
        xh = np.zeros((D, 63), dtype=np.float32)
        m = dict({nm: W[nm] for nm in ATTN_W}, x2=x2_shards[i], xh=xh)
        if i % 2 == 1:
            m["xh"] = np.ascontiguousarray(x2_shards[i - 1][:, 2048 - 63:2048])
            m["l1a_mask0"] = W["l1a_maskR"]
        in_maps.append(m)
    res = run_bass_kernel_spmd(c.nc, in_maps, core_ids=list(range(N_CORES)))
    return [r["x3"] for r in res.results]


L1FFN_W = ["ones_bf", "l1f_w1T", "l1f_w2T", "l1f_h1T", "l1f_h2T", "l1f_w3T",
           "l1f_s1", "l1f_s2", "l1f_h1s", "l1f_h2s", "l1f_w3s"]


def launch_l1ffn(x3_shards, W):
    c = Ctx()
    dx3 = c.din("x3", [D, T])
    dW = _declare(c, W, L1FFN_W)
    do = c.dout("x4", [D, T])
    emit_l1ffn(c, dx3, dW, do)
    c.nc.finalize()
    in_maps = [dict({nm: W[nm] for nm in L1FFN_W}, x3=x3_shards[i]) for i in range(N_CORES)]
    res = run_bass_kernel_spmd(c.nc, in_maps, core_ids=list(range(N_CORES)))
    return [r["x4"] for r in res.results]


def shard_rows(full):
    out = []
    for cid in range(N_CORES):
        bi, hi = cid // 2, cid % 2
        out.append(np.ascontiguousarray(full[bi, hi * T:(hi + 1) * T, :].T))
    return out


def unshard_rows(shards):
    full = np.empty((4, 4096, D), dtype=np.float32)
    for cid in range(N_CORES):
        bi, hi = cid // 2, cid % 2
        full[bi, hi * T:(hi + 1) * T, :] = np.asarray(shards[cid]).T
    return full


ALL_W = None


def build_fused(W, dev=False):
    global ALL_W
    ALL_W = sorted(set(MAMBA_W) | set(MOE_W) | set(ATTN_W) | set(L1FFN_W))
    c = Ctx()
    nc = c.nc
    dxin = c.din("xin", [D, 2051])
    dmask = c.din("smask", [128, 1])
    dW = _declare(c, W, ALL_W)
    dout = c.dout("out", [D, T])
    mk = c.dout if dev else c.dtmp
    dx1 = mk("f_x1", [D, T])
    dx2 = mk("f_x2", [D, T])
    dx3 = mk("f_x3", [D, T])
    dsend = c.dtmp("f_send", [128, 2048])
    drecv = c.dtmp("f_recv", [256, 2048])
    dsend2 = c.dtmp("f_send2", [D, 63])
    drecv2 = c.dtmp("f_recv2", [2 * D, 63])
    ddum = c.dtmp("f_dumstate", [128, 2048])
    cc_sem = c.sem("cc")

    c.push_scope()
    with nc.named_scope("mamba_front"):
        emit_l0attn(c, dxin, None, dW, dx1, dsend, front=True)
    c.pop_scope()
    with nc.Block() as blk:
        @blk.gpsimd
        def _(g):
            g.collective_compute(
                "AllGather", ALU.bypass,
                replica_groups=[[0, 1], [2, 3], [4, 5], [6, 7]],
                ins=[dsend.ap().opt()], outs=[drecv.ap().opt()],
            ).then_inc(cc_sem)
            g.wait_ge(cc_sem, 1)
    c.semc["cc"] = 1
    c.push_scope()
    with nc.named_scope("mamba_full"):
        emit_l0attn(c, dxin, drecv.ap()[0:128, :], dW, dx1, ddum, dmask=dmask)
    c.pop_scope()
    c.push_scope()
    with nc.named_scope("moe"):
        emit_l0ffn(c, dx1, dW, dx2)
    c.pop_scope()
    dma, st_sem = c.sem("dma"), c.sem("st")
    dma0 = c.semc["dma"]
    with nc.Block() as blk:
        @blk.gpsimd
        def _(g):
            g.dma_start(dsend2.ap(), dx2.ap()[:, T - 63:T]).then_inc(dma, 16)
            g.wait_ge(dma, dma0 + 16)
            g.collective_compute(
                "AllGather", ALU.bypass,
                replica_groups=[[0, 1], [2, 3], [4, 5], [6, 7]],
                ins=[dsend2.ap().opt()], outs=[drecv2.ap().opt()],
            ).then_inc(cc_sem)
            g.wait_ge(cc_sem, 2)
    c.semc["dma"] = dma0 + 16
    c.semc["cc"] = 2
    c.push_scope()
    with nc.named_scope("attn"):
        emit_l1attn(c, dx2, drecv2.ap()[0:D, :], dW, dx3)
    c.pop_scope()
    c.push_scope()
    with nc.named_scope("l1ffn"):
        emit_l1ffn(c, dx3, dW, dout)
    c.pop_scope()
    return c


def kernel_fused(**inputs):
    W = prep_weights(inputs)
    c = build_fused(W)
    c.nc.finalize()
    xin_shards = shard_xin(inputs["L0_attn_inputs"])
    in_maps = []
    for i in range(N_CORES):
        m = {nm: W[nm] for nm in ALL_W}
        m["xin"] = xin_shards[i]
        m["smask"] = np.full((128, 1), 1.0 if i % 2 == 1 else 0.0, np.float32)
        if i % 2 == 1:
            m["l1a_mask0"] = W["l1a_maskR"]
        in_maps.append(m)
    res = run_bass_kernel_spmd(c.nc, in_maps, core_ids=list(range(N_CORES)))
    return unshard_rows([r["out"] for r in res.results])


def kernel_unfused(**inputs):
    W = prep_weights(inputs)
    xin_shards = shard_xin(inputs["L0_attn_inputs"])
    zeros = [np.zeros((128, 2048), np.float32) for _ in range(N_CORES)]
    _, F = launch_l0attn(xin_shards, zeros, W)
    S0 = [zeros[i] if i % 2 == 0 else np.ascontiguousarray(np.asarray(F[i - 1]))
          for i in range(N_CORES)]
    x1_shards, _ = launch_l0attn(xin_shards, S0, W)
    x1_shards = [np.ascontiguousarray(np.asarray(a)) for a in x1_shards]
    x2_shards = [np.ascontiguousarray(np.asarray(a)) for a in launch_l0ffn(x1_shards, W)]
    x3_shards = [np.ascontiguousarray(np.asarray(a)) for a in launch_l1attn(x2_shards, W)]
    x4_shards = launch_l1ffn(x3_shards, W)
    return unshard_rows(x4_shards)


def kernel(**inputs):
    return kernel_fused(**inputs)


# version 06
# speedup vs baseline: 1.1390x; 1.0826x over previous
import numpy as np
from contextlib import ExitStack
import concourse.bass as bass
import concourse.mybir as mybir
from concourse.bass_utils import run_bass_kernel_spmd
import ml_dtypes

F32 = mybir.dt.float32
BF16 = mybir.dt.bfloat16
AF = mybir.ActivationFunctionType
ALU = mybir.AluOpType
AX = mybir.AxisListType.X
bfd = ml_dtypes.bfloat16
f32 = np.float32

T = 2048
D = 1024
N_CORES = 8
EPS = f32(1e-5)


def ap0(ap, extra_dims):
    dims = [list(d) for d in ap.ap] + [list(d) for d in extra_dims]
    return bass.AP(ap.tensor, ap.offset, dims)


class Ctx:
    def __init__(self):
        self.nc = bass.Bass("TRN2", target_bir_lowering=False)
        self.es = ExitStack()
        self.semc = {}
        self._sems = {}

    def sem(self, name):
        if name not in self._sems:
            self._sems[name] = self.es.enter_context(self.nc.semaphore(name))
            self.semc[name] = 0
        return self._sems[name]

    def _uniq(self, name):
        if not hasattr(self, "_names"):
            self._names = set()
        base, n = name, 1
        while name in self._names:
            n += 1
            name = f"{base}_{n}"
        self._names.add(name)
        return name

    def sb(self, name, shape, dt=F32):
        es = self.scopes[-1] if getattr(self, "scopes", None) else self.es
        return es.enter_context(self.nc.sbuf_tensor(self._uniq(name), shape, dt))

    def sb_base(self, name, shape, dt=F32):
        return self.es.enter_context(self.nc.sbuf_tensor(self._uniq(name), shape, dt))

    def push_scope(self):
        if not hasattr(self, "scopes"):
            self.scopes = []
        self.scopes.append(ExitStack())

    def pop_scope(self):
        self.scopes.pop().close()

    def ps(self, name, shape, dt=F32):
        return self.es.enter_context(self.nc.psum_tensor(name, shape, dt))

    def dtmp(self, name, shape, dt=F32):
        return self.nc.dram_tensor(self._uniq(name), list(shape), dt)

    def din(self, name, shape, dt=F32):
        return self.nc.dram_tensor(name, list(shape), dt, kind="ExternalInput")

    def dout(self, name, shape, dt=F32):
        return self.nc.dram_tensor(name, list(shape), dt, kind="ExternalOutput")


def get_pool(c):
    if not hasattr(c, "_pool"):
        psa = c.ps("P_psa", [128, 1024])
        psb = c.ps("P_psb", [128, 1024])
        pp0 = c.ps("P_pp0", [128, 512])
        pp1 = c.ps("P_pp1", [128, 512])
        pbf = c.ps("P_pbf", [128, 2048], BF16)
        c._pool = dict(psa=psa, psb=psb, pbf=pbf,
                       pp=[pp0[:], pp1[:], psb[:, 0:512], psb[:, 512:1024]])
    return c._pool


def loadblk(c, pairs):
    nc, sem = c.nc, c.sem("dma")
    n0 = c.semc["dma"]
    with nc.Block() as blk:
        @blk.gpsimd
        def _(g):
            n = n0
            for dst, src in pairs:
                g.dma_start(dst, src).then_inc(sem, 16)
                n += 16
            g.wait_ge(sem, n)
    c.semc["dma"] = n0 + 16 * len(pairs)


def fview(d, t0, Tt):
    return d.ap()[:, t0:t0 + Tt].rearrange("(kc p) t -> p kc t", p=128)


def wslice(dw, o0, ow):
    return dw.ap()[:, o0:o0 + ow].rearrange("(kc p) o -> p kc o", p=128)


def emit_ln(c, xnsl, ones_bf, S, Tt, kcs=8, cast_from=None):
    nc = c.nc
    sq = S["sq"]
    if cast_from is not None:
        with nc.Block() as blk:
            @blk.vector
            def _(v):
                for kc in range(kcs):
                    v.tensor_copy(xnsl(kc), cast_from(kc))
    ps_a, ps_b = S["ps_a"], S["ps_b"]
    mu, va, rstd = S["mu"], S["va"], S["rstd"]
    rdim = 1.0 / (kcs * 128)
    with nc.Block() as blk:
        @blk.scalar
        def _(s):
            for kc in range(kcs):
                s.activation(sq[:, kc * Tt:(kc + 1) * Tt], xnsl(kc), AF.Square)
    with nc.Block() as blk:
        @blk.tensor
        def _(t):
            ntt = (Tt + 511) // 512
            for tt in range(ntt):
                w = min(512, Tt - tt * 512)
                for kc in range(kcs):
                    t.matmul(ps_a[:, tt * 512:tt * 512 + w], ones_bf[:],
                             xnsl(kc)[:, tt * 512:tt * 512 + w],
                             start=(kc == 0), stop=(kc == kcs - 1))
                for kc in range(kcs):
                    t.matmul(ps_b[:, tt * 512:tt * 512 + w], ones_bf[:],
                             sq[:, kc * Tt + tt * 512:kc * Tt + tt * 512 + w],
                             start=(kc == 0), stop=(kc == kcs - 1))
    smf = c.sem("fv")
    with nc.Block() as blk:
        @blk.vector
        def _(v):
            f0 = c.semc["fv"]
            v.tensor_single_scalar(mu[:, 0:Tt], ps_a[:, 0:Tt], rdim, ALU.mult)
            v.tensor_single_scalar(va[:, 0:Tt], ps_b[:, 0:Tt], rdim,
                                   ALU.mult).then_inc(smf, 1)
            v.wait_ge(smf, f0 + 1)
            v.tensor_mul(rstd[:, 0:Tt], mu[:, 0:Tt], mu[:, 0:Tt]).then_inc(smf, 1)
            v.wait_ge(smf, f0 + 2)
            v.tensor_sub(va[:, 0:Tt], va[:, 0:Tt], rstd[:, 0:Tt])
            c.semc["fv"] = f0 + 2
    with nc.Block() as blk:
        @blk.scalar
        def _(s):
            f0 = c.semc["fv"]
            s.activation(va[:, 0:Tt], va[:, 0:Tt], AF.Ln, bias=EPS).then_inc(smf, 1)
            s.wait_ge(smf, f0 + 1)
            s.activation(rstd[:, 0:Tt], va[:, 0:Tt], AF.Exp, scale=f32(-0.5))
            c.semc["fv"] = f0 + 1
    with nc.Block() as blk:
        @blk.vector
        def _(v):
            st = S["st"]
            f0 = c.semc["fv"]
            for kc in range(kcs):
                ntt = (Tt + 511) // 512
                for tt in range(ntt):
                    w = min(512, Tt - tt * 512)
                    v.tensor_sub(st[:, 0:w], xnsl(kc)[:, tt * 512:tt * 512 + w],
                                 mu[:, tt * 512:tt * 512 + w]).then_inc(smf, 1)
                    f0 += 1
                    v.wait_ge(smf, f0)
                    v.tensor_mul(xnsl(kc)[:, tt * 512:tt * 512 + w], st[:, 0:w],
                                 rstd[:, tt * 512:tt * 512 + w])
            c.semc["fv"] = f0


def emit_l1ffn(c, dx3, dW, dout_x4):
    nc = c.nc
    TH = 1024
    ones_bf = c.sb("l4_ones", [128, 128], BF16)
    s1 = c.sb("l4_s1", [128, 32]); s2 = c.sb("l4_s2", [128, 32])
    h1s = c.sb("l4_h1s", [128, 32]); h2s = c.sb("l4_h2s", [128, 32])
    w3s = c.sb("l4_w3s", [128, 8])
    loadblk(c, [(ones_bf[:], dW["ones_bf"].ap()),
                (s1[:], dW["l1f_s1"].ap()), (s2[:], dW["l1f_s2"].ap()),
                (h1s[:], dW["l1f_h1s"].ap()), (h2s[:], dW["l1f_h2s"].ap()),
                (w3s[:], dW["l1f_w3s"].ap())])

    xn = c.sb("l4_xn", [128, 8 * TH], BF16)
    hh = c.sb("l4_h", [128, 32 * TH], BF16)
    h2 = c.sb("l4_h2", [128, 32 * TH], BF16)
    P = get_pool(c)
    S = dict(sq=c.sb("l4_sq", [128, 8 * TH], BF16),
             ps_a=P["psa"], ps_b=P["psb"],
             mu=c.sb("l4_mu", [128, TH]), va=c.sb("l4_va", [128, TH]),
             rstd=c.sb("l4_rstd", [128, TH]), st=None)
    wb1 = [c.sb(f"l4_wb1_{i}", [128, 8 * 128], BF16) for i in range(2)]
    wb2 = [c.sb(f"l4_wb2_{i}", [128, 8 * 128], BF16) for i in range(2)]
    wbig = [c.sb(f"l4_wbg{i}", [128, 32 * 128], BF16) for i in range(2)]
    pp = P["pp"]
    atmp = [c.sb(f"l4_at{i}", [128, 512], BF16) for i in range(2)]
    ost = [c.sb(f"l4_ost{i}", [128, 512]) for i in range(2)]
    xt = [c.sb(f"l4_xt{i}", [128, 512]) for i in range(2)]
    S["st"] = ost[0]
    sm_mm, sm_ev, sm_as, sm_wl = (c.sem("mm"), c.sem("ev"), c.sem("as"), c.sem("wl"))
    dma = c.sem("dma")

    for half in range(2):
        t0 = half * TH
        loadblk(c, [(xn[:].rearrange("p (kc t) -> p kc t", kc=8), fview(dx3, t0, TH))])
        emit_ln(c, lambda kc: xn[:, kc * TH:(kc + 1) * TH], ones_bf, S, TH)

        mm0, ev0, as0, wl0 = (c.semc[s] for s in ("mm", "ev", "as", "wl"))
        with nc.Block() as blk:
            @blk.gpsimd
            def _(g):
                for o in range(32):
                    if o >= 2:
                        g.wait_ge(sm_mm, mm0 + 2 * (o - 1))
                    g.dma_start(wb1[o % 2][:], wslice(dW["l1f_w1T"], o * 128, 128)).then_inc(sm_wl, 16)
                    g.dma_start(wb2[o % 2][:], wslice(dW["l1f_w2T"], o * 128, 128)).then_inc(sm_wl, 16)
            @blk.tensor
            def _(t):
                for o in range(32):
                    t.wait_ge(sm_wl, wl0 + 32 * (o + 1))
                    for tt in range(2):
                        j = o * 2 + tt
                        if j >= 2:
                            t.wait_ge(sm_ev, ev0 + j - 1)
                            t.wait_ge(sm_as, as0 + j - 1)
                        p1, p2 = pp[2 * (j % 2)], pp[2 * (j % 2) + 1]
                        for kc in range(8):
                            t.matmul(p1[:], wb1[o % 2][:, kc * 128:(kc + 1) * 128],
                                     xn[:, kc * TH + tt * 512: kc * TH + (tt + 1) * 512],
                                     start=(kc == 0), stop=(kc == 7))
                        for kc in range(8):
                            mmi = t.matmul(p2[:], wb2[o % 2][:, kc * 128:(kc + 1) * 128],
                                           xn[:, kc * TH + tt * 512: kc * TH + (tt + 1) * 512],
                                           start=(kc == 0), stop=(kc == 7))
                        mmi.then_inc(sm_mm, 1)
            @blk.scalar
            def _(s):
                for o in range(32):
                    for tt in range(2):
                        j = o * 2 + tt
                        s.wait_ge(sm_mm, mm0 + j + 1)
                        if j >= 2:
                            s.wait_ge(sm_ev, ev0 + j - 1)
                        s.activation(atmp[j % 2][:], pp[2 * (j % 2)][:], AF.Silu,
                                     scale=s1[:, o:o + 1]).then_inc(sm_as, 1)
            @blk.vector
            def _(v):
                for o in range(32):
                    for tt in range(2):
                        j = o * 2 + tt
                        v.wait_ge(sm_mm, mm0 + j + 1)
                        v.tensor_scalar_mul(ost[j % 2][:], pp[2 * (j % 2) + 1][:], s2[:, o:o + 1])
                        v.wait_ge(sm_as, as0 + j + 1)
                        v.tensor_mul(hh[:, o * TH + tt * 512: o * TH + (tt + 1) * 512],
                                     atmp[j % 2][:], ost[j % 2][:]).then_inc(sm_ev, 1)
        c.semc["mm"] = mm0 + 64; c.semc["ev"] = ev0 + 64
        c.semc["as"] = as0 + 64; c.semc["wl"] = wl0 + 32 * 32

        for (dwname, src_t, dst_t, scol) in (("l1f_h1T", hh, h2, h1s),
                                             ("l1f_h2T", h2, hh, h2s)):
            mm0, ev0, wl0 = c.semc["mm"], c.semc["ev"], c.semc["wl"]
            dw = dW[dwname]
            with nc.Block() as blk:
                @blk.gpsimd
                def _(g):
                    for o in range(32):
                        if o >= 2:
                            g.wait_ge(sm_mm, mm0 + 2 * (o - 1))
                        g.dma_start(wbig[o % 2][:], wslice(dw, o * 128, 128)).then_inc(sm_wl, 16)
                @blk.tensor
                def _(t):
                    for o in range(32):
                        t.wait_ge(sm_wl, wl0 + 16 * (o + 1))
                        for tt in range(2):
                            j = o * 2 + tt
                            if j >= 2:
                                t.wait_ge(sm_ev, ev0 + j - 1)
                            for kc in range(32):
                                mmi = t.matmul(pp[j % 2][:],
                                               wbig[o % 2][:, kc * 128:(kc + 1) * 128],
                                               src_t[:, kc * TH + tt * 512: kc * TH + (tt + 1) * 512],
                                               start=(kc == 0), stop=(kc == 31))
                            mmi.then_inc(sm_mm, 1)
                @blk.scalar
                def _(s):
                    for o in range(32):
                        for tt in range(2):
                            j = o * 2 + tt
                            s.wait_ge(sm_mm, mm0 + j + 1)
                            s.activation(dst_t[:, o * TH + tt * 512: o * TH + (tt + 1) * 512],
                                         pp[j % 2][:], AF.Gelu,
                                         scale=scol[:, o:o + 1]).then_inc(sm_ev, 1)
            c.semc["mm"] = mm0 + 64; c.semc["ev"] = ev0 + 64
            c.semc["wl"] = wl0 + 16 * 32

        st_sem, st0 = c.sem("st"), c.semc["st"]
        mm0, ev0, wl0, dma0 = c.semc["mm"], c.semc["ev"], c.semc["wl"], c.semc["dma"]
        with nc.Block() as blk:
            @blk.gpsimd
            def _(g):
                for o in range(8):
                    if o >= 2:
                        g.wait_ge(sm_mm, mm0 + 2 * (o - 1))
                    g.dma_start(wbig[o % 2][:], wslice(dW["l1f_w3T"], o * 128, 128)).then_inc(sm_wl, 16)
                    for tt in range(2):
                        j = o * 2 + tt
                        if j >= 2:
                            g.wait_ge(sm_ev, ev0 + j - 1)
                        g.dma_start(xt[j % 2][:],
                                    dx3.ap()[o * 128:(o + 1) * 128,
                                             t0 + tt * 512:t0 + (tt + 1) * 512]).then_inc(dma, 16)
            @blk.tensor
            def _(t):
                for o in range(8):
                    t.wait_ge(sm_wl, wl0 + 16 * (o + 1))
                    for tt in range(2):
                        j = o * 2 + tt
                        if j >= 2:
                            t.wait_ge(sm_ev, ev0 + j - 1)
                        for kc in range(32):
                            mmi = t.matmul(pp[j % 2][:],
                                           wbig[o % 2][:, kc * 128:(kc + 1) * 128],
                                           hh[:, kc * TH + tt * 512: kc * TH + (tt + 1) * 512],
                                           start=(kc == 0), stop=(kc == 31))
                        mmi.then_inc(sm_mm, 1)
            @blk.vector
            def _(v):
                for o in range(8):
                    for tt in range(2):
                        j = o * 2 + tt
                        v.wait_ge(sm_mm, mm0 + j + 1)
                        v.wait_ge(dma, dma0 + 16 * (j + 1))
                        if j >= 2:
                            v.wait_ge(st_sem, st0 + 16 * (j - 1))
                        v.scalar_tensor_tensor(ost[j % 2][:], pp[j % 2][:], w3s[:, o:o + 1],
                                               xt[j % 2][:], ALU.mult, ALU.add).then_inc(sm_ev, 1)
            @blk.sync
            def _(sy):
                for o in range(8):
                    for tt in range(2):
                        j = o * 2 + tt
                        sy.wait_ge(sm_ev, ev0 + j + 1)
                        sy.dma_start(dout_x4.ap()[o * 128:(o + 1) * 128,
                                                  t0 + tt * 512:t0 + (tt + 1) * 512],
                                     ost[j % 2][:]).then_inc(st_sem, 16)
                sy.wait_ge(st_sem, st0 + 16 * 16)
        c.semc["mm"] = mm0 + 16; c.semc["ev"] = ev0 + 16
        c.semc["wl"] = wl0 + 16 * 8; c.semc["dma"] = dma0 + 16 * 16
        c.semc["st"] = st0 + 16 * 16



def bcol(ap_col, n):
    return bass.AP(ap_col.tensor, ap_col.offset, [list(ap_col.ap[0]), [0, n]])


def brep(ap2d, rep, inner):
    return bass.AP(ap2d.tensor, ap2d.offset,
                   [list(ap2d.ap[0]), [0, rep], list(ap2d.ap[1])[:1] + [inner]])


def emit_l0ffn(c, dx1, dW, dout_x2):
    nc = c.nc
    TH = 1024
    ones_bf = c.sb("m_ones", [128, 128], BF16)
    onescol = c.sb("m_onec", [128, 1])
    gw = c.sb("m_gw", [128, 8 * 8])
    oh8 = c.sb("m_oh8", [8, 8 * 128])
    b1c = c.sb("m_b1c", [128, 8 * 16])
    b2sb = c.sb("m_b2", [8, 1024])
    idf = c.sb("m_idf", [128, 128])
    loadblk(c, [(ones_bf[:], dW["ones_bf"].ap()), (onescol[:], dW["onescol"].ap()),
                (gw[:].rearrange("p (kc e) -> p kc e", kc=8),
                 dW["moe_gw"].ap().rearrange("(kc p) e -> p kc e", p=128)),
                (oh8[:], dW["moe_oh8"].ap()), (b1c[:], dW["moe_b1c"].ap()),
                (b2sb[:], dW["moe_b2"].ap()), (idf[:], dW["ident_f32"].ap())])

    xn = c.sb("m_xn", [128, 8 * TH], BF16)
    hsb = c.sb("m_h", [128, 16 * 512], BF16)
    acc = c.sb("m_acc", [128, 8 * TH])
    w1e = c.sb("m_w1e", [128, 8 * 2048], BF16)
    w2e = c.sb("m_w2e", [128, 16 * 1024], BF16)
    cwb = c.sb("m_cwb", [128, TH], BF16)
    cwT = c.sb("m_cwT", [8, TH])
    cw_all = c.sb("m_cwall", [128, 8 * 8])
    negcs = c.sb("m_negcs", [1, 8])
    ga = c.sb("m_ga", [128, 64]); gb = c.sb("m_gb", [128, 64])
    gsc = c.sb("m_gsc", [128, 8]); gsm = c.sb("m_gsm", [128, 8])
    gss = c.sb("m_gss", [128, 8])

    gx = [c.sb(f"m_gx{i}", [128, 8 * 128]) for i in range(2)]
    P = get_pool(c)
    S = dict(sq=acc[:, 3072:7168].bitcast(BF16),
             ps_a=P["psa"], ps_b=P["psb"],
             mu=acc[:, 0:1024], va=acc[:, 1024:2048],
             rstd=acc[:, 2048:3072], st=None)
    xt = [c.sb(f"m_xt{i}", [128, 512]) for i in range(2)]
    ost = [c.sb(f"m_ost{i}", [128, 512]) for i in range(2)]
    S["st"] = ost[0]
    pp = P["pp"]
    psg = S["ps_a"]
    sm_mm, sm_ev, sm_as, sm_wl = (c.sem("mm"), c.sem("ev"), c.sem("as"), c.sem("wl"))
    sm_v = c.sem("vv")
    dma = c.sem("dma")
    st_sem = c.sem("st")

    with nc.Block() as blk:
        @blk.tensor
        def _(t):
            for kc in range(8):
                mmi = t.matmul(psg[0:1, 0:8], onescol[:], gw[:, kc * 8:(kc + 1) * 8],
                               start=(kc == 0), stop=(kc == 7))
            mmi.then_inc(sm_mm, 1)
        @blk.vector
        def _(v):
            v.wait_ge(sm_mm, c.semc["mm"] + 1)
            v.tensor_single_scalar(negcs[:], psg[0:1, 0:8], -1.0, ALU.mult)
    c.semc["mm"] += 1

    for half in range(2):
        t0 = half * TH

        loadblk(c, [(xn[:].rearrange("p (kc t) -> p kc t", kc=8), fview(dx1, t0, TH))])
        emit_ln(c, lambda kc: xn[:, kc * TH:(kc + 1) * TH], ones_bf, S, TH)

        mm0, ev0, dma0 = (c.semc[s] for s in ("mm", "ev", "dma"))
        mu, rstd = S["mu"], S["rstd"]
        with nc.Block() as blk:
            @blk.gpsimd
            def _(g):
                for ti in range(8):
                    if ti >= 2:
                        g.wait_ge(sm_mm, mm0 + ti - 1)
                    g.dma_start(gx[ti % 2][:].rearrange("p (kc t) -> p kc t", kc=8),
                                dx1.ap()[:, t0 + ti * 128:t0 + (ti + 1) * 128]
                                .rearrange("(kc p) t -> p kc t", p=128)).then_inc(dma, 16)
            @blk.tensor
            def _(t):
                for ti in range(8):
                    t.wait_ge(dma, dma0 + 16 * (ti + 1))
                    po = ti * 16
                    for kc in range(8):
                        t.matmul(psg[:, po:po + 8], gx[ti % 2][:, kc * 128:(kc + 1) * 128],
                                 gw[:, kc * 8:(kc + 1) * 8], start=(kc == 0), stop=False)
                    t.matmul(psg[:, po:po + 8], mu[0:1, ti * 128:(ti + 1) * 128],
                             negcs[:], start=False, stop=True)
                    t.matmul(psg[:, po + 8:po + 9], rstd[0:1, ti * 128:(ti + 1) * 128],
                             onescol[0:1, 0:1], start=True, stop=True).then_inc(sm_mm, 1)
        c.semc["mm"] = mm0 + 8; c.semc["dma"] = dma0 + 16 * 8
        smf = c.sem("fv"); f0 = c.semc["fv"]
        with nc.Block() as blk:
            @blk.vector
            def _(v):
                fc = f0
                for ti in range(8):
                    ins = v.tensor_copy(gsc[:, ti:ti + 1], psg[:, ti * 16 + 8:ti * 16 + 9])
                ins.then_inc(smf, 1); fc += 1
                v.wait_ge(smf, fc)
                for ti in range(8):
                    ins = v.tensor_tensor(ga[:, ti * 8:(ti + 1) * 8], psg[:, ti * 16:ti * 16 + 8],
                                          bcol(gsc[:, ti:ti + 1], 8), ALU.mult)
                for ti in range(8):
                    ins = v.reduce_max(gsm[:, ti:ti + 1], ga[:, ti * 8:(ti + 1) * 8], axis=AX)
                ins.then_inc(smf, 1); fc += 1
                v.wait_ge(smf, fc)
                for ti in range(8):
                    v.tensor_tensor(ga[:, ti * 8:(ti + 1) * 8], ga[:, ti * 8:(ti + 1) * 8],
                                    bcol(gsm[:, ti:ti + 1], 8), ALU.subtract)
        c.semc["fv"] = f0 + 2
        with nc.Block() as blk:
            @blk.scalar
            def _(s):
                for ti in range(8):
                    s.activation(gb[:, ti * 8:(ti + 1) * 8], ga[:, ti * 8:(ti + 1) * 8],
                                 AF.Exp)
        f0 = c.semc["fv"]
        with nc.Block() as blk:
            @blk.vector
            def _(v):
                fc = f0
                def fence(ins):
                    nonlocal fc
                    ins.then_inc(smf, 1); fc += 1
                    v.wait_ge(smf, fc)
                for ti in range(8):
                    ins = v.reduce_sum(gss[:, ti:ti + 1], gb[:, ti * 8:(ti + 1) * 8], axis=AX)
                for ti in range(8):
                    ins = v.reciprocal(gsc[:, ti:ti + 1], gss[:, ti:ti + 1])
                fence(ins)
                for ti in range(8):
                    sl = slice(ti * 8, (ti + 1) * 8)
                    v.tensor_tensor(ga[:, sl], gb[:, sl], bcol(gsc[:, ti:ti + 1], 8),
                                    ALU.mult)
                for ti in range(8):
                    ins = v.reduce_max(gsm[:, ti:ti + 1], ga[:, ti * 8:(ti + 1) * 8], axis=AX)
                fence(ins)
                for ti in range(8):
                    sl = slice(ti * 8, (ti + 1) * 8)
                    v.tensor_tensor(gb[:, sl], ga[:, sl], bcol(gsm[:, ti:ti + 1], 8),
                                    ALU.is_ge)
                for ti in range(8):
                    sl = slice(ti * 8, (ti + 1) * 8)
                    v.scalar_tensor_tensor(gb[:, sl], gb[:, sl], -1e9, ga[:, sl],
                                           ALU.mult, ALU.add)
                for ti in range(8):
                    ins = v.reduce_max(gsm[:, ti:ti + 1], gb[:, ti * 8:(ti + 1) * 8], axis=AX)
                fence(ins)
                for ti in range(8):
                    sl = slice(ti * 8, (ti + 1) * 8)
                    v.tensor_tensor(gb[:, sl], ga[:, sl], bcol(gsm[:, ti:ti + 1], 8),
                                    ALU.is_ge)
                for ti in range(8):
                    sl = slice(ti * 8, (ti + 1) * 8)
                    v.tensor_mul(gb[:, sl], ga[:, sl], gb[:, sl])
                for ti in range(8):
                    v.reduce_sum(gss[:, ti:ti + 1], gb[:, ti * 8:(ti + 1) * 8], axis=AX)
                for ti in range(8):
                    ins = v.tensor_single_scalar(gss[:, ti:ti + 1], gss[:, ti:ti + 1],
                                                 1e-8, ALU.add)
                for ti in range(8):
                    ins = v.reciprocal(gsc[:, ti:ti + 1], gss[:, ti:ti + 1])
                fence(ins)
                for ti in range(8):
                    sl = slice(ti * 8, (ti + 1) * 8)
                    v.tensor_tensor(cw_all[:, sl], gb[:, sl],
                                    bcol(gsc[:, ti:ti + 1], 8), ALU.mult)
        c.semc["fv"] = fc if False else c.semc["fv"] + 4
        mm0, ev0 = c.semc["mm"], c.semc["ev"]
        with nc.Block() as blk:
            @blk.tensor
            def _(t):
                for ti in range(8):
                    if ti >= 2:
                        t.wait_ge(sm_ev, ev0 + ti - 1)
                    t.transpose(pp[ti % 2][0:8, 0:128], cw_all[:, ti * 8:(ti + 1) * 8],
                                idf[:]).then_inc(sm_mm, 1)
            @blk.vector
            def _(v):
                for ti in range(8):
                    v.wait_ge(sm_mm, mm0 + ti + 1)
                    v.tensor_copy(cwT[:, ti * 128:(ti + 1) * 128],
                                  pp[ti % 2][0:8, 0:128]).then_inc(sm_ev, 1)
        c.semc["mm"] = mm0 + 8; c.semc["ev"] = ev0 + 8

        for e in range(8):
            loadblk(c, [(w1e[:].rearrange("p (kc o) -> p kc o", kc=8),
                         dW["moe_W1f"].ap()[e * 1024:(e + 1) * 1024, :]
                         .rearrange("(kc p) o -> p kc o", p=128)),
                        (w2e[:].rearrange("p (kc o) -> p kc o", kc=16),
                         dW["moe_W2f"].ap()[e * 2048:(e + 1) * 2048, :]
                         .rearrange("(kc p) o -> p kc o", p=128))])
            mm0, ev0 = c.semc["mm"], c.semc["ev"]
            with nc.Block() as blk:
                @blk.tensor
                def _(t):
                    for tt in range(2):
                        t.matmul(pp[tt][:], oh8[:, e * 128:(e + 1) * 128],
                                 cwT[:, tt * 512:(tt + 1) * 512],
                                 start=True, stop=True).then_inc(sm_mm, 1)
                @blk.vector
                def _(v):
                    for tt in range(2):
                        v.wait_ge(sm_mm, mm0 + tt + 1)
                        v.tensor_copy(cwb[:, tt * 512:(tt + 1) * 512], pp[tt][:])
            c.semc["mm"] = mm0 + 2; c.semc["ev"] = ev0

            mm0, ev0, as0, v0 = (c.semc[s] for s in ("mm", "ev", "as", "vv"))
            with nc.Block() as blk:
                @blk.tensor
                def _(t):
                    nmm = mm0
                    for tt in range(2):
                        for ho in range(16):
                            jh = tt * 16 + ho
                            if jh >= 2:
                                t.wait_ge(sm_as, as0 + jh - 1)
                            for kc in range(8):
                                mmi = t.matmul(pp[2 + jh % 2][:],
                                               w1e[:, kc * 2048 + ho * 128: kc * 2048 + (ho + 1) * 128],
                                               xn[:, kc * TH + tt * 512: kc * TH + (tt + 1) * 512],
                                               start=(kc == 0), stop=(kc == 7))
                            mmi.then_inc(sm_mm, 1)
                            nmm += 1
                        t.wait_ge(sm_v, v0 + tt + 1)
                        for oo in range(8):
                            jw = tt * 8 + oo
                            if jw >= 2:
                                t.wait_ge(sm_ev, ev0 + jw - 1)
                            for kc in range(16):
                                mmi = t.matmul(pp[jw % 2][:],
                                               w2e[:, kc * 1024 + oo * 128: kc * 1024 + (oo + 1) * 128],
                                               hsb[:, kc * 512:(kc + 1) * 512],
                                               start=(kc == 0), stop=(kc == 15),
                                               skip_group_check=True)
                            mmi.then_inc(sm_mm, 1)
                            nmm += 1
                @blk.scalar
                def _(s):
                    for tt in range(2):
                        for ho in range(16):
                            jh = tt * 16 + ho
                            s.wait_ge(sm_mm, mm0 + tt * 24 + ho + 1)
                            s.activation(hsb[:, ho * 512:(ho + 1) * 512], pp[2 + jh % 2][:],
                                         AF.Relu, bias=b1c[:, e * 16 + ho:e * 16 + ho + 1]
                                         ).then_inc(sm_as, 1)
                @blk.vector
                def _(v):
                    for tt in range(2):
                        v.wait_ge(sm_as, as0 + 16 * (tt + 1))
                        v.tensor_mul(hsb[:], hsb[:],
                                     brep(cwb[:, tt * 512:(tt + 1) * 512], 16, 512)
                                     ).then_inc(sm_v, 1)
                        for oo in range(8):
                            jw = tt * 8 + oo
                            v.wait_ge(sm_mm, mm0 + tt * 24 + 16 + oo + 1)
                            dsl = acc[:, oo * TH + tt * 512: oo * TH + (tt + 1) * 512]
                            if e == 0:
                                v.tensor_copy(dsl, pp[jw % 2][:]).then_inc(sm_ev, 1)
                            else:
                                v.tensor_add(dsl, dsl, pp[jw % 2][:]).then_inc(sm_ev, 1)
            c.semc["mm"] = mm0 + 48; c.semc["ev"] = ev0 + 16
            c.semc["as"] = as0 + 32; c.semc["vv"] = v0 + 2

        mm0, ev0 = c.semc["mm"], c.semc["ev"]
        with nc.Block() as blk:
            @blk.tensor
            def _(t):
                for j in range(16):
                    oo, tt = j // 2, j % 2
                    if j >= 2:
                        t.wait_ge(sm_ev, ev0 + j - 1)
                    t.matmul(pp[j % 2][:], b2sb[0:8, oo * 128:(oo + 1) * 128],
                             cwT[:, tt * 512:(tt + 1) * 512],
                             start=True, stop=True).then_inc(sm_mm, 1)
            @blk.vector
            def _(v):
                for j in range(16):
                    oo, tt = j // 2, j % 2
                    v.wait_ge(sm_mm, mm0 + j + 1)
                    dsl = acc[:, oo * TH + tt * 512: oo * TH + (tt + 1) * 512]
                    v.tensor_add(dsl, dsl, pp[j % 2][:]).then_inc(sm_ev, 1)
        c.semc["mm"] = mm0 + 16; c.semc["ev"] = ev0 + 16

        mm0, ev0, dma0, st0 = c.semc["mm"], c.semc["ev"], c.semc["dma"], c.semc["st"]
        with nc.Block() as blk:
            @blk.gpsimd
            def _(g):
                for j in range(16):
                    oo, tt = j // 2, j % 2
                    if j >= 2:
                        g.wait_ge(sm_ev, ev0 + j - 1)
                    g.dma_start(xt[j % 2][:],
                                dx1.ap()[oo * 128:(oo + 1) * 128,
                                         t0 + tt * 512:t0 + (tt + 1) * 512]).then_inc(dma, 16)
            @blk.vector
            def _(v):
                for j in range(16):
                    oo, tt = j // 2, j % 2
                    v.wait_ge(dma, dma0 + 16 * (j + 1))
                    if j >= 2:
                        v.wait_ge(st_sem, st0 + 16 * (j - 1))
                    asl = acc[:, oo * TH + tt * 512: oo * TH + (tt + 1) * 512]
                    v.tensor_add(ost[j % 2][:], asl, xt[j % 2][:])
                    v.tensor_add(ost[j % 2][:], ost[j % 2][:],
                                 xn[:, oo * TH + tt * 512: oo * TH + (tt + 1) * 512]
                                 ).then_inc(sm_ev, 1)
            @blk.sync
            def _(sy):
                for j in range(16):
                    oo, tt = j // 2, j % 2
                    sy.wait_ge(sm_ev, ev0 + j + 1)
                    sy.dma_start(dout_x2.ap()[oo * 128:(oo + 1) * 128,
                                              t0 + tt * 512:t0 + (tt + 1) * 512],
                                 ost[j % 2][:]).then_inc(st_sem, 16)
                sy.wait_ge(st_sem, st0 + 16 * 16)
        c.semc["ev"] = ev0 + 16; c.semc["dma"] = dma0 + 16 * 16
        c.semc["st"] = st0 + 16 * 16
    return dict(xn=xn, acc=acc, cw_all=cw_all, cwT=cwT, hsb=hsb,
                mu=S["mu"], rstd=S["rstd"], negcs=negcs, cwb=cwb,
                ga=ga, gb=gb, gsc=gsc, gsm=gsm, gss=gss)


def dumpblk(c, pairs):
    loadblk(c, pairs)



def emit_l1attn(c, dx2, dxh63, dW, dout_x3):
    nc = c.nc
    TF = 2176
    ones_bf = c.sb("a_ones", [128, 128], BF16)
    onesrow_bf = c.sb("a_onesr", [1, 128], BF16)
    bq = c.sb("a_bq", [128, 8]); bk = c.sb("a_bk", [128, 8])
    bout = c.sb("a_bout", [128, 8])
    bv = c.sb("a_bv", [1, 1024], BF16)
    maskR = c.sb("a_maskR", [128, 256]); mask0 = c.sb("a_mask0", [128, 256])
    idb = c.sb("a_idb", [128, 128], BF16)
    wv = c.sb("a_wv", [128, 8 * 1024], BF16)
    wout = c.sb("a_wout", [128, 8 * 1024], BF16)
    loadblk(c, [(ones_bf[:], dW["ones_bf"].ap()), (onesrow_bf[:], dW["onesrow_bf"].ap()),
                (bq[:], dW["l1a_bq"].ap()), (bk[:], dW["l1a_bk"].ap()),
                (bout[:], dW["l1a_bout"].ap()),
                (bv[:], dW["l1a_bv"].ap()),
                (maskR[:], dW["l1a_maskR"].ap()), (mask0[:], dW["l1a_mask0"].ap()),
                (idb[:], dW["ident_bf"].ap()),
                (wv[:].rearrange("p (kc o) -> p kc o", kc=8),
                 dW["l1a_wvT"].ap().rearrange("(kc p) o -> p kc o", p=128)),
                (wout[:].rearrange("p (kc o) -> p kc o", kc=8),
                 dW["l1a_woutT"].ap().rearrange("(kc p) o -> p kc o", p=128))])

    xn2 = c.sb("a_xn", [128, 8 * TF], BF16)
    q_sb = c.sb("a_q", [128, 8 * 2048], BF16)
    k_sb = c.sb("a_k", [128, 8 * TF], BF16)
    v_sb = c.sb("a_v", [128, 17 * 1024], BF16)
    o_sb = c.sb("a_o", [128, 8 * 128], BF16)
    x2t = c.sb("a_x2t", [128, 8 * 128])
    ssb_all = c.sb("a_ssba", [128, 16 * 256])

    ost = [c.sb(f"a_ost{i}", [128, 128]) for i in range(2)]
    wbq = [c.sb(f"a_wbq{i}", [128, 8 * 128], BF16) for i in range(2)]
    P = get_pool(c)
    S = dict(sq=q_sb,
             ps_a=P["psa"], ps_b=P["psb"],
             mu=ssb_all[:, 0:1024], va=ssb_all[:, 1024:2048],
             rstd=ssb_all[:, 2048:3072], st=None)
    S["st"] = x2t
    pbf = P["pbf"]
    pp = P["pp"]
    sm_mm, sm_ev, sm_as, sm_wl = (c.sem("mm"), c.sem("ev"), c.sem("as"), c.sem("wl"))
    sm_v = c.sem("vv"); smf = c.sem("fv")
    dma = c.sem("dma"); st_sem = c.sem("st")

    with nc.Block() as blk:
        @blk.gpsimd
        def _(g):
            g.memset(xn2[:].rearrange("p (kc t) -> p kc t", kc=8)[:, :, 0:65], 0.0)
    loadblk(c, [(xn2[:].rearrange("p (kc t) -> p kc t", kc=8)[:, :, 65:128],
                 dxh63.rearrange("(kc p) t -> p kc t", p=128)),
                (xn2[:].rearrange("p (kc t) -> p kc t", kc=8)[:, :, 128:TF],
                 fview(dx2, 0, 2048))])
    for (off, Tt) in ((0, 1024), (1024, 1024), (2048, 128)):
        emit_ln(c, lambda kc, off=off, Tt=Tt: xn2[:, kc * TF + off: kc * TF + off + Tt],
                ones_bf, S, Tt)

    for (wname, dst, bias, Tt, toff) in (("l1a_wqT", q_sb, bq, 2048, 128),
                                         ("l1a_wkT", k_sb, bk, TF, 0)):
        mm0, ev0, wl0 = c.semc["mm"], c.semc["ev"], c.semc["wl"]
        ntt = (Tt + 511) // 512
        with nc.Block() as blk:
            @blk.gpsimd
            def _(g):
                for o in range(8):
                    if o >= 2:
                        g.wait_ge(sm_mm, mm0 + ntt * (o - 1))
                    g.dma_start(wbq[o % 2][:], wslice(dW[wname], o * 128, 128)).then_inc(sm_wl, 16)
            @blk.tensor
            def _(t):
                for o in range(8):
                    t.wait_ge(sm_wl, wl0 + 16 * (o + 1))
                    for tt in range(ntt):
                        w = min(512, Tt - tt * 512)
                        j = o * ntt + tt
                        if j >= 2:
                            t.wait_ge(sm_ev, ev0 + j - 1)
                        for kc in range(8):
                            mmi = t.matmul(pp[j % 2][:, 0:w],
                                           wbq[o % 2][:, kc * 128:(kc + 1) * 128],
                                           xn2[:, kc * TF + toff + tt * 512: kc * TF + toff + tt * 512 + w],
                                           start=(kc == 0), stop=(kc == 7))
                        mmi.then_inc(sm_mm, 1)
            @blk.vector
            def _(v):
                for o in range(8):
                    for tt in range(ntt):
                        w = min(512, Tt - tt * 512)
                        j = o * ntt + tt
                        v.wait_ge(sm_mm, mm0 + j + 1)
                        v.tensor_scalar(dst[:, o * Tt + tt * 512: o * Tt + tt * 512 + w],
                                        pp[j % 2][:, 0:w], bias[:, o:o + 1], None,
                                        ALU.add).then_inc(sm_ev, 1)
        c.semc["mm"] = mm0 + 8 * ntt; c.semc["ev"] = ev0 + 8 * ntt
        c.semc["wl"] = wl0 + 16 * 8

    mm0, ev0 = c.semc["mm"], c.semc["ev"]
    with nc.Block() as blk:
        @blk.tensor
        def _(t):
            for ti in range(17):
                for oh in range(2):
                    j = ti * 2 + oh
                    if j >= 2:
                        t.wait_ge(sm_ev, ev0 + j - 1)
                    for kc in range(8):
                        t.matmul(pp[j % 2][:],
                                 xn2[:, kc * TF + ti * 128:kc * TF + (ti + 1) * 128],
                                 wv[:, kc * 1024 + oh * 512: kc * 1024 + (oh + 1) * 512],
                                 start=(kc == 0), stop=False)
                    mmi = t.matmul(pp[j % 2][:], onesrow_bf[:],
                                   bv[:, oh * 512:(oh + 1) * 512], start=False, stop=True)
                    mmi.then_inc(sm_mm, 1)
        @blk.vector
        def _(v):
            for ti in range(17):
                for oh in range(2):
                    j = ti * 2 + oh
                    v.wait_ge(sm_mm, mm0 + j + 1)
                    v.tensor_copy(v_sb[:, ti * 1024 + oh * 512: ti * 1024 + (oh + 1) * 512],
                                  pp[j % 2][:]).then_inc(sm_ev, 1)
    c.semc["mm"] = mm0 + 34; c.semc["ev"] = ev0 + 34

    SCALE = 0.125
    P_all = c.sb("a_Pall", [128, 16 * 256], BF16)
    PT_all = ssb_all[:].bitcast(BF16)[:, 0:4096]
    mc = c.sb("a_mc", [128, 48])

    def s_slot(h):
        ps = S["ps_a"] if h % 4 < 2 else S["ps_b"]
        return ps[:, (h % 2) * 512 + ((h // 4) % 2) * 256:
                  (h % 2) * 512 + ((h // 4) % 2) * 256 + 256]

    def t_slot(j):
        g = j // 4
        return pbf[:, (g % 2) * 1024 + (j % 4) * 128:(g % 2) * 1024 + (j % 4) * 128 + 128]

    def o_slot(h):
        ps = S["ps_a"] if h % 4 < 2 else S["ps_b"]
        return ps[0:64, (h % 2) * 512 + (h // 4) * 128:(h % 2) * 512 + (h // 4) * 128 + 128]

    for qb in range(16):
        mm0, ev0, as0, v0, dma0, st0 = (c.semc[s] for s in
                                        ("mm", "ev", "as", "vv", "dma", "st"))
        mask = mask0 if qb == 0 else maskR
        with nc.Block() as blk:
            @blk.gpsimd
            def _(g):
                g.dma_start(x2t[:].rearrange("p (kc t) -> p kc t", kc=8),
                            dx2.ap()[:, qb * 128:(qb + 1) * 128]
                            .rearrange("(kc p) t -> p kc t", p=128)).then_inc(dma, 16)
            @blk.tensor
            def _(t):
                for h in range(16):
                    kc, pb = h // 2, (h % 2) * 64
                    if h >= 4:
                        t.wait_ge(sm_ev, ev0 + h - 3)
                    t.matmul(s_slot(h),
                             q_sb[pb:pb + 64, kc * 2048 + qb * 128: kc * 2048 + (qb + 1) * 128],
                             k_sb[pb:pb + 64, kc * TF + qb * 128: kc * TF + qb * 128 + 256],
                             start=True, stop=True).then_inc(sm_mm, 1)
                t.wait_ge(sm_v, v0 + 2)
                for j in range(32):
                    h, ki = j // 2, j % 2
                    if j % 4 == 0 and j >= 8:
                        t.wait_ge(sm_ev, ev0 + 16 + 4 * (j // 4 - 1))
                    ins = t.transpose(t_slot(j),
                                      P_all[:, h * 256 + ki * 128: h * 256 + ki * 128 + 128],
                                      idb[:])
                    if j % 4 == 3:
                        ins.then_inc(sm_mm, 1)
                for h in range(16):
                    if h == 0:
                        t.wait_ge(sm_ev, ev0 + 48)
                    if h >= 4:
                        t.wait_ge(sm_ev, ev0 + 48 + h - 3)
                    for ki in range(2):
                        mmi = t.matmul(o_slot(h),
                                       v_sb[:, (qb + ki) * 1024 + h * 64: (qb + ki) * 1024 + (h + 1) * 64],
                                       PT_all[:, h * 256 + ki * 128: h * 256 + ki * 128 + 128],
                                       start=(ki == 0), stop=(ki == 1))
                    mmi.then_inc(sm_mm, 1)
                for o in range(8):
                    t.wait_ge(sm_ev, ev0 + 64 + (o - 1 if o >= 2 else 0))
                    for kc in range(8):
                        mmi = t.matmul(pp[o % 2][:, 0:128],
                                       wout[:, kc * 1024 + o * 128: kc * 1024 + (o + 1) * 128],
                                       o_sb[:, kc * 128:(kc + 1) * 128],
                                       start=(kc == 0), stop=(kc == 7))
                    mmi.then_inc(sm_mm, 1)
            @blk.vector
            def _(v):
                for h in range(16):
                    v.wait_ge(sm_mm, mm0 + h + 1)
                    v.scalar_tensor_tensor(ssb_all[:, h * 256:(h + 1) * 256], s_slot(h),
                                           SCALE, mask[:], ALU.mult,
                                           ALU.add).then_inc(sm_ev, 1)
                for h in range(16):
                    v.reduce_max(mc[:, h:h + 1], ssb_all[:, h * 256:(h + 1) * 256], axis=AX)
                for h in range(16):
                    ins = v.tensor_single_scalar(mc[:, 16 + h:17 + h], mc[:, h:h + 1],
                                                 -1.0, ALU.mult)
                ins.then_inc(sm_v, 1)
                v.wait_ge(sm_as, as0 + 1)
                for h in range(16):
                    v.reduce_sum(mc[:, 32 + h:33 + h], P_all[:, h * 256:(h + 1) * 256], axis=AX)
                for h in range(16):
                    v.reciprocal(mc[:, h:h + 1], mc[:, 32 + h:33 + h])
                for h in range(16):
                    ins = v.tensor_tensor(P_all[:, h * 256:(h + 1) * 256],
                                          P_all[:, h * 256:(h + 1) * 256],
                                          bcol(mc[:, h:h + 1], 256), ALU.mult)
                ins.then_inc(sm_v, 1)
                for j in range(32):
                    h, ki = j // 2, j % 2
                    v.wait_ge(sm_mm, mm0 + 16 + j // 4 + 1)
                    v.tensor_copy(PT_all[:, h * 256 + ki * 128: h * 256 + ki * 128 + 128],
                                  t_slot(j)).then_inc(sm_ev, 1)
                for h in range(16):
                    v.wait_ge(sm_mm, mm0 + 24 + h + 1)
                    v.tensor_copy(o_sb[(h % 2) * 64:(h % 2) * 64 + 64,
                                       (h // 2) * 128:(h // 2) * 128 + 128],
                                  o_slot(h)).then_inc(sm_ev, 1)
                for o in range(8):
                    v.wait_ge(sm_mm, mm0 + 40 + o + 1)
                    v.wait_ge(dma, dma0 + 16)
                    if o >= 2:
                        v.wait_ge(st_sem, st0 + 16 * (o - 1))
                    v.tensor_scalar(ost[o % 2][:], pp[o % 2][:, 0:128], bout[:, o:o + 1],
                                    None, ALU.add)
                    v.tensor_add(ost[o % 2][:], ost[o % 2][:],
                                 x2t[:, o * 128:(o + 1) * 128]).then_inc(sm_ev, 1)
            @blk.scalar
            def _(s):
                s.wait_ge(sm_v, v0 + 1)
                for h in range(16):
                    ins = s.activation(P_all[:, h * 256:(h + 1) * 256],
                                       ssb_all[:, h * 256:(h + 1) * 256], AF.Exp,
                                       bias=mc[:, 16 + h:17 + h])
                ins.then_inc(sm_as, 1)
            @blk.sync
            def _(sy):
                for o in range(8):
                    sy.wait_ge(sm_ev, ev0 + 64 + o + 1)
                    sy.dma_start(dout_x3.ap()[o * 128:(o + 1) * 128, qb * 128:(qb + 1) * 128],
                                 ost[o % 2][:]).then_inc(st_sem, 16)
                sy.wait_ge(st_sem, st0 + 16 * 8)
        c.semc["mm"] = mm0 + 48
        c.semc["ev"] = ev0 + 72
        c.semc["as"] = as0 + 1
        c.semc["vv"] = v0 + 2
        c.semc["dma"] = dma0 + 16
        c.semc["st"] = st0 + 16 * 8


def restride(ap2d, off, step, count):
    return bass.AP(ap2d.tensor, ap2d.offset + off, [list(ap2d.ap[0]), [step, count]])


def emit_l0attn(c, dxin, dS0, dW, dout_x1, dstate, dbg=False, dbg_outs=None,
                front=False, dmask=None, onepass=False, cdl2=None):
    nc = c.nc
    c._spill_wait = None
    TH = 1024
    ones_bf = c.sb("b_ones", [128, 128], BF16)
    onesrow_f = c.sb("b_onesrf", [1, 128])
    idf = c.sb("b_idf", [128, 128])
    idb = c.sb("b_idb", [128, 128], BF16)
    triu = c.sb("b_triu", [128, 128])
    convw = c.sb("b_convw", [128, 72])
    convb = c.sb("b_convb", [128, 18])
    dtb = c.sb("b_dtb", [32, 1])
    Acol = c.sb("b_acol", [32, 1])
    Drep = c.sb("b_drep", [128, 32])
    normw = c.sb("b_normw", [128, 16])
    ones32 = c.sb("b_ones32", [32, 128])
    woutr = c.sb("b_woutr", [128, 16 * 1024], BF16)
    carry = c.sb("b_carry", [128, 2048])
    prevb = c.sb("b_prevb", [128, 2048], BF16)
    da32 = c.sb("b_da32", [32, 128])
    cdl = c.sb("b_cdl", [32, 8])
    cdflat = c.sb("b_cdflat", [1, 256])
    Lm = c.sb("b_Lm", [33, 128])
    TD = c.sb("b_TD", [128, 64])
    ETexp = c.sb("b_ETexp", [128, 32])
    ddT = c.sb("b_ddT", [128, 32])
    GmT = c.sb("b_GmT", [128, 128], BF16)
    BT = c.sb("b_BT", [128, 128], BF16)
    rms1 = c.sb("b_rms1", [128, 4])
    ost = [c.sb(f"b_ost{i}", [128, 512]) for i in range(2)]
    xres = c.sb("b_xres", [128, 8 * 128])
    zst = [c.sb(f"b_zst{i}", [128, 512], BF16) for i in range(2)]
    winb = [c.sb(f"b_winb{i}", [128, 8 * 128], BF16) for i in range(2)]
    P = get_pool(c)
    psa = P["psa"]; psb = P["psb"]; pp = P["pp"]; pbf = P["pbf"]
    dz = c.dtmp("b_dz", [2048, 2048], BF16)
    rets = {"dz": dz}
    if onepass:
        rets["dysp"] = c.dtmp("b_dysp", [16, 128, 2048], BF16)
        rets["dcsp"] = c.dtmp("b_dcsp", [16, 128, 128], BF16)
        rets["desp"] = c.dtmp("b_desp", [16, 128, 32])
        CDL2 = cdl2 if cdl2 is not None else c.sb("b_CDL2", [32, 16])
        rets["CDL2"] = CDL2
    sm_mm, sm_ev, sm_as, sm_wl = (c.sem("mm"), c.sem("ev"), c.sem("as"), c.sem("wl"))
    sm_v = c.sem("vv"); smf = c.sem("fv")
    dma = c.sem("dma"); st_sem = c.sem("st")

    loadblk(c, [(ones_bf[:], dW["ones_bf"].ap()), (onesrow_f[:], dW["onesrow_f"].ap()),
                (idf[:], dW["ident_f32"].ap()), (idb[:], dW["ident_bf"].ap()),
                (triu[:], dW["mb_triu"].ap()), (convw[:], dW["mb_convw"].ap()),
                (convb[:], dW["mb_convb"].ap()), (dtb[:], dW["mb_dtb"].ap()),
                (Acol[:], dW["mb_alog"].ap()), (Drep[:], dW["mb_drep"].ap()),
                (normw[:], dW["mb_normw"].ap()), (ones32[:], dW["ones32"].ap()),
                (woutr[:].rearrange("p (kc o) -> p kc o", kc=16),
                 dW["mb_woutT"].ap().rearrange("(kc p) o -> p kc o", p=128))]
            + ([(carry[:], dS0)] if dS0 is not None else []))
    if dS0 is None:
        with nc.Block() as blk:
            @blk.gpsimd
            def _(g):
                g.memset(carry[:], 0.0)
    if dmask is not None:
        mcol = c.sb("b_mcol", [128, 1])
        loadblk(c, [(mcol[:], dmask.ap())])
        with nc.Block() as blk:
            @blk.vector
            def _(v):
                v.tensor_mul(carry[:], carry[:], bcol(mcol[:], 2048))
    with nc.Block() as blk:
        @blk.scalar
        def _(s):
            s.activation(Acol[:], Acol[:], AF.Exp)
        @blk.gpsimd
        def _(g):
            g.memset(Lm[32:33, 0:128], 1.0)
    with nc.Block() as blk:
        @blk.vector
        def _(v):
            v.tensor_single_scalar(Acol[:], Acol[:], -1.0, ALU.mult)

    for half in range(1 if dbg else 2):
        t0g = half * TH
        c.push_scope()
        x4b = c.sb(f"h_x4b{half}", [128, 16 * 1024], BF16)
        BCh = c.sb(f"h_BC{half}", [128, 2 * 1024], BF16)
        CD2 = c.sb(f"h_CD2{half}", [64, 1024])
        CDB = c.sb(f"h_CDB{half}", [128, 256])

        c.push_scope()
        xnh = c.sb(f"p1_xn{half}", [128, 8 * 1027], BF16)
        xbc = c.sb(f"p1_xbc{half}", [128, 18 * 1027], BF16)
        wzb = [c.sb(f"p1_wzb{half}_{i}", [128, 8 * 512], BF16) for i in range(2)]
        S = dict(sq=xbc, ps_a=psa, ps_b=psb,
                 mu=c.sb(f"p1_mu{half}", [128, 512]), va=c.sb(f"p1_va{half}", [128, 512]),
                 rstd=c.sb(f"p1_rstd{half}", [128, 512]), st=ost[0])
        loadblk(c, [(xnh[:].rearrange("p (kc t) -> p kc t", kc=8),
                     fview(dxin, t0g, 1027))])
        for (off, Tt) in ((0, 512), (512, 512), (1024, 3)):
            emit_ln(c, lambda kc, off=off, Tt=Tt: xnh[:, kc * 1027 + off: kc * 1027 + off + Tt],
                    ones_bf, S, Tt)

        tjobs = ((0, 512), (512, 512), (1024, 3))
        mm0, ev0, wl0 = c.semc["mm"], c.semc["ev"], c.semc["wl"]
        with nc.Block() as blk:
            @blk.gpsimd
            def _(g):
                for o in range(19):
                    if o >= 2:
                        g.wait_ge(sm_mm, mm0 + 3 * (o - 1))
                    ow = 128 if o < 18 else 32
                    g.dma_start(winb[o % 2][:, 0:8 * ow],
                                wslice(dW["mb_winT"], 2048 + o * 128, ow)).then_inc(sm_wl, 16)
            @blk.tensor
            def _(t):
                for o in range(19):
                    ow = 128 if o < 18 else 32
                    t.wait_ge(sm_wl, wl0 + 16 * (o + 1))
                    for tj, (tt0, w) in enumerate(tjobs):
                        j = o * 3 + tj
                        if j >= 2:
                            t.wait_ge(sm_ev, ev0 + j - 1)
                        for kc in range(8):
                            mmi = t.matmul(pp[j % 2][0:ow, 0:w],
                                           winb[o % 2][:, kc * ow:(kc + 1) * ow],
                                           xnh[:, kc * 1027 + tt0: kc * 1027 + tt0 + w],
                                           start=(kc == 0), stop=(kc == 7))
                        mmi.then_inc(sm_mm, 1)
            @blk.vector
            def _(v):
                for o in range(19):
                    for tj, (tt0, w) in enumerate(tjobs):
                        j = o * 3 + tj
                        v.wait_ge(sm_mm, mm0 + j + 1)
                        if o < 18:
                            ins = v.tensor_copy(xbc[:, o * 1027 + tt0: o * 1027 + tt0 + w],
                                                pp[j % 2][:, 0:w])
                        else:
                            lo = max(tt0, 3); hi = tt0 + w
                            ins = v.tensor_copy(CD2[0:32, lo - 3:hi - 3],
                                                pp[j % 2][0:32, lo - tt0:hi - tt0])
                        ins.then_inc(sm_ev, 1)
        c.semc["mm"] = mm0 + 57; c.semc["ev"] = ev0 + 57; c.semc["wl"] = wl0 + 19 * 16

        if not front:
            mm0, ev0, wl0, st0 = (c.semc[s] for s in ("mm", "ev", "wl", "st"))
            with nc.Block() as blk:
                @blk.gpsimd
                def _(g):
                    for oz in range(4):
                        if oz >= 2:
                            g.wait_ge(sm_mm, mm0 + 8 * (oz - 1))
                        g.dma_start(wzb[oz % 2][:].rearrange("p (kc o) -> p kc o", kc=8),
                                    wslice(dW["mb_winT"], oz * 512, 512)).then_inc(sm_wl, 16)
                @blk.tensor
                def _(t):
                    for oz in range(4):
                        t.wait_ge(sm_wl, wl0 + 16 * (oz + 1))
                        for ti in range(8):
                            j = oz * 8 + ti
                            if j >= 2:
                                t.wait_ge(sm_ev, ev0 + j - 1)
                            for kc in range(8):
                                mmi = t.matmul(pp[j % 2][:],
                                               xnh[:, kc * 1027 + 3 + ti * 128: kc * 1027 + 3 + (ti + 1) * 128],
                                               wzb[oz % 2][:, kc * 512:(kc + 1) * 512],
                                               start=(kc == 0), stop=(kc == 7))
                            mmi.then_inc(sm_mm, 1)
                @blk.vector
                def _(v):
                    for oz in range(4):
                        for ti in range(8):
                            j = oz * 8 + ti
                            v.wait_ge(sm_mm, mm0 + j + 1)
                            if j >= 2:
                                v.wait_ge(st_sem, st0 + 16 * (j - 1))
                            v.tensor_copy(zst[j % 2][:], pp[j % 2][:]).then_inc(sm_ev, 1)
                @blk.sync
                def _(sy):
                    for oz in range(4):
                        for ti in range(8):
                            j = oz * 8 + ti
                            sy.wait_ge(sm_ev, ev0 + j + 1)
                            sy.dma_start(dz.ap()[t0g + ti * 128:t0g + (ti + 1) * 128,
                                                 oz * 512:(oz + 1) * 512],
                                         zst[j % 2][:]).then_inc(st_sem, 16)
                    sy.wait_ge(st_sem, st0 + 16 * 32)
            c.semc["mm"] = mm0 + 32; c.semc["ev"] = ev0 + 32
            c.semc["wl"] = wl0 + 16 * 4; c.semc["st"] = st0 + 16 * 32

        v0, as0 = c.semc["vv"], c.semc["as"]
        with nc.Block() as blk:
            @blk.vector
            def _(v):
                for ct in range(18):
                    ps = psa if ct % 2 == 0 else psb
                    if ct >= 2:
                        v.wait_ge(sm_as, as0 + ct - 1)
                    v.tensor_scalar_mul(ps[:], xbc[:, ct * 1027 + 0: ct * 1027 + 1024],
                                        convw[:, ct * 4:ct * 4 + 1])
                    for k in range(1, 4):
                        ins = v.scalar_tensor_tensor(ps[:], xbc[:, ct * 1027 + k: ct * 1027 + k + 1024],
                                                     convw[:, ct * 4 + k:ct * 4 + k + 1],
                                                     ps[:], ALU.mult, ALU.add)
                    ins.then_inc(sm_v, 1)
            @blk.scalar
            def _(s):
                for ct in range(18):
                    ps = psa if ct % 2 == 0 else psb
                    s.wait_ge(sm_v, v0 + ct + 1)
                    if ct == 16 and onepass and getattr(c, "_spill_wait", None):
                        s.wait_ge(dma, c._spill_wait)
                    if ct < 16:
                        dst = x4b[:, ct * 1024:(ct + 1) * 1024]
                    else:
                        dst = BCh[:, (ct - 16) * 1024:(ct - 15) * 1024]
                    s.activation(dst, ps[:], AF.Silu,
                                 bias=convb[:, ct:ct + 1]).then_inc(sm_as, 1)
        c.semc["vv"] = v0 + 18; c.semc["as"] = as0 + 18
        if dbg_outs:
            loadblk(c, [(dbg_outs["xnh"].ap(), xnh[:]), (dbg_outs["xbc"].ap(), xbc[:])])
        c.pop_scope()

        c.push_scope()
        E = c.sb(f"p2_E{half}", [128, 4096], BF16)
        Mp = c.sb(f"p2_Mp{half}", [128, 4096], BF16)
        xT = c.sb(f"p2_xT{half}", [128, 2048], BF16)
        xtf = c.sb(f"p2_xtf{half}", [128, 2048], BF16)
        Rmat = c.sb(f"p2_Rmat{half}", [33, 4096])
        IDall = c.sb(f"p2_IDall{half}", [128, 4096], BF16)
        ytmp = c.sb(f"p2_ytmp{half}", [128, 1024])
        ysb = c.sb(f"p2_ysb{half}", [128, 2048])
        sz = c.sb(f"p2_sz{half}", [128, 2048])
        yn = c.sb(f"p2_yn{half}", [128, 2048], BF16)
        ynT = c.sb(f"p2_ynT{half}", [128, 2048], BF16)
        zch = c.sb(f"p2_zch{half}", [128, 2048], BF16)
        loadblk(c, [(Rmat[0:32, :], dW["mb_oh32"].ap())])
        if not front:
            with nc.Block() as blk:
                @blk.vector
                def _(v):
                    for h in range(32):
                        v.tensor_tensor(IDall[:, h * 128:(h + 1) * 128], idb[:],
                                        bcol(Drep[:, h:h + 1], 128), ALU.mult)
        with nc.Block() as blk:
            @blk.vector
            def _(v):
                v.tensor_scalar(CD2[0:32, :], CD2[0:32, :], dtb[:], None, ALU.add)
        with nc.Block() as blk:
            @blk.scalar
            def _(s):
                s.activation(ytmp[0:32, 0:1024], CD2[0:32, :], AF.Abs)
                s.activation(CD2[32:64, :], ytmp[0:32, 0:1024], AF.Exp, scale=f32(-1.0))
                s.activation(ytmp[0:32, 0:1024], CD2[32:64, :], AF.Ln, bias=f32(1.0))
        with nc.Block() as blk:
            @blk.vector
            def _(v):
                v.tensor_scalar(ysb[0:32, 0:1024], CD2[0:32, :], 0.0, None, ALU.max)
                v.tensor_add(CD2[0:32, :], ysb[0:32, 0:1024], ytmp[0:32, 0:1024])
        with nc.Block() as blk:
            @blk.vector
            def _(v):
                for ci in range(8):
                    v.tensor_scalar_mul(da32[:], CD2[0:32, ci * 128:(ci + 1) * 128], Acol[:])
                    v.tensor_tensor_scan(CD2[32:64, ci * 128:(ci + 1) * 128],
                                         da32[:], ones32[:], 0.0, ALU.add, ALU.mult)
        with nc.Block() as blk:
            @blk.scalar
            def _(s):
                s.activation(cdl[:], restride(CD2[32:64, 0:1024], 127, 128, 8), AF.Exp)
            if onepass:
                @blk.vector
                def _(v):
                    v.tensor_copy(CDL2[:, half * 8:(half + 1) * 8],
                                  restride(CD2[32:64, 0:1024], 127, 128, 8))
        dma0 = c.semc["dma"]
        with nc.Block() as blk:
            @blk.gpsimd
            def _(g):
                g.dma_start(cdflat[:].rearrange("p (a b) -> p a b", a=32), cdl[:]).then_inc(dma, 16)
                g.wait_ge(dma, dma0 + 16)
        c.semc["dma"] = dma0 + 16
        mm0 = c.semc["mm"]
        with nc.Block() as blk:
            @blk.tensor
            def _(t):
                t.matmul(psa[:, 0:256], onesrow_f[:], cdflat[:],
                         start=True, stop=True).then_inc(sm_mm, 1)
            @blk.vector
            def _(v):
                v.wait_ge(sm_mm, mm0 + 1)
                v.tensor_copy(CDB[:], psa[:, 0:256])
        c.semc["mm"] = mm0 + 1

        for ci in range(8):
            g0 = t0g + ci * 128
            csl = slice(ci * 128, (ci + 1) * 128)
            mm0, v0 = c.semc["mm"], c.semc["vv"]
            with nc.Block() as blk:
                @blk.tensor
                def _(t):
                    t.transpose(pp[0][:, 0:64], CD2[:, csl], idf[0:64, 0:64]).then_inc(sm_mm, 1)
                    for b in range(17):
                        if b >= 2:
                            t.wait_ge(sm_v, v0 + b - 1)
                        src = (x4b[:, b * 1024 + csl.start:b * 1024 + csl.stop] if b < 16
                               else BCh[:, csl])
                        t.transpose(pbf[:, (b % 2) * 1024:(b % 2) * 1024 + 128],
                                    src, idb[:]).then_inc(sm_mm, 1)
                @blk.vector
                def _(v):
                    v.wait_ge(sm_mm, mm0 + 1)
                    v.tensor_copy(TD[:], pp[0][:, 0:64])
                    for b in range(17):
                        v.wait_ge(sm_mm, mm0 + b + 2)
                        dst = xT[:, b * 128:(b + 1) * 128] if b < 16 else BT[:]
                        v.tensor_copy(dst, pbf[:, (b % 2) * 1024:(b % 2) * 1024 + 128]
                                      ).then_inc(sm_v, 1)
            c.semc["mm"] = mm0 + 18; c.semc["vv"] = v0 + 17

            dma0 = c.semc["dma"]
            with nc.Block() as blk:
                @blk.vector
                def _(v):
                    v.tensor_single_scalar(Lm[0:32, :], CD2[32:64, csl], -1.0, ALU.mult)
                @blk.gpsimd
                def _(g):
                    g.dma_start(Rmat[32:33, :].rearrange("p (a b) -> p a b", a=32),
                                CD2[32:64, csl]).then_inc(dma, 16)
                    g.wait_ge(dma, dma0 + 16)
                @blk.scalar
                def _(s):
                    if onepass and getattr(c, "_spill_wait", None):
                        s.wait_ge(dma, c._spill_wait)
                    s.activation(ETexp[:], TD[:, 32:64], AF.Exp)
            c.semc["dma"] = dma0 + 16

            mm0, v0, as0 = c.semc["mm"], c.semc["vv"], c.semc["as"]
            with nc.Block() as blk:
                @blk.tensor
                def _(t):
                    for j in range(8):
                        if j >= 4:
                            t.wait_ge(sm_v, v0 + j - 3)
                        ds = (psa if j % 4 < 2 else psb)[:, (j % 2) * 512:(j % 2) * 512 + 512]
                        t.matmul(ds, Lm[:], Rmat[:, j * 512:(j + 1) * 512],
                                 start=True, stop=True).then_inc(sm_mm, 1)
                @blk.vector
                def _(v):
                    for j in range(8):
                        v.wait_ge(sm_mm, mm0 + j + 1)
                        if j >= 2:
                            v.wait_ge(sm_as, as0 + j - 1)
                        ds = (psa if j % 4 < 2 else psb)[:, (j % 2) * 512:(j % 2) * 512 + 512]
                        v.tensor_scalar(ytmp[:, (j % 2) * 512:(j % 2) * 512 + 512], ds,
                                        0.0, None, ALU.min).then_inc(sm_v, 1)
                @blk.scalar
                def _(s):
                    for j in range(8):
                        s.wait_ge(sm_v, v0 + j + 1)
                        s.activation(E[:, j * 512:(j + 1) * 512],
                                     ytmp[:, (j % 2) * 512:(j % 2) * 512 + 512],
                                     AF.Exp).then_inc(sm_as, 1)
            c.semc["mm"] = mm0 + 8; c.semc["vv"] = v0 + 8; c.semc["as"] = as0 + 8

            mm0, v0, f0 = c.semc["mm"], c.semc["vv"], c.semc["fv"]
            with nc.Block() as blk:
                if not front:
                    @blk.tensor
                    def _(t):
                        t.matmul(pp[0][:, 0:128], BCh[:, csl], BCh[:, 1024 + csl.start:1024 + csl.stop],
                                 start=True, stop=True).then_inc(sm_mm, 1)
                @blk.vector
                def _(v):
                    if not front:
                        v.wait_ge(sm_mm, mm0 + 1)
                        v.tensor_tensor(GmT[:], pp[0][:, 0:128], triu[:], ALU.mult)
                        v.tensor_tensor(Mp[:], E[:],
                                        bass.AP(GmT[:].tensor, GmT[:].offset,
                                                [list(GmT[:].ap[0]), [0, 32], [1, 128]]),
                                        ALU.mult)
                        v.tensor_tensor(Mp[:], Mp[:],
                                        bass.AP(TD[:].tensor, TD[:].offset,
                                                [list(TD[:].ap[0]), [1, 32], [0, 128]]),
                                        ALU.mult)
                        v.tensor_tensor(Mp[:], Mp[:], IDall[:], ALU.add)
                    v.tensor_tensor(ddT[:], restride(E[:], 127, 128, 32), TD[:, 0:32],
                                    ALU.mult).then_inc(smf, 1)
                    v.wait_ge(smf, f0 + 1)
                    v.tensor_tensor(xtf[:], xT[:],
                                    bass.AP(ddT[:].tensor, ddT[:].offset,
                                            [list(ddT[:].ap[0]), [1, 32], [0, 64]]),
                                    ALU.mult).then_inc(sm_v, 1)
            c.semc["mm"] = mm0 + (0 if front else 1)
            c.semc["vv"] = v0 + 1; c.semc["fv"] = f0 + 1

            mm0, v0 = c.semc["mm"], c.semc["vv"]
            with nc.Block() as blk:
                @blk.tensor
                def _(t):
                    for j in range(4):
                        ds = (psa if j < 2 else psb)[:, (j % 2) * 512:(j % 2) * 512 + 512]
                        t.matmul(ds, BT[:], xtf[:, j * 512:(j + 1) * 512],
                                 start=True, stop=True).then_inc(sm_mm, 1)
                @blk.vector
                def _(v):
                    v.wait_ge(sm_mm, mm0 + 4)
                    v.tensor_copy(prevb[:], carry[:])
                    v.tensor_tensor(sz[:], carry[:],
                                    bass.AP(CDB[:].tensor, CDB[:].offset + ci,
                                            [list(CDB[:].ap[0]), [8, 32], [0, 64]]),
                                    ALU.mult)
                    v.tensor_add(carry[:, 0:1024], sz[:, 0:1024], psa[:])
                    v.tensor_add(carry[:, 1024:2048], sz[:, 1024:2048], psb[:])
            c.semc["mm"] = mm0 + 4

            if not front:
                mm0, v0 = c.semc["mm"], c.semc["vv"]
                with nc.Block() as blk:
                    @blk.tensor
                    def _(t):
                        for h in range(32):
                            ds = (psa if h < 16 else psb)[:, (h % 16) * 64:(h % 16) * 64 + 64]
                            mmi = t.matmul(ds, Mp[:, h * 128:(h + 1) * 128],
                                           xT[:, h * 64:(h + 1) * 64],
                                           start=True, stop=True)
                        mmi.then_inc(sm_mm, 1)
                        for j in range(2):
                            mmi = t.matmul(pp[j][:], BCh[:, 1024 + csl.start:1024 + csl.stop],
                                           prevb[:, j * 512:(j + 1) * 512],
                                           start=True, stop=True)
                        mmi.then_inc(sm_mm, 1)
                        t.wait_ge(sm_v, v0 + 1)
                        for j in range(2):
                            mmi = t.matmul(pp[j][:], BCh[:, 1024 + csl.start:1024 + csl.stop],
                                           prevb[:, 1024 + j * 512:1024 + (j + 1) * 512],
                                           start=True, stop=True)
                        mmi.then_inc(sm_mm, 1)
                    @blk.vector
                    def _(v):
                        if onepass and getattr(c, "_spill_wait", None):
                            v.wait_ge(dma, c._spill_wait)
                        v.wait_ge(sm_mm, mm0 + 2)
                        for j in range(2):
                            v.tensor_tensor(ytmp[:, j * 512:(j + 1) * 512], pp[j][:],
                                            bass.AP(ETexp[:].tensor, ETexp[:].offset + j * 8,
                                                    [list(ETexp[:].ap[0]), [1, 8], [0, 64]]),
                                            ALU.mult)
                        v.tensor_add(ysb[:, 0:1024], ytmp[:], psa[:]).then_inc(sm_v, 1)
                        v.wait_ge(sm_mm, mm0 + 3)
                        for j in range(2):
                            v.tensor_tensor(ytmp[:, j * 512:(j + 1) * 512], pp[j][:],
                                            bass.AP(ETexp[:].tensor, ETexp[:].offset + 16 + j * 8,
                                                    [list(ETexp[:].ap[0]), [1, 8], [0, 64]]),
                                            ALU.mult)
                        v.tensor_add(ysb[:, 1024:2048], ytmp[:], psb[:])
                c.semc["mm"] = mm0 + 3; c.semc["vv"] = v0 + 1

            if onepass:
                g = half * 8 + ci
                dma0s = c.semc["dma"]
                with nc.Block() as blk:
                    @blk.gpsimd
                    def _(gg):
                        gg.dma_start(rets["dysp"].ap()[g], ysb[:]).then_inc(dma, 16)
                        gg.dma_start(rets["dcsp"].ap()[g],
                                     BCh[:, 1024 + csl.start:1024 + csl.stop]).then_inc(dma, 16)
                        gg.dma_start(rets["desp"].ap()[g], ETexp[:]).then_inc(dma, 16)
                c.semc["dma"] = dma0s + 48
                c._spill_wait = c.semc["dma"]

            if not front and not onepass:
                dma0, as0, v0, f0 = (c.semc[s] for s in ("dma", "as", "vv", "fv"))
                with nc.Block() as blk:
                    @blk.gpsimd
                    def _(g):
                        g.dma_start(zch[:], dz.ap()[g0:g0 + 128, :]).then_inc(dma, 16)
                    @blk.scalar
                    def _(s):
                        s.wait_ge(dma, dma0 + 16)
                        s.activation(sz[:], zch[:], AF.Silu).then_inc(sm_as, 1)
                        s.wait_ge(sm_v, v0 + 1)
                        s.activation(rms1[:, 2:3], rms1[:, 1:2], AF.Ln, bias=EPS)
                        s.activation(rms1[:, 3:4], rms1[:, 2:3], AF.Exp,
                                     scale=f32(-0.5)).then_inc(sm_as, 1)
                    @blk.vector
                    def _(v):
                        v.wait_ge(sm_as, as0 + 1)
                        v.tensor_mul(ysb[:], ysb[:], sz[:])
                        v.tensor_mul(sz[:], ysb[:], ysb[:])
                        ins = v.reduce_sum(rms1[:, 0:1], sz[:], axis=AX)
                        ins.then_inc(smf, 1)
                        v.wait_ge(smf, f0 + 1)
                        v.tensor_single_scalar(rms1[:, 1:2], rms1[:, 0:1],
                                               1.0 / 2048.0, ALU.mult).then_inc(sm_v, 1)
                        v.wait_ge(sm_as, as0 + 2)
                        v.tensor_tensor(yn[:], ysb[:], bcol(rms1[:, 3:4], 2048), ALU.mult)
                        v.nop().then_inc(sm_v, 1)
                c.semc["dma"] = dma0 + 16; c.semc["as"] = as0 + 2
                c.semc["vv"] = v0 + 2; c.semc["fv"] = f0 + 1

                mm0, v0 = c.semc["mm"], c.semc["vv"]
                with nc.Block() as blk:
                    @blk.tensor
                    def _(t):
                        for b in range(16):
                            if b >= 2:
                                t.wait_ge(sm_v, v0 + b - 1)
                            t.transpose(pbf[:, (b % 2) * 1024:(b % 2) * 1024 + 128],
                                        yn[:, b * 128:(b + 1) * 128], idb[:]).then_inc(sm_mm, 1)
                    @blk.vector
                    def _(v):
                        for b in range(16):
                            v.wait_ge(sm_mm, mm0 + b + 1)
                            v.tensor_scalar_mul(ynT[:, b * 128:(b + 1) * 128],
                                                pbf[:, (b % 2) * 1024:(b % 2) * 1024 + 128],
                                                normw[:, b:b + 1]).then_inc(sm_v, 1)
                c.semc["mm"] = mm0 + 16; c.semc["vv"] = v0 + 16

                mm0, ev0, dma0, st0 = (c.semc[s] for s in ("mm", "ev", "dma", "st"))
                with nc.Block() as blk:
                    @blk.gpsimd
                    def _(g):
                        g.dma_start(xres[:].rearrange("p (kc t) -> p kc t", kc=8),
                                    fview(dxin, 3 + g0, 128)).then_inc(dma, 16)
                    @blk.tensor
                    def _(t):
                        for o in range(8):
                            if o >= 2:
                                t.wait_ge(sm_ev, ev0 + o - 1)
                            for kc in range(16):
                                mmi = t.matmul(pp[o % 2][:, 0:128],
                                               woutr[:, kc * 1024 + o * 128: kc * 1024 + (o + 1) * 128],
                                               ynT[:, kc * 128:(kc + 1) * 128],
                                               start=(kc == 0), stop=(kc == 15))
                            mmi.then_inc(sm_mm, 1)
                    @blk.vector
                    def _(v):
                        for o in range(8):
                            v.wait_ge(sm_mm, mm0 + o + 1)
                            v.wait_ge(dma, dma0 + 16)
                            if o >= 2:
                                v.wait_ge(st_sem, st0 + 16 * (o - 1))
                            v.tensor_add(ost[o % 2][:, 0:128], pp[o % 2][:, 0:128],
                                         xres[:, o * 128:(o + 1) * 128]).then_inc(sm_ev, 1)
                    @blk.sync
                    def _(sy):
                        for o in range(8):
                            sy.wait_ge(sm_ev, ev0 + o + 1)
                            sy.dma_start(dout_x1.ap()[o * 128:(o + 1) * 128, g0:g0 + 128],
                                         ost[o % 2][:, 0:128]).then_inc(st_sem, 16)
                        sy.wait_ge(st_sem, st0 + 16 * 8)
                c.semc["mm"] = mm0 + 8; c.semc["ev"] = ev0 + 8
                c.semc["dma"] = dma0 + 16; c.semc["st"] = st0 + 16 * 8
        if dbg:
            c.dbg_handles = dict(x4b=x4b, BCh=BCh, CD2=CD2, CDB=CDB, E=E, Mp=Mp,
                                 xT=xT, xtf=xtf, ysb=ysb, yn=yn, ynT=ynT, TD=TD,
                                 ETexp=ETexp, ddT=ddT, GmT=GmT, BT=BT, carry=carry,
                                 prevb=prevb, IDall=IDall, zch=zch, rms1=rms1)
            break
        c.pop_scope()
        c.pop_scope()

    st0 = c.semc["st"]
    with nc.Block() as blk:
        @blk.sync
        def _(sy):
            sy.dma_start(dstate.ap(), carry[:]).then_inc(st_sem, 16)
            sy.wait_ge(st_sem, st0 + 16)
    c.semc["st"] = st0 + 16
    return rets



def emit_l0part2(c, dxin, dS0ap, dmask, dW, rets, dout_x1):
    nc = c.nc
    P = get_pool(c)
    psa = P["psa"]; psb = P["psb"]; pp = P["pp"]; pbf = P["pbf"]
    CDL2 = rets["CDL2"]
    dz, dysp, dcsp, desp = rets["dz"], rets["dysp"], rets["dcsp"], rets["desp"]
    sm_mm, sm_ev, sm_as, sm_v = c.sem("mm"), c.sem("ev"), c.sem("as"), c.sem("vv")
    smf = c.sem("fv"); dma = c.sem("dma"); st_sem = c.sem("st")

    idb = c.sb("q_idb", [128, 128], BF16)
    onesrow_f = c.sb("q_onesr", [1, 128])
    normw = c.sb("q_normw", [128, 16])
    woutr = c.sb("q_woutr", [128, 16 * 1024], BF16)
    ones32 = c.sb("q_ones32", [32, 128])
    S0m = c.sb("q_S0m", [128, 2048], BF16)
    S0f = c.sb("q_S0f", [128, 2048])
    mcol = c.sb("q_mcol", [128, 1])
    CC = c.sb("q_CC", [32, 17])
    cdflat = c.sb("q_cdflat", [1, 512])
    CUMB = c.sb("q_CUMB", [128, 512])
    ysb = c.sb("q_ysb", [128, 2048])
    S0c = c.sb("q_S0c", [128, 2048], BF16)
    Cc = c.sb("q_Cc", [128, 128], BF16)
    ETe = c.sb("q_ETe", [128, 32])
    ytmp = c.sb("q_ytmp", [128, 1024])
    sz = c.sb("q_sz", [128, 2048])
    yn = c.sb("q_yn", [128, 2048], BF16)
    ynT = c.sb("q_ynT", [128, 2048], BF16)
    zch = c.sb("q_zch", [128, 2048], BF16)
    xres = c.sb("q_xres", [128, 8 * 128])
    rms1 = c.sb("q_rms1", [128, 4])
    ost = [c.sb(f"q_ost{i}", [128, 512]) for i in range(2)]
    loadblk(c, [(idb[:], dW["ident_bf"].ap()), (onesrow_f[:], dW["onesrow_f"].ap()),
                (normw[:], dW["mb_normw"].ap()), (ones32[:], dW["ones32"].ap()),
                (woutr[:].rearrange("p (kc o) -> p kc o", kc=16),
                 dW["mb_woutT"].ap().rearrange("(kc p) o -> p kc o", p=128)),
                (S0f[:], dS0ap), (mcol[:], dmask.ap())])
    with nc.Block() as blk:
        @blk.vector
        def _(v):
            v.tensor_tensor(S0m[:], S0f[:], bcol(mcol[:], 2048), ALU.mult)
        @blk.gpsimd
        def _(g):
            g.memset(CC[:, 0:1], 0.0)
    with nc.Block() as blk:
        @blk.vector
        def _(v):
            v.tensor_tensor_scan(CC[:, 1:17], CDL2[:], ones32[:, 0:16], 0.0,
                                 ALU.add, ALU.mult)
    with nc.Block() as blk:
        @blk.scalar
        def _(s):
            s.activation(CC[:, 0:16], CC[:, 0:16], AF.Exp)
    dma0 = c.semc["dma"]
    with nc.Block() as blk:
        @blk.gpsimd
        def _(g):
            g.dma_start(cdflat[:].rearrange("p (a b) -> p a b", a=32), CC[:, 0:16]
                        ).then_inc(dma, 16)
            g.wait_ge(dma, dma0 + 16)
    c.semc["dma"] = dma0 + 16
    mm0 = c.semc["mm"]
    with nc.Block() as blk:
        @blk.tensor
        def _(t):
            t.matmul(psa[:, 0:512], onesrow_f[:], cdflat[:],
                     start=True, stop=True).then_inc(sm_mm, 1)
        @blk.vector
        def _(v):
            v.wait_ge(sm_mm, mm0 + 1)
            v.tensor_copy(CUMB[:], psa[:, 0:512])
    c.semc["mm"] = mm0 + 1

    for g in range(16):
        g0 = g * 128
        dma0, mm0, v0, as0, f0, st0 = (c.semc[s] for s in
                                       ("dma", "mm", "vv", "as", "fv", "st"))
        with nc.Block() as blk:
            @blk.gpsimd
            def _(gg):
                gg.dma_start(ysb[:], dysp.ap()[g]).then_inc(dma, 16)
                gg.dma_start(Cc[:], dcsp.ap()[g]).then_inc(dma, 16)
                gg.dma_start(ETe[:], desp.ap()[g]).then_inc(dma, 16)
                gg.dma_start(zch[:], dz.ap()[g0:g0 + 128, :]).then_inc(dma, 16)
                gg.dma_start(xres[:].rearrange("p (kc t) -> p kc t", kc=8),
                             fview(dxin, 3 + g0, 128)).then_inc(dma, 16)
            @blk.vector
            def _(v):
                v.tensor_tensor(S0c[:], S0m[:],
                                bass.AP(CUMB[:].tensor, CUMB[:].offset + g,
                                        [list(CUMB[:].ap[0]), [16, 32], [0, 64]]),
                                ALU.mult).then_inc(sm_v, 1)
            @blk.tensor
            def _(t):
                t.wait_ge(sm_v, v0 + 1)
                t.wait_ge(dma, dma0 + 32)
                for j in range(4):
                    ds = (psa if j < 2 else psb)[:, (j % 2) * 512:(j % 2) * 512 + 512]
                    t.matmul(ds, Cc[:], S0c[:, j * 512:(j + 1) * 512],
                             start=True, stop=True).then_inc(sm_mm, 1)
        c.semc["dma"] = dma0 + 80
        dma_all = dma0 + 80
        c.semc["mm"] = mm0 + 4; c.semc["vv"] = v0 + 1
        v0, as0 = c.semc["vv"], c.semc["as"]
        with nc.Block() as blk:
            @blk.vector
            def _(v):
                v.wait_ge(dma, dma0 + 48)
                for jh in range(2):
                    ps = psa if jh == 0 else psb
                    v.tensor_tensor(ytmp[:], ps[:],
                                    bass.AP(ETe[:].tensor, ETe[:].offset + jh * 16,
                                            [list(ETe[:].ap[0]), [1, 16], [0, 64]]),
                                    ALU.mult)
                    v.tensor_add(ysb[:, jh * 1024:(jh + 1) * 1024],
                                 ysb[:, jh * 1024:(jh + 1) * 1024], ytmp[:])
                v.nop().then_inc(sm_v, 1)
            @blk.scalar
            def _(s):
                s.wait_ge(dma, dma0 + 64)
                s.activation(sz[:], zch[:], AF.Silu).then_inc(sm_as, 1)
        c.semc["vv"] = v0 + 1; c.semc["as"] = as0 + 1
        v0, as0, f0 = c.semc["vv"], c.semc["as"], c.semc["fv"]
        with nc.Block() as blk:
            @blk.vector
            def _(v):
                v.tensor_mul(ysb[:], ysb[:], sz[:])
                v.tensor_mul(sz[:], ysb[:], ysb[:])
                v.reduce_sum(rms1[:, 0:1], sz[:], axis=AX).then_inc(smf, 1)
                v.wait_ge(smf, f0 + 1)
                v.tensor_single_scalar(rms1[:, 1:2], rms1[:, 0:1],
                                       1.0 / 2048.0, ALU.mult).then_inc(sm_v, 1)
                v.wait_ge(sm_as, as0 + 1)
                v.tensor_tensor(yn[:], ysb[:], bcol(rms1[:, 3:4], 2048), ALU.mult)
                v.nop().then_inc(sm_v, 1)
            @blk.scalar
            def _(s):
                s.wait_ge(sm_v, v0 + 1)
                s.activation(rms1[:, 2:3], rms1[:, 1:2], AF.Ln, bias=EPS)
                s.activation(rms1[:, 3:4], rms1[:, 2:3], AF.Exp,
                             scale=f32(-0.5)).then_inc(sm_as, 1)
        c.semc["vv"] = v0 + 2; c.semc["as"] = as0 + 1; c.semc["fv"] = f0 + 1
        mm0, v0 = c.semc["mm"], c.semc["vv"]
        with nc.Block() as blk:
            @blk.tensor
            def _(t):
                for b in range(16):
                    if b >= 2:
                        t.wait_ge(sm_v, v0 + b - 1)
                    t.transpose(pbf[:, (b % 2) * 1024:(b % 2) * 1024 + 128],
                                yn[:, b * 128:(b + 1) * 128], idb[:]).then_inc(sm_mm, 1)
            @blk.vector
            def _(v):
                for b in range(16):
                    v.wait_ge(sm_mm, mm0 + b + 1)
                    v.tensor_scalar_mul(ynT[:, b * 128:(b + 1) * 128],
                                        pbf[:, (b % 2) * 1024:(b % 2) * 1024 + 128],
                                        normw[:, b:b + 1]).then_inc(sm_v, 1)
        c.semc["mm"] = mm0 + 16; c.semc["vv"] = v0 + 16
        mm0, ev0, dma0, st0 = (c.semc[s] for s in ("mm", "ev", "dma", "st"))
        with nc.Block() as blk:
            @blk.tensor
            def _(t):
                for o in range(8):
                    if o >= 2:
                        t.wait_ge(sm_ev, ev0 + o - 1)
                    for kc in range(16):
                        mmi = t.matmul(pp[o % 2][:, 0:128],
                                       woutr[:, kc * 1024 + o * 128: kc * 1024 + (o + 1) * 128],
                                       ynT[:, kc * 128:(kc + 1) * 128],
                                       start=(kc == 0), stop=(kc == 15))
                    mmi.then_inc(sm_mm, 1)
            @blk.vector
            def _(v):
                v.wait_ge(dma, dma_all)
                for o in range(8):
                    v.wait_ge(sm_mm, mm0 + o + 1)
                    if o >= 2:
                        v.wait_ge(st_sem, st0 + 16 * (o - 1))
                    v.tensor_add(ost[o % 2][:, 0:128], pp[o % 2][:, 0:128],
                                 xres[:, o * 128:(o + 1) * 128]).then_inc(sm_ev, 1)
            @blk.sync
            def _(sy):
                for o in range(8):
                    sy.wait_ge(sm_ev, ev0 + o + 1)
                    sy.dma_start(dout_x1.ap()[o * 128:(o + 1) * 128, g0:g0 + 128],
                                 ost[o % 2][:, 0:128]).then_inc(st_sem, 16)
                sy.wait_ge(st_sem, st0 + 16 * 8)
        c.semc["mm"] = mm0 + 8; c.semc["ev"] = ev0 + 8
        c.semc["st"] = st0 + 16 * 8


def _percol(v):
    return np.ascontiguousarray(v.reshape(-1, 128).T)


def prep_weights(I):
    W = {}
    W["ones_bf"] = np.ones((128, 128), dtype=bfd)
    W["ident_bf"] = np.eye(128, dtype=np.float32).astype(bfd)
    W["ident_f32"] = np.eye(128, dtype=np.float32)
    W["l1f_w1T"] = np.ascontiguousarray(I["L1_ffn_w1_w"].T)
    W["l1f_w2T"] = np.ascontiguousarray(I["L1_ffn_w2_w"].T)
    W["l1f_h1T"] = np.ascontiguousarray(I["L1_ffn_h1_w"].T)
    W["l1f_h2T"] = np.ascontiguousarray(I["L1_ffn_h2_w"].T)
    W["l1f_w3T"] = np.ascontiguousarray(I["L1_ffn_w3_w"].T)
    W["l1f_s1"] = _percol(I["L1_ffn_w1_s"])
    W["l1f_s2"] = _percol(I["L1_ffn_w2_s"])
    W["l1f_h1s"] = _percol(I["L1_ffn_h1_s"])
    W["l1f_h2s"] = _percol(I["L1_ffn_h2_s"])
    W["l1f_w3s"] = _percol(I["L1_ffn_w3_s"])
    W["onescol"] = np.ones((128, 1), dtype=np.float32)
    W["moe_gw"] = np.ascontiguousarray(I["L0_ffn_gate_W"])
    oh8 = np.zeros((8, 8 * 128), dtype=np.float32)
    for e in range(8):
        oh8[e, e * 128:(e + 1) * 128] = 1.0
    W["moe_oh8"] = oh8
    W["moe_b1c"] = np.concatenate([_percol(I["L0_ffn_b1"][e]) for e in range(8)], axis=1)
    W["moe_b2"] = np.ascontiguousarray(I["L0_ffn_b2"])
    W["moe_W1f"] = np.ascontiguousarray(I["L0_ffn_W1"].reshape(8192, 2048))
    W["moe_W2f"] = np.ascontiguousarray(I["L0_ffn_W2"].reshape(16384, 1024))
    W["onesrow_bf"] = np.ones((1, 128), dtype=bfd)
    wqkvT = np.ascontiguousarray(I["L1_attn_Wqkv"].T)
    W["l1a_wqT"] = np.ascontiguousarray(wqkvT[:, 0:1024])
    W["l1a_wkT"] = np.ascontiguousarray(wqkvT[:, 1024:2048])
    W["l1a_wvT"] = np.ascontiguousarray(wqkvT[:, 2048:3072])
    W["l1a_woutT"] = np.ascontiguousarray(I["L1_attn_Wout"].T)
    W["l1a_bq"] = _percol(I["L1_attn_bqkv"][0:1024])
    W["l1a_bk"] = _percol(I["L1_attn_bqkv"][1024:2048])
    W["l1a_bv"] = I["L1_attn_bqkv"][2048:3072].reshape(1, 1024).astype(bfd)
    W["l1a_bout"] = _percol(I["L1_attn_bout"])
    ll = np.arange(128)[:, None]; jj = np.arange(256)[None, :]
    mR = np.where((jj >= ll + 65) & (jj <= ll + 128), 0.0, -1e9).astype(np.float32)
    m0 = np.where((jj >= ll + 65) & (jj <= ll + 128) & (jj >= 128), 0.0, -1e9).astype(np.float32)
    W["l1a_maskR"] = mR
    W["l1a_mask0"] = m0
    W["onesrow_f"] = np.ones((1, 128), dtype=np.float32)
    W["ones32"] = np.ones((32, 128), dtype=np.float32)
    W["mb_winT"] = np.ascontiguousarray(I["L0_attn_W_in"].T)
    cw4 = I["L0_attn_conv_w"]
    W["mb_convw"] = np.ascontiguousarray(
        np.concatenate([cw4[ct * 128:(ct + 1) * 128, :] for ct in range(18)], axis=1))
    W["mb_convb"] = _percol(I["L0_attn_conv_b"])
    W["mb_dtb"] = np.ascontiguousarray(I["L0_attn_dt_bias"][:, None])
    W["mb_alog"] = np.ascontiguousarray(I["L0_attn_A_log"][:, None])
    W["mb_drep"] = np.ascontiguousarray(np.tile(I["L0_attn_D"][None, :], (128, 1)))
    W["mb_normw"] = _percol(I["L0_attn_norm_w"])
    W["mb_woutT"] = np.ascontiguousarray(I["L0_attn_W_out"].T)
    W["mb_oh32"] = np.ascontiguousarray(
        np.repeat(np.eye(32, dtype=np.float32), 128, axis=1))
    W["mb_triu"] = np.triu(np.ones((128, 128), dtype=np.float32))
    return W


def _declare(c, W, names):
    d = {}
    for nm in names:
        a = W[nm]
        dt = BF16 if a.dtype == bfd else F32
        d[nm] = c.din(nm, a.shape, dt)
    return d


MOE_W = ["ones_bf", "onescol", "ident_f32", "moe_gw", "moe_oh8", "moe_b1c",
         "moe_b2", "moe_W1f", "moe_W2f"]


def launch_l0ffn(x1_shards, W):
    c = Ctx()
    dx1 = c.din("x1", [D, T])
    dW = _declare(c, W, MOE_W)
    do = c.dout("x2", [D, T])
    emit_l0ffn(c, dx1, dW, do)
    c.nc.finalize()
    in_maps = [dict({nm: W[nm] for nm in MOE_W}, x1=x1_shards[i]) for i in range(N_CORES)]
    res = run_bass_kernel_spmd(c.nc, in_maps, core_ids=list(range(N_CORES)))
    return [r["x2"] for r in res.results]


MAMBA_W = ["ones_bf", "onesrow_f", "ident_f32", "ident_bf", "ones32",
           "mb_winT", "mb_convw", "mb_convb", "mb_dtb", "mb_alog", "mb_drep",
           "mb_normw", "mb_woutT", "mb_oh32", "mb_triu"]


def shard_xin(x_full):
    out = []
    for cid in range(N_CORES):
        bi, hi = cid // 2, cid % 2
        if hi == 0:
            rows = np.concatenate([np.zeros((3, D), np.float32),
                                   x_full[bi, 0:T, :]], axis=0)
        else:
            rows = x_full[bi, T - 3:2 * T, :]
        out.append(np.ascontiguousarray(rows.T))
    return out


def launch_l0attn(xin_shards, S0_list, W):
    c = Ctx()
    dxin = c.din("xin", [D, 2051])
    dS0 = c.din("S0", [128, 2048])
    dW = _declare(c, W, MAMBA_W)
    do = c.dout("x1", [D, T])
    dst = c.dout("fstate", [128, 2048])
    emit_l0attn(c, dxin, dS0.ap(), dW, do, dst)
    c.nc.finalize()
    in_maps = [dict({nm: W[nm] for nm in MAMBA_W}, xin=xin_shards[i], S0=S0_list[i])
               for i in range(N_CORES)]
    res = run_bass_kernel_spmd(c.nc, in_maps, core_ids=list(range(N_CORES)))
    return [r["x1"] for r in res.results], [r["fstate"] for r in res.results]


ATTN_W = ["ones_bf", "onesrow_bf", "ident_bf", "l1a_wqT", "l1a_wkT", "l1a_wvT",
          "l1a_woutT", "l1a_bq", "l1a_bk", "l1a_bv", "l1a_bout", "l1a_maskR",
          "l1a_mask0"]


def launch_l1attn(x2_shards, W):
    c = Ctx()
    dx2 = c.din("x2", [D, T])
    dxh = c.din("xh", [D, 63])
    dW = _declare(c, W, ATTN_W)
    do = c.dout("x3", [D, T])
    emit_l1attn(c, dx2, dxh.ap(), dW, do)
    c.nc.finalize()
    in_maps = []
    for i in range(N_CORES):
        xh = np.zeros((D, 63), dtype=np.float32)
        m = dict({nm: W[nm] for nm in ATTN_W}, x2=x2_shards[i], xh=xh)
        if i % 2 == 1:
            m["xh"] = np.ascontiguousarray(x2_shards[i - 1][:, 2048 - 63:2048])
            m["l1a_mask0"] = W["l1a_maskR"]
        in_maps.append(m)
    res = run_bass_kernel_spmd(c.nc, in_maps, core_ids=list(range(N_CORES)))
    return [r["x3"] for r in res.results]


L1FFN_W = ["ones_bf", "l1f_w1T", "l1f_w2T", "l1f_h1T", "l1f_h2T", "l1f_w3T",
           "l1f_s1", "l1f_s2", "l1f_h1s", "l1f_h2s", "l1f_w3s"]


def launch_l1ffn(x3_shards, W):
    c = Ctx()
    dx3 = c.din("x3", [D, T])
    dW = _declare(c, W, L1FFN_W)
    do = c.dout("x4", [D, T])
    emit_l1ffn(c, dx3, dW, do)
    c.nc.finalize()
    in_maps = [dict({nm: W[nm] for nm in L1FFN_W}, x3=x3_shards[i]) for i in range(N_CORES)]
    res = run_bass_kernel_spmd(c.nc, in_maps, core_ids=list(range(N_CORES)))
    return [r["x4"] for r in res.results]


def shard_rows(full):
    out = []
    for cid in range(N_CORES):
        bi, hi = cid // 2, cid % 2
        out.append(np.ascontiguousarray(full[bi, hi * T:(hi + 1) * T, :].T))
    return out


def unshard_rows(shards):
    full = np.empty((4, 4096, D), dtype=np.float32)
    for cid in range(N_CORES):
        bi, hi = cid // 2, cid % 2
        full[bi, hi * T:(hi + 1) * T, :] = np.asarray(shards[cid]).T
    return full


ALL_W = None


def build_fused(W, dev=False):
    global ALL_W
    ALL_W = sorted(set(MAMBA_W) | set(MOE_W) | set(ATTN_W) | set(L1FFN_W))
    c = Ctx()
    nc = c.nc
    dxin = c.din("xin", [D, 2051])
    dmask = c.din("smask", [128, 1])
    dW = _declare(c, W, ALL_W)
    dout = c.dout("out", [D, T])
    mk = c.dout if dev else c.dtmp
    dx1 = mk("f_x1", [D, T])
    dx2 = mk("f_x2", [D, T])
    dx3 = mk("f_x3", [D, T])
    dsend = c.dtmp("f_send", [128, 2048])
    drecv = c.dtmp("f_recv", [256, 2048])
    dsend2 = c.dtmp("f_send2", [D, 63])
    drecv2 = c.dtmp("f_recv2", [2 * D, 63])
    ddum = c.dtmp("f_dumstate", [128, 2048])
    cc_sem = c.sem("cc")

    cdl2 = c.sb("f_CDL2", [32, 16])
    c.push_scope()
    with nc.named_scope("mamba_main"):
        rets = emit_l0attn(c, dxin, None, dW, dx1, dsend, onepass=True, cdl2=cdl2)
    c.pop_scope()
    with nc.Block() as blk:
        @blk.gpsimd
        def _(g):
            g.collective_compute(
                "AllGather", ALU.bypass,
                replica_groups=[[0, 1], [2, 3], [4, 5], [6, 7]],
                ins=[dsend.ap().opt()], outs=[drecv.ap().opt()],
            ).then_inc(cc_sem)
            g.wait_ge(cc_sem, 1)
    c.semc["cc"] = 1
    c.push_scope()
    with nc.named_scope("mamba_post"):
        emit_l0part2(c, dxin, drecv.ap()[0:128, :], dmask, dW, rets, dx1)
    c.pop_scope()
    c.push_scope()
    with nc.named_scope("moe"):
        emit_l0ffn(c, dx1, dW, dx2)
    c.pop_scope()
    dma, st_sem = c.sem("dma"), c.sem("st")
    dma0 = c.semc["dma"]
    with nc.Block() as blk:
        @blk.gpsimd
        def _(g):
            g.dma_start(dsend2.ap(), dx2.ap()[:, T - 63:T]).then_inc(dma, 16)
            g.wait_ge(dma, dma0 + 16)
            g.collective_compute(
                "AllGather", ALU.bypass,
                replica_groups=[[0, 1], [2, 3], [4, 5], [6, 7]],
                ins=[dsend2.ap().opt()], outs=[drecv2.ap().opt()],
            ).then_inc(cc_sem)
            g.wait_ge(cc_sem, 2)
    c.semc["dma"] = dma0 + 16
    c.semc["cc"] = 2
    c.push_scope()
    with nc.named_scope("attn"):
        emit_l1attn(c, dx2, drecv2.ap()[0:D, :], dW, dx3)
    c.pop_scope()
    c.push_scope()
    with nc.named_scope("l1ffn"):
        emit_l1ffn(c, dx3, dW, dout)
    c.pop_scope()
    return c


def kernel_fused(**inputs):
    W = prep_weights(inputs)
    c = build_fused(W)
    c.nc.finalize()
    xin_shards = shard_xin(inputs["L0_attn_inputs"])
    in_maps = []
    for i in range(N_CORES):
        m = {nm: W[nm] for nm in ALL_W}
        m["xin"] = xin_shards[i]
        m["smask"] = np.full((128, 1), 1.0 if i % 2 == 1 else 0.0, np.float32)
        if i % 2 == 1:
            m["l1a_mask0"] = W["l1a_maskR"]
        in_maps.append(m)
    res = run_bass_kernel_spmd(c.nc, in_maps, core_ids=list(range(N_CORES)))
    return unshard_rows([r["out"] for r in res.results])


def kernel_unfused(**inputs):
    W = prep_weights(inputs)
    xin_shards = shard_xin(inputs["L0_attn_inputs"])
    zeros = [np.zeros((128, 2048), np.float32) for _ in range(N_CORES)]
    _, F = launch_l0attn(xin_shards, zeros, W)
    S0 = [zeros[i] if i % 2 == 0 else np.ascontiguousarray(np.asarray(F[i - 1]))
          for i in range(N_CORES)]
    x1_shards, _ = launch_l0attn(xin_shards, S0, W)
    x1_shards = [np.ascontiguousarray(np.asarray(a)) for a in x1_shards]
    x2_shards = [np.ascontiguousarray(np.asarray(a)) for a in launch_l0ffn(x1_shards, W)]
    x3_shards = [np.ascontiguousarray(np.asarray(a)) for a in launch_l1attn(x2_shards, W)]
    x4_shards = launch_l1ffn(x3_shards, W)
    return unshard_rows(x4_shards)


def kernel(**inputs):
    return kernel_fused(**inputs)


# version 07 (final)
# speedup vs baseline: 1.1925x; 1.0470x over previous
import numpy as np
from contextlib import ExitStack
import concourse.bass as bass
import concourse.mybir as mybir
from concourse.bass_utils import run_bass_kernel_spmd
import ml_dtypes

F32 = mybir.dt.float32
BF16 = mybir.dt.bfloat16
AF = mybir.ActivationFunctionType
ALU = mybir.AluOpType
AX = mybir.AxisListType.X
bfd = ml_dtypes.bfloat16
f32 = np.float32

T = 2048
D = 1024
N_CORES = 8
EPS = f32(1e-5)


def ap0(ap, extra_dims):
    dims = [list(d) for d in ap.ap] + [list(d) for d in extra_dims]
    return bass.AP(ap.tensor, ap.offset, dims)


class Ctx:
    def __init__(self):
        self.nc = bass.Bass("TRN2", target_bir_lowering=False)
        self.es = ExitStack()
        self.semc = {}
        self._sems = {}

    def sem(self, name):
        if name not in self._sems:
            self._sems[name] = self.es.enter_context(self.nc.semaphore(name))
            self.semc[name] = 0
        return self._sems[name]

    def _uniq(self, name):
        if not hasattr(self, "_names"):
            self._names = set()
        base, n = name, 1
        while name in self._names:
            n += 1
            name = f"{base}_{n}"
        self._names.add(name)
        return name

    def sb(self, name, shape, dt=F32):
        es = self.scopes[-1] if getattr(self, "scopes", None) else self.es
        return es.enter_context(self.nc.sbuf_tensor(self._uniq(name), shape, dt))

    def sb_base(self, name, shape, dt=F32):
        return self.es.enter_context(self.nc.sbuf_tensor(self._uniq(name), shape, dt))

    def push_scope(self):
        if not hasattr(self, "scopes"):
            self.scopes = []
        self.scopes.append(ExitStack())

    def pop_scope(self):
        self.scopes.pop().close()

    def ps(self, name, shape, dt=F32):
        return self.es.enter_context(self.nc.psum_tensor(name, shape, dt))

    def dtmp(self, name, shape, dt=F32):
        return self.nc.dram_tensor(self._uniq(name), list(shape), dt)

    def din(self, name, shape, dt=F32):
        return self.nc.dram_tensor(name, list(shape), dt, kind="ExternalInput")

    def dout(self, name, shape, dt=F32):
        return self.nc.dram_tensor(name, list(shape), dt, kind="ExternalOutput")


def get_pool(c):
    if not hasattr(c, "_pool"):
        psa = c.ps("P_psa", [128, 1024])
        psb = c.ps("P_psb", [128, 1024])
        pp0 = c.ps("P_pp0", [128, 512])
        pp1 = c.ps("P_pp1", [128, 512])
        pbf = c.ps("P_pbf", [128, 2048], BF16)
        c._pool = dict(psa=psa, psb=psb, pbf=pbf,
                       pp=[pp0[:], pp1[:], psb[:, 0:512], psb[:, 512:1024]])
    return c._pool


def loadblk(c, pairs):
    nc, sem = c.nc, c.sem("dma")
    n0 = c.semc["dma"]
    with nc.Block() as blk:
        @blk.gpsimd
        def _(g):
            n = n0
            for dst, src in pairs:
                g.dma_start(dst, src).then_inc(sem, 16)
                n += 16
            g.wait_ge(sem, n)
    c.semc["dma"] = n0 + 16 * len(pairs)


def fview(d, t0, Tt):
    return d.ap()[:, t0:t0 + Tt].rearrange("(kc p) t -> p kc t", p=128)


def wslice(dw, o0, ow):
    return dw.ap()[:, o0:o0 + ow].rearrange("(kc p) o -> p kc o", p=128)


def emit_ln(c, xnsl, ones_bf, S, Tt, kcs=8, cast_from=None):
    nc = c.nc
    sq = S["sq"]
    if cast_from is not None:
        with nc.Block() as blk:
            @blk.vector
            def _(v):
                for kc in range(kcs):
                    v.tensor_copy(xnsl(kc), cast_from(kc))
    ps_a, ps_b = S["ps_a"], S["ps_b"]
    mu, va, rstd = S["mu"], S["va"], S["rstd"]
    rdim = 1.0 / (kcs * 128)
    with nc.Block() as blk:
        @blk.scalar
        def _(s):
            for kc in range(kcs):
                s.activation(sq[:, kc * Tt:(kc + 1) * Tt], xnsl(kc), AF.Square)
    with nc.Block() as blk:
        @blk.tensor
        def _(t):
            ntt = (Tt + 511) // 512
            for tt in range(ntt):
                w = min(512, Tt - tt * 512)
                for kc in range(kcs):
                    t.matmul(ps_a[:, tt * 512:tt * 512 + w], ones_bf[:],
                             xnsl(kc)[:, tt * 512:tt * 512 + w],
                             start=(kc == 0), stop=(kc == kcs - 1))
                for kc in range(kcs):
                    t.matmul(ps_b[:, tt * 512:tt * 512 + w], ones_bf[:],
                             sq[:, kc * Tt + tt * 512:kc * Tt + tt * 512 + w],
                             start=(kc == 0), stop=(kc == kcs - 1))
    smf = c.sem("fv")
    with nc.Block() as blk:
        @blk.vector
        def _(v):
            f0 = c.semc["fv"]
            v.tensor_single_scalar(mu[:, 0:Tt], ps_a[:, 0:Tt], rdim, ALU.mult)
            v.tensor_single_scalar(va[:, 0:Tt], ps_b[:, 0:Tt], rdim,
                                   ALU.mult).then_inc(smf, 1)
            v.wait_ge(smf, f0 + 1)
            v.tensor_mul(rstd[:, 0:Tt], mu[:, 0:Tt], mu[:, 0:Tt]).then_inc(smf, 1)
            v.wait_ge(smf, f0 + 2)
            v.tensor_sub(va[:, 0:Tt], va[:, 0:Tt], rstd[:, 0:Tt])
            c.semc["fv"] = f0 + 2
    with nc.Block() as blk:
        @blk.scalar
        def _(s):
            f0 = c.semc["fv"]
            s.activation(va[:, 0:Tt], va[:, 0:Tt], AF.Ln, bias=EPS).then_inc(smf, 1)
            s.wait_ge(smf, f0 + 1)
            s.activation(rstd[:, 0:Tt], va[:, 0:Tt], AF.Exp, scale=f32(-0.5))
            c.semc["fv"] = f0 + 1
    with nc.Block() as blk:
        @blk.vector
        def _(v):
            st = S["st"]
            f0 = c.semc["fv"]
            for kc in range(kcs):
                ntt = (Tt + 511) // 512
                for tt in range(ntt):
                    w = min(512, Tt - tt * 512)
                    v.tensor_sub(st[:, 0:w], xnsl(kc)[:, tt * 512:tt * 512 + w],
                                 mu[:, tt * 512:tt * 512 + w]).then_inc(smf, 1)
                    f0 += 1
                    v.wait_ge(smf, f0)
                    v.tensor_mul(xnsl(kc)[:, tt * 512:tt * 512 + w], st[:, 0:w],
                                 rstd[:, tt * 512:tt * 512 + w])
            c.semc["fv"] = f0


def emit_l1ffn(c, dx3, dW, dout_x4):
    nc = c.nc
    TH = 1024
    ones_bf = c.sb("l4_ones", [128, 128], BF16)
    s1 = c.sb("l4_s1", [128, 32]); s2 = c.sb("l4_s2", [128, 32])
    h1s = c.sb("l4_h1s", [128, 32]); h2s = c.sb("l4_h2s", [128, 32])
    w3s = c.sb("l4_w3s", [128, 8])
    loadblk(c, [(ones_bf[:], dW["ones_bf"].ap()),
                (s1[:], dW["l1f_s1"].ap()), (s2[:], dW["l1f_s2"].ap()),
                (h1s[:], dW["l1f_h1s"].ap()), (h2s[:], dW["l1f_h2s"].ap()),
                (w3s[:], dW["l1f_w3s"].ap())])

    xn = c.sb("l4_xn", [128, 8 * TH], BF16)
    hh = c.sb("l4_h", [128, 32 * TH], BF16)
    h2 = c.sb("l4_h2", [128, 32 * TH], BF16)
    P = get_pool(c)
    S = dict(sq=c.sb("l4_sq", [128, 8 * TH], BF16),
             ps_a=P["psa"], ps_b=P["psb"],
             mu=c.sb("l4_mu", [128, TH]), va=c.sb("l4_va", [128, TH]),
             rstd=c.sb("l4_rstd", [128, TH]), st=None)
    wb1 = [c.sb(f"l4_wb1_{i}", [128, 8 * 128], BF16) for i in range(2)]
    wb2 = [c.sb(f"l4_wb2_{i}", [128, 8 * 128], BF16) for i in range(2)]
    wbig = [c.sb(f"l4_wbg{i}", [128, 32 * 128], BF16) for i in range(2)]
    pp = P["pp"]
    atmp = [c.sb(f"l4_at{i}", [128, 512], BF16) for i in range(2)]
    ost = [c.sb(f"l4_ost{i}", [128, 512]) for i in range(2)]
    xt = [c.sb(f"l4_xt{i}", [128, 512]) for i in range(2)]
    S["st"] = ost[0]
    sm_mm, sm_ev, sm_as, sm_wl = (c.sem("mm"), c.sem("ev"), c.sem("as"), c.sem("wl"))
    dma = c.sem("dma")

    for half in range(2):
        t0 = half * TH
        loadblk(c, [(xn[:].rearrange("p (kc t) -> p kc t", kc=8), fview(dx3, t0, TH))])
        emit_ln(c, lambda kc: xn[:, kc * TH:(kc + 1) * TH], ones_bf, S, TH)

        mm0, ev0, as0, wl0 = (c.semc[s] for s in ("mm", "ev", "as", "wl"))
        with nc.Block() as blk:
            @blk.gpsimd
            def _(g):
                for o in range(32):
                    if o >= 2:
                        g.wait_ge(sm_mm, mm0 + 2 * (o - 1))
                    g.dma_start(wb1[o % 2][:], wslice(dW["l1f_w1T"], o * 128, 128)).then_inc(sm_wl, 16)
                    g.dma_start(wb2[o % 2][:], wslice(dW["l1f_w2T"], o * 128, 128)).then_inc(sm_wl, 16)
            @blk.tensor
            def _(t):
                for o in range(32):
                    t.wait_ge(sm_wl, wl0 + 32 * (o + 1))
                    for tt in range(2):
                        j = o * 2 + tt
                        if j >= 2:
                            t.wait_ge(sm_ev, ev0 + j - 1)
                            t.wait_ge(sm_as, as0 + j - 1)
                        p1, p2 = pp[2 * (j % 2)], pp[2 * (j % 2) + 1]
                        for kc in range(8):
                            t.matmul(p1[:], wb1[o % 2][:, kc * 128:(kc + 1) * 128],
                                     xn[:, kc * TH + tt * 512: kc * TH + (tt + 1) * 512],
                                     start=(kc == 0), stop=(kc == 7))
                        for kc in range(8):
                            mmi = t.matmul(p2[:], wb2[o % 2][:, kc * 128:(kc + 1) * 128],
                                           xn[:, kc * TH + tt * 512: kc * TH + (tt + 1) * 512],
                                           start=(kc == 0), stop=(kc == 7))
                        mmi.then_inc(sm_mm, 1)
            @blk.scalar
            def _(s):
                for o in range(32):
                    for tt in range(2):
                        j = o * 2 + tt
                        s.wait_ge(sm_mm, mm0 + j + 1)
                        if j >= 2:
                            s.wait_ge(sm_ev, ev0 + j - 1)
                        s.activation(atmp[j % 2][:], pp[2 * (j % 2)][:], AF.Silu,
                                     scale=s1[:, o:o + 1]).then_inc(sm_as, 1)
            @blk.vector
            def _(v):
                for o in range(32):
                    for tt in range(2):
                        j = o * 2 + tt
                        v.wait_ge(sm_mm, mm0 + j + 1)
                        v.tensor_scalar_mul(ost[j % 2][:], pp[2 * (j % 2) + 1][:], s2[:, o:o + 1])
                        v.wait_ge(sm_as, as0 + j + 1)
                        v.tensor_mul(hh[:, o * TH + tt * 512: o * TH + (tt + 1) * 512],
                                     atmp[j % 2][:], ost[j % 2][:]).then_inc(sm_ev, 1)
        c.semc["mm"] = mm0 + 64; c.semc["ev"] = ev0 + 64
        c.semc["as"] = as0 + 64; c.semc["wl"] = wl0 + 32 * 32

        for (dwname, src_t, dst_t, scol) in (("l1f_h1T", hh, h2, h1s),
                                             ("l1f_h2T", h2, hh, h2s)):
            mm0, ev0, wl0 = c.semc["mm"], c.semc["ev"], c.semc["wl"]
            dw = dW[dwname]
            with nc.Block() as blk:
                @blk.gpsimd
                def _(g):
                    for o in range(32):
                        if o >= 2:
                            g.wait_ge(sm_mm, mm0 + 2 * (o - 1))
                        g.dma_start(wbig[o % 2][:], wslice(dw, o * 128, 128)).then_inc(sm_wl, 16)
                @blk.tensor
                def _(t):
                    for o in range(32):
                        t.wait_ge(sm_wl, wl0 + 16 * (o + 1))
                        for tt in range(2):
                            j = o * 2 + tt
                            if j >= 2:
                                t.wait_ge(sm_ev, ev0 + j - 1)
                            for kc in range(32):
                                mmi = t.matmul(pp[j % 2][:],
                                               wbig[o % 2][:, kc * 128:(kc + 1) * 128],
                                               src_t[:, kc * TH + tt * 512: kc * TH + (tt + 1) * 512],
                                               start=(kc == 0), stop=(kc == 31))
                            mmi.then_inc(sm_mm, 1)
                @blk.scalar
                def _(s):
                    for o in range(32):
                        for tt in range(2):
                            j = o * 2 + tt
                            s.wait_ge(sm_mm, mm0 + j + 1)
                            s.activation(dst_t[:, o * TH + tt * 512: o * TH + (tt + 1) * 512],
                                         pp[j % 2][:], AF.Gelu,
                                         scale=scol[:, o:o + 1]).then_inc(sm_ev, 1)
            c.semc["mm"] = mm0 + 64; c.semc["ev"] = ev0 + 64
            c.semc["wl"] = wl0 + 16 * 32

        st_sem, st0 = c.sem("st"), c.semc["st"]
        mm0, ev0, wl0, dma0 = c.semc["mm"], c.semc["ev"], c.semc["wl"], c.semc["dma"]
        with nc.Block() as blk:
            @blk.gpsimd
            def _(g):
                for o in range(8):
                    if o >= 2:
                        g.wait_ge(sm_mm, mm0 + 2 * (o - 1))
                    g.dma_start(wbig[o % 2][:], wslice(dW["l1f_w3T"], o * 128, 128)).then_inc(sm_wl, 16)
                    for tt in range(2):
                        j = o * 2 + tt
                        if j >= 2:
                            g.wait_ge(sm_ev, ev0 + j - 1)
                        g.dma_start(xt[j % 2][:],
                                    dx3.ap()[o * 128:(o + 1) * 128,
                                             t0 + tt * 512:t0 + (tt + 1) * 512]).then_inc(dma, 16)
            @blk.tensor
            def _(t):
                for o in range(8):
                    t.wait_ge(sm_wl, wl0 + 16 * (o + 1))
                    for tt in range(2):
                        j = o * 2 + tt
                        if j >= 2:
                            t.wait_ge(sm_ev, ev0 + j - 1)
                        for kc in range(32):
                            mmi = t.matmul(pp[j % 2][:],
                                           wbig[o % 2][:, kc * 128:(kc + 1) * 128],
                                           hh[:, kc * TH + tt * 512: kc * TH + (tt + 1) * 512],
                                           start=(kc == 0), stop=(kc == 31))
                        mmi.then_inc(sm_mm, 1)
            @blk.vector
            def _(v):
                for o in range(8):
                    for tt in range(2):
                        j = o * 2 + tt
                        v.wait_ge(sm_mm, mm0 + j + 1)
                        v.wait_ge(dma, dma0 + 16 * (j + 1))
                        if j >= 2:
                            v.wait_ge(st_sem, st0 + 16 * (j - 1))
                        v.scalar_tensor_tensor(ost[j % 2][:], pp[j % 2][:], w3s[:, o:o + 1],
                                               xt[j % 2][:], ALU.mult, ALU.add).then_inc(sm_ev, 1)
            @blk.sync
            def _(sy):
                for o in range(8):
                    for tt in range(2):
                        j = o * 2 + tt
                        sy.wait_ge(sm_ev, ev0 + j + 1)
                        sy.dma_start(dout_x4.ap()[o * 128:(o + 1) * 128,
                                                  t0 + tt * 512:t0 + (tt + 1) * 512],
                                     ost[j % 2][:]).then_inc(st_sem, 16)
                sy.wait_ge(st_sem, st0 + 16 * 16)
        c.semc["mm"] = mm0 + 16; c.semc["ev"] = ev0 + 16
        c.semc["wl"] = wl0 + 16 * 8; c.semc["dma"] = dma0 + 16 * 16
        c.semc["st"] = st0 + 16 * 16



def bcol(ap_col, n):
    return bass.AP(ap_col.tensor, ap_col.offset, [list(ap_col.ap[0]), [0, n]])


def brep(ap2d, rep, inner):
    return bass.AP(ap2d.tensor, ap2d.offset,
                   [list(ap2d.ap[0]), [0, rep], list(ap2d.ap[1])[:1] + [inner]])


def emit_l0ffn(c, dx1, dW, dout_x2):
    nc = c.nc
    TH = 1024
    ones_bf = c.sb("m_ones", [128, 128], BF16)
    onescol = c.sb("m_onec", [128, 1])
    gw = c.sb("m_gw", [128, 8 * 8])
    oh8 = c.sb("m_oh8", [8, 8 * 128])
    b1c = c.sb("m_b1c", [128, 8 * 16])
    b2sb = c.sb("m_b2", [8, 1024])
    idf = c.sb("m_idf", [128, 128])
    loadblk(c, [(ones_bf[:], dW["ones_bf"].ap()), (onescol[:], dW["onescol"].ap()),
                (gw[:].rearrange("p (kc e) -> p kc e", kc=8),
                 dW["moe_gw"].ap().rearrange("(kc p) e -> p kc e", p=128)),
                (oh8[:], dW["moe_oh8"].ap()), (b1c[:], dW["moe_b1c"].ap()),
                (b2sb[:], dW["moe_b2"].ap()), (idf[:], dW["ident_f32"].ap())])

    xn = c.sb("m_xn", [128, 8 * TH], BF16)
    hsb = c.sb("m_h", [128, 16 * 512], BF16)
    acc = c.sb("m_acc", [128, 8 * TH])
    w1e = [c.sb(f"m_w1e{i}", [128, 8 * 2048], BF16) for i in range(2)]
    w2e = c.sb("m_w2e", [128, 16 * 1024], BF16)
    cwb = c.sb("m_cwb", [128, TH], BF16)
    cwT = c.sb("m_cwT", [8, TH])
    cw_all = c.sb("m_cwall", [128, 8 * 8])
    negcs = c.sb("m_negcs", [1, 8])
    ga = c.sb("m_ga", [128, 64]); gb = c.sb("m_gb", [128, 64])
    gsc = c.sb("m_gsc", [128, 8]); gsm = c.sb("m_gsm", [128, 8])
    gss = c.sb("m_gss", [128, 8])

    gx = [c.sb(f"m_gx{i}", [128, 8 * 128]) for i in range(2)]
    P = get_pool(c)
    S = dict(sq=acc[:, 3072:7168].bitcast(BF16),
             ps_a=P["psa"], ps_b=P["psb"],
             mu=acc[:, 0:1024], va=acc[:, 1024:2048],
             rstd=acc[:, 2048:3072], st=None)
    xt = [c.sb(f"m_xt{i}", [128, 512]) for i in range(2)]
    ost = [c.sb(f"m_ost{i}", [128, 512]) for i in range(2)]
    S["st"] = ost[0]
    pp = P["pp"]
    psg = S["ps_a"]
    sm_mm, sm_ev, sm_as, sm_wl = (c.sem("mm"), c.sem("ev"), c.sem("as"), c.sem("wl"))
    sm_v = c.sem("vv")
    dma = c.sem("dma")
    st_sem = c.sem("st")

    with nc.Block() as blk:
        @blk.tensor
        def _(t):
            for kc in range(8):
                mmi = t.matmul(psg[0:1, 0:8], onescol[:], gw[:, kc * 8:(kc + 1) * 8],
                               start=(kc == 0), stop=(kc == 7))
            mmi.then_inc(sm_mm, 1)
        @blk.vector
        def _(v):
            v.wait_ge(sm_mm, c.semc["mm"] + 1)
            v.tensor_single_scalar(negcs[:], psg[0:1, 0:8], -1.0, ALU.mult)
    c.semc["mm"] += 1

    for half in range(2):
        t0 = half * TH

        loadblk(c, [(xn[:].rearrange("p (kc t) -> p kc t", kc=8), fview(dx1, t0, TH))])
        emit_ln(c, lambda kc: xn[:, kc * TH:(kc + 1) * TH], ones_bf, S, TH)

        mm0, ev0, dma0 = (c.semc[s] for s in ("mm", "ev", "dma"))
        mu, rstd = S["mu"], S["rstd"]
        with nc.Block() as blk:
            @blk.gpsimd
            def _(g):
                for ti in range(8):
                    if ti >= 2:
                        g.wait_ge(sm_mm, mm0 + ti - 1)
                    g.dma_start(gx[ti % 2][:].rearrange("p (kc t) -> p kc t", kc=8),
                                dx1.ap()[:, t0 + ti * 128:t0 + (ti + 1) * 128]
                                .rearrange("(kc p) t -> p kc t", p=128)).then_inc(dma, 16)
            @blk.tensor
            def _(t):
                for ti in range(8):
                    t.wait_ge(dma, dma0 + 16 * (ti + 1))
                    po = ti * 16
                    for kc in range(8):
                        t.matmul(psg[:, po:po + 8], gx[ti % 2][:, kc * 128:(kc + 1) * 128],
                                 gw[:, kc * 8:(kc + 1) * 8], start=(kc == 0), stop=False)
                    t.matmul(psg[:, po:po + 8], mu[0:1, ti * 128:(ti + 1) * 128],
                             negcs[:], start=False, stop=True)
                    t.matmul(psg[:, po + 8:po + 9], rstd[0:1, ti * 128:(ti + 1) * 128],
                             onescol[0:1, 0:1], start=True, stop=True).then_inc(sm_mm, 1)
        c.semc["mm"] = mm0 + 8; c.semc["dma"] = dma0 + 16 * 8
        smf = c.sem("fv"); f0 = c.semc["fv"]
        with nc.Block() as blk:
            @blk.vector
            def _(v):
                fc = f0
                for ti in range(8):
                    ins = v.tensor_copy(gsc[:, ti:ti + 1], psg[:, ti * 16 + 8:ti * 16 + 9])
                ins.then_inc(smf, 1); fc += 1
                v.wait_ge(smf, fc)
                for ti in range(8):
                    ins = v.tensor_tensor(ga[:, ti * 8:(ti + 1) * 8], psg[:, ti * 16:ti * 16 + 8],
                                          bcol(gsc[:, ti:ti + 1], 8), ALU.mult)
                for ti in range(8):
                    ins = v.reduce_max(gsm[:, ti:ti + 1], ga[:, ti * 8:(ti + 1) * 8], axis=AX)
                ins.then_inc(smf, 1); fc += 1
                v.wait_ge(smf, fc)
                for ti in range(8):
                    v.tensor_tensor(ga[:, ti * 8:(ti + 1) * 8], ga[:, ti * 8:(ti + 1) * 8],
                                    bcol(gsm[:, ti:ti + 1], 8), ALU.subtract)
        c.semc["fv"] = f0 + 2
        with nc.Block() as blk:
            @blk.scalar
            def _(s):
                for ti in range(8):
                    s.activation(gb[:, ti * 8:(ti + 1) * 8], ga[:, ti * 8:(ti + 1) * 8],
                                 AF.Exp)
        f0 = c.semc["fv"]
        with nc.Block() as blk:
            @blk.vector
            def _(v):
                fc = f0
                def fence(ins):
                    nonlocal fc
                    ins.then_inc(smf, 1); fc += 1
                    v.wait_ge(smf, fc)
                for ti in range(8):
                    ins = v.reduce_sum(gss[:, ti:ti + 1], gb[:, ti * 8:(ti + 1) * 8], axis=AX)
                for ti in range(8):
                    ins = v.reciprocal(gsc[:, ti:ti + 1], gss[:, ti:ti + 1])
                fence(ins)
                for ti in range(8):
                    sl = slice(ti * 8, (ti + 1) * 8)
                    v.tensor_tensor(ga[:, sl], gb[:, sl], bcol(gsc[:, ti:ti + 1], 8),
                                    ALU.mult)
                for ti in range(8):
                    ins = v.reduce_max(gsm[:, ti:ti + 1], ga[:, ti * 8:(ti + 1) * 8], axis=AX)
                fence(ins)
                for ti in range(8):
                    sl = slice(ti * 8, (ti + 1) * 8)
                    v.tensor_tensor(gb[:, sl], ga[:, sl], bcol(gsm[:, ti:ti + 1], 8),
                                    ALU.is_ge)
                for ti in range(8):
                    sl = slice(ti * 8, (ti + 1) * 8)
                    v.scalar_tensor_tensor(gb[:, sl], gb[:, sl], -1e9, ga[:, sl],
                                           ALU.mult, ALU.add)
                for ti in range(8):
                    ins = v.reduce_max(gsm[:, ti:ti + 1], gb[:, ti * 8:(ti + 1) * 8], axis=AX)
                fence(ins)
                for ti in range(8):
                    sl = slice(ti * 8, (ti + 1) * 8)
                    v.tensor_tensor(gb[:, sl], ga[:, sl], bcol(gsm[:, ti:ti + 1], 8),
                                    ALU.is_ge)
                for ti in range(8):
                    sl = slice(ti * 8, (ti + 1) * 8)
                    v.tensor_mul(gb[:, sl], ga[:, sl], gb[:, sl])
                for ti in range(8):
                    v.reduce_sum(gss[:, ti:ti + 1], gb[:, ti * 8:(ti + 1) * 8], axis=AX)
                for ti in range(8):
                    ins = v.tensor_single_scalar(gss[:, ti:ti + 1], gss[:, ti:ti + 1],
                                                 1e-8, ALU.add)
                for ti in range(8):
                    ins = v.reciprocal(gsc[:, ti:ti + 1], gss[:, ti:ti + 1])
                fence(ins)
                for ti in range(8):
                    sl = slice(ti * 8, (ti + 1) * 8)
                    v.tensor_tensor(cw_all[:, sl], gb[:, sl],
                                    bcol(gsc[:, ti:ti + 1], 8), ALU.mult)
        c.semc["fv"] = fc if False else c.semc["fv"] + 4
        mm0, ev0 = c.semc["mm"], c.semc["ev"]
        with nc.Block() as blk:
            @blk.tensor
            def _(t):
                for ti in range(8):
                    if ti >= 2:
                        t.wait_ge(sm_ev, ev0 + ti - 1)
                    t.transpose(pp[ti % 2][0:8, 0:128], cw_all[:, ti * 8:(ti + 1) * 8],
                                idf[:]).then_inc(sm_mm, 1)
            @blk.vector
            def _(v):
                for ti in range(8):
                    v.wait_ge(sm_mm, mm0 + ti + 1)
                    v.tensor_copy(cwT[:, ti * 128:(ti + 1) * 128],
                                  pp[ti % 2][0:8, 0:128]).then_inc(sm_ev, 1)
        c.semc["mm"] = mm0 + 8; c.semc["ev"] = ev0 + 8

        loadblk(c, [(w1e[0][:].rearrange("p (kc o) -> p kc o", kc=8),
                     dW["moe_W1f"].ap()[0:1024, :].rearrange("(kc p) o -> p kc o", p=128)),
                    (w2e[:].rearrange("p (kc o) -> p kc o", kc=16),
                     dW["moe_W2f"].ap()[0:2048, :].rearrange("(kc p) o -> p kc o", p=128))])
        for e in range(8):
            mm0, ev0 = c.semc["mm"], c.semc["ev"]
            with nc.Block() as blk:
                @blk.tensor
                def _(t):
                    for tt in range(2):
                        t.matmul(pp[tt][:], oh8[:, e * 128:(e + 1) * 128],
                                 cwT[:, tt * 512:(tt + 1) * 512],
                                 start=True, stop=True).then_inc(sm_mm, 1)
                @blk.vector
                def _(v):
                    for tt in range(2):
                        v.wait_ge(sm_mm, mm0 + tt + 1)
                        v.tensor_copy(cwb[:, tt * 512:(tt + 1) * 512], pp[tt][:])
            c.semc["mm"] = mm0 + 2; c.semc["ev"] = ev0

            mm0, ev0, as0, v0, wl0 = (c.semc[s] for s in ("mm", "ev", "as", "vv", "wl"))
            with nc.Block() as blk:
                @blk.gpsimd
                def _(g):
                    if e < 7:
                        g.dma_start(w1e[(e + 1) % 2][:].rearrange("p (kc o) -> p kc o", kc=8),
                                    dW["moe_W1f"].ap()[(e + 1) * 1024:(e + 2) * 1024, :]
                                    .rearrange("(kc p) o -> p kc o", p=128)).then_inc(sm_wl, 16)
                        g.wait_ge(sm_mm, mm0 + 48)
                        g.dma_start(w2e[:].rearrange("p (kc o) -> p kc o", kc=16),
                                    dW["moe_W2f"].ap()[(e + 1) * 2048:(e + 2) * 2048, :]
                                    .rearrange("(kc p) o -> p kc o", p=128)).then_inc(sm_wl, 16)
                @blk.tensor
                def _(t):
                    if e > 0:
                        t.wait_ge(sm_wl, wl0 - 16)
                    for tt in range(2):
                        for ho in range(16):
                            jh = tt * 16 + ho
                            if jh >= 2:
                                t.wait_ge(sm_as, as0 + jh - 1)
                            for kc in range(8):
                                mmi = t.matmul(pp[2 + jh % 2][:],
                                               w1e[e % 2][:, kc * 2048 + ho * 128: kc * 2048 + (ho + 1) * 128],
                                               xn[:, kc * TH + tt * 512: kc * TH + (tt + 1) * 512],
                                               start=(kc == 0), stop=(kc == 7))
                            mmi.then_inc(sm_mm, 1)
                        t.wait_ge(sm_v, v0 + tt + 1)
                        if tt == 0 and e > 0:
                            t.wait_ge(sm_wl, wl0)
                        for oo in range(8):
                            jw = tt * 8 + oo
                            if jw >= 2:
                                t.wait_ge(sm_ev, ev0 + jw - 1)
                            for kc in range(16):
                                mmi = t.matmul(pp[jw % 2][:],
                                               w2e[:, kc * 1024 + oo * 128: kc * 1024 + (oo + 1) * 128],
                                               hsb[:, kc * 512:(kc + 1) * 512],
                                               start=(kc == 0), stop=(kc == 15),
                                               skip_group_check=True)
                            mmi.then_inc(sm_mm, 1)
                @blk.scalar
                def _(s):
                    for tt in range(2):
                        for ho in range(16):
                            jh = tt * 16 + ho
                            s.wait_ge(sm_mm, mm0 + tt * 24 + ho + 1)
                            s.activation(hsb[:, ho * 512:(ho + 1) * 512], pp[2 + jh % 2][:],
                                         AF.Relu, bias=b1c[:, e * 16 + ho:e * 16 + ho + 1]
                                         ).then_inc(sm_as, 1)
                @blk.vector
                def _(v):
                    for tt in range(2):
                        v.wait_ge(sm_as, as0 + 16 * (tt + 1))
                        v.tensor_mul(hsb[:], hsb[:],
                                     brep(cwb[:, tt * 512:(tt + 1) * 512], 16, 512)
                                     ).then_inc(sm_v, 1)
                        for oo in range(8):
                            jw = tt * 8 + oo
                            v.wait_ge(sm_mm, mm0 + tt * 24 + 16 + oo + 1)
                            dsl = acc[:, oo * TH + tt * 512: oo * TH + (tt + 1) * 512]
                            if e == 0:
                                v.tensor_copy(dsl, pp[jw % 2][:]).then_inc(sm_ev, 1)
                            else:
                                v.tensor_add(dsl, dsl, pp[jw % 2][:]).then_inc(sm_ev, 1)
            c.semc["mm"] = mm0 + 48; c.semc["ev"] = ev0 + 16
            c.semc["as"] = as0 + 32; c.semc["vv"] = v0 + 2
            if e < 7:
                c.semc["wl"] = c.semc["wl"] + 32

        mm0, ev0 = c.semc["mm"], c.semc["ev"]
        with nc.Block() as blk:
            @blk.tensor
            def _(t):
                for j in range(16):
                    oo, tt = j // 2, j % 2
                    if j >= 2:
                        t.wait_ge(sm_ev, ev0 + j - 1)
                    t.matmul(pp[j % 2][:], b2sb[0:8, oo * 128:(oo + 1) * 128],
                             cwT[:, tt * 512:(tt + 1) * 512],
                             start=True, stop=True).then_inc(sm_mm, 1)
            @blk.vector
            def _(v):
                for j in range(16):
                    oo, tt = j // 2, j % 2
                    v.wait_ge(sm_mm, mm0 + j + 1)
                    dsl = acc[:, oo * TH + tt * 512: oo * TH + (tt + 1) * 512]
                    v.tensor_add(dsl, dsl, pp[j % 2][:]).then_inc(sm_ev, 1)
        c.semc["mm"] = mm0 + 16; c.semc["ev"] = ev0 + 16

        mm0, ev0, dma0, st0 = c.semc["mm"], c.semc["ev"], c.semc["dma"], c.semc["st"]
        with nc.Block() as blk:
            @blk.gpsimd
            def _(g):
                for j in range(16):
                    oo, tt = j // 2, j % 2
                    if j >= 2:
                        g.wait_ge(sm_ev, ev0 + j - 1)
                    g.dma_start(xt[j % 2][:],
                                dx1.ap()[oo * 128:(oo + 1) * 128,
                                         t0 + tt * 512:t0 + (tt + 1) * 512]).then_inc(dma, 16)
            @blk.vector
            def _(v):
                for j in range(16):
                    oo, tt = j // 2, j % 2
                    v.wait_ge(dma, dma0 + 16 * (j + 1))
                    if j >= 2:
                        v.wait_ge(st_sem, st0 + 16 * (j - 1))
                    asl = acc[:, oo * TH + tt * 512: oo * TH + (tt + 1) * 512]
                    v.tensor_add(ost[j % 2][:], asl, xt[j % 2][:])
                    v.tensor_add(ost[j % 2][:], ost[j % 2][:],
                                 xn[:, oo * TH + tt * 512: oo * TH + (tt + 1) * 512]
                                 ).then_inc(sm_ev, 1)
            @blk.sync
            def _(sy):
                for j in range(16):
                    oo, tt = j // 2, j % 2
                    sy.wait_ge(sm_ev, ev0 + j + 1)
                    sy.dma_start(dout_x2.ap()[oo * 128:(oo + 1) * 128,
                                              t0 + tt * 512:t0 + (tt + 1) * 512],
                                 ost[j % 2][:]).then_inc(st_sem, 16)
                sy.wait_ge(st_sem, st0 + 16 * 16)
        c.semc["ev"] = ev0 + 16; c.semc["dma"] = dma0 + 16 * 16
        c.semc["st"] = st0 + 16 * 16
    return dict(xn=xn, acc=acc, cw_all=cw_all, cwT=cwT, hsb=hsb,
                mu=S["mu"], rstd=S["rstd"], negcs=negcs, cwb=cwb,
                ga=ga, gb=gb, gsc=gsc, gsm=gsm, gss=gss)


def dumpblk(c, pairs):
    loadblk(c, pairs)



def emit_l1attn(c, dx2, dxh63, dW, dout_x3):
    nc = c.nc
    TF = 2176
    ones_bf = c.sb("a_ones", [128, 128], BF16)
    onesrow_bf = c.sb("a_onesr", [1, 128], BF16)
    bq = c.sb("a_bq", [128, 8]); bk = c.sb("a_bk", [128, 8])
    bout = c.sb("a_bout", [128, 8])
    bv = c.sb("a_bv", [1, 1024], BF16)
    maskR = c.sb("a_maskR", [128, 256]); mask0 = c.sb("a_mask0", [128, 256])
    idb = c.sb("a_idb", [128, 128], BF16)
    wv = c.sb("a_wv", [128, 8 * 1024], BF16)
    wout = c.sb("a_wout", [128, 8 * 1024], BF16)
    loadblk(c, [(ones_bf[:], dW["ones_bf"].ap()), (onesrow_bf[:], dW["onesrow_bf"].ap()),
                (bq[:], dW["l1a_bq"].ap()), (bk[:], dW["l1a_bk"].ap()),
                (bout[:], dW["l1a_bout"].ap()),
                (bv[:], dW["l1a_bv"].ap()),
                (maskR[:], dW["l1a_maskR"].ap()), (mask0[:], dW["l1a_mask0"].ap()),
                (idb[:], dW["ident_bf"].ap()),
                (wv[:].rearrange("p (kc o) -> p kc o", kc=8),
                 dW["l1a_wvT"].ap().rearrange("(kc p) o -> p kc o", p=128)),
                (wout[:].rearrange("p (kc o) -> p kc o", kc=8),
                 dW["l1a_woutT"].ap().rearrange("(kc p) o -> p kc o", p=128))])

    xn2 = c.sb("a_xn", [128, 8 * TF], BF16)
    q_sb = c.sb("a_q", [128, 8 * 2048], BF16)
    k_sb = c.sb("a_k", [128, 8 * TF], BF16)
    v_sb = c.sb("a_v", [128, 17 * 1024], BF16)
    o_sb = c.sb("a_o", [128, 8 * 128], BF16)
    x2t = c.sb("a_x2t", [128, 8 * 128])
    ssb_all = c.sb("a_ssba", [128, 16 * 256])

    ost = [c.sb(f"a_ost{i}", [128, 128]) for i in range(2)]
    wbq = [c.sb(f"a_wbq{i}", [128, 8 * 128], BF16) for i in range(2)]
    P = get_pool(c)
    S = dict(sq=q_sb,
             ps_a=P["psa"], ps_b=P["psb"],
             mu=ssb_all[:, 0:1024], va=ssb_all[:, 1024:2048],
             rstd=ssb_all[:, 2048:3072], st=None)
    S["st"] = x2t
    pbf = P["pbf"]
    pp = P["pp"]
    sm_mm, sm_ev, sm_as, sm_wl = (c.sem("mm"), c.sem("ev"), c.sem("as"), c.sem("wl"))
    sm_v = c.sem("vv"); smf = c.sem("fv")
    dma = c.sem("dma"); st_sem = c.sem("st")

    with nc.Block() as blk:
        @blk.gpsimd
        def _(g):
            g.memset(xn2[:].rearrange("p (kc t) -> p kc t", kc=8)[:, :, 0:65], 0.0)
    loadblk(c, [(xn2[:].rearrange("p (kc t) -> p kc t", kc=8)[:, :, 65:128],
                 dxh63.rearrange("(kc p) t -> p kc t", p=128)),
                (xn2[:].rearrange("p (kc t) -> p kc t", kc=8)[:, :, 128:TF],
                 fview(dx2, 0, 2048))])
    for (off, Tt) in ((0, 1024), (1024, 1024), (2048, 128)):
        emit_ln(c, lambda kc, off=off, Tt=Tt: xn2[:, kc * TF + off: kc * TF + off + Tt],
                ones_bf, S, Tt)

    for (wname, dst, bias, Tt, toff) in (("l1a_wqT", q_sb, bq, 2048, 128),
                                         ("l1a_wkT", k_sb, bk, TF, 0)):
        mm0, ev0, wl0 = c.semc["mm"], c.semc["ev"], c.semc["wl"]
        ntt = (Tt + 511) // 512
        with nc.Block() as blk:
            @blk.gpsimd
            def _(g):
                for o in range(8):
                    if o >= 2:
                        g.wait_ge(sm_mm, mm0 + ntt * (o - 1))
                    g.dma_start(wbq[o % 2][:], wslice(dW[wname], o * 128, 128)).then_inc(sm_wl, 16)
            @blk.tensor
            def _(t):
                for o in range(8):
                    t.wait_ge(sm_wl, wl0 + 16 * (o + 1))
                    for tt in range(ntt):
                        w = min(512, Tt - tt * 512)
                        j = o * ntt + tt
                        if j >= 2:
                            t.wait_ge(sm_ev, ev0 + j - 1)
                        for kc in range(8):
                            mmi = t.matmul(pp[j % 2][:, 0:w],
                                           wbq[o % 2][:, kc * 128:(kc + 1) * 128],
                                           xn2[:, kc * TF + toff + tt * 512: kc * TF + toff + tt * 512 + w],
                                           start=(kc == 0), stop=(kc == 7))
                        mmi.then_inc(sm_mm, 1)
            @blk.vector
            def _(v):
                for o in range(8):
                    for tt in range(ntt):
                        w = min(512, Tt - tt * 512)
                        j = o * ntt + tt
                        v.wait_ge(sm_mm, mm0 + j + 1)
                        v.tensor_scalar(dst[:, o * Tt + tt * 512: o * Tt + tt * 512 + w],
                                        pp[j % 2][:, 0:w], bias[:, o:o + 1], None,
                                        ALU.add).then_inc(sm_ev, 1)
        c.semc["mm"] = mm0 + 8 * ntt; c.semc["ev"] = ev0 + 8 * ntt
        c.semc["wl"] = wl0 + 16 * 8

    mm0, ev0 = c.semc["mm"], c.semc["ev"]
    with nc.Block() as blk:
        @blk.tensor
        def _(t):
            for ti in range(17):
                for oh in range(2):
                    j = ti * 2 + oh
                    if j >= 2:
                        t.wait_ge(sm_ev, ev0 + j - 1)
                    for kc in range(8):
                        t.matmul(pp[j % 2][:],
                                 xn2[:, kc * TF + ti * 128:kc * TF + (ti + 1) * 128],
                                 wv[:, kc * 1024 + oh * 512: kc * 1024 + (oh + 1) * 512],
                                 start=(kc == 0), stop=False)
                    mmi = t.matmul(pp[j % 2][:], onesrow_bf[:],
                                   bv[:, oh * 512:(oh + 1) * 512], start=False, stop=True)
                    mmi.then_inc(sm_mm, 1)
        @blk.vector
        def _(v):
            for ti in range(17):
                for oh in range(2):
                    j = ti * 2 + oh
                    v.wait_ge(sm_mm, mm0 + j + 1)
                    v.tensor_copy(v_sb[:, ti * 1024 + oh * 512: ti * 1024 + (oh + 1) * 512],
                                  pp[j % 2][:]).then_inc(sm_ev, 1)
    c.semc["mm"] = mm0 + 34; c.semc["ev"] = ev0 + 34

    SCALE = 0.125
    P_all = c.sb("a_Pall", [128, 16 * 256], BF16)
    PT_all = ssb_all[:].bitcast(BF16)[:, 0:4096]
    mc = c.sb("a_mc", [128, 48])

    def s_slot(h):
        ps = S["ps_a"] if h % 4 < 2 else S["ps_b"]
        return ps[:, (h % 2) * 512 + ((h // 4) % 2) * 256:
                  (h % 2) * 512 + ((h // 4) % 2) * 256 + 256]

    def t_slot(j):
        g = j // 4
        return pbf[:, (g % 2) * 1024 + (j % 4) * 128:(g % 2) * 1024 + (j % 4) * 128 + 128]

    def o_slot(h):
        ps = S["ps_a"] if h % 4 < 2 else S["ps_b"]
        return ps[0:64, (h % 2) * 512 + (h // 4) * 128:(h % 2) * 512 + (h // 4) * 128 + 128]

    for qb in range(16):
        mm0, ev0, as0, v0, dma0, st0 = (c.semc[s] for s in
                                        ("mm", "ev", "as", "vv", "dma", "st"))
        mask = mask0 if qb == 0 else maskR
        with nc.Block() as blk:
            @blk.gpsimd
            def _(g):
                g.dma_start(x2t[:].rearrange("p (kc t) -> p kc t", kc=8),
                            dx2.ap()[:, qb * 128:(qb + 1) * 128]
                            .rearrange("(kc p) t -> p kc t", p=128)).then_inc(dma, 16)
            @blk.tensor
            def _(t):
                for h in range(16):
                    kc, pb = h // 2, (h % 2) * 64
                    if h >= 4:
                        t.wait_ge(sm_ev, ev0 + h - 3)
                    t.matmul(s_slot(h),
                             q_sb[pb:pb + 64, kc * 2048 + qb * 128: kc * 2048 + (qb + 1) * 128],
                             k_sb[pb:pb + 64, kc * TF + qb * 128: kc * TF + qb * 128 + 256],
                             start=True, stop=True).then_inc(sm_mm, 1)
                t.wait_ge(sm_v, v0 + 2)
                for j in range(32):
                    h, ki = j // 2, j % 2
                    if j % 4 == 0 and j >= 8:
                        t.wait_ge(sm_ev, ev0 + 16 + 4 * (j // 4 - 1))
                    ins = t.transpose(t_slot(j),
                                      P_all[:, h * 256 + ki * 128: h * 256 + ki * 128 + 128],
                                      idb[:])
                    if j % 4 == 3:
                        ins.then_inc(sm_mm, 1)
                for h in range(16):
                    if h == 0:
                        t.wait_ge(sm_ev, ev0 + 48)
                    if h >= 4:
                        t.wait_ge(sm_ev, ev0 + 48 + h - 3)
                    for ki in range(2):
                        mmi = t.matmul(o_slot(h),
                                       v_sb[:, (qb + ki) * 1024 + h * 64: (qb + ki) * 1024 + (h + 1) * 64],
                                       PT_all[:, h * 256 + ki * 128: h * 256 + ki * 128 + 128],
                                       start=(ki == 0), stop=(ki == 1))
                    mmi.then_inc(sm_mm, 1)
                for o in range(8):
                    t.wait_ge(sm_ev, ev0 + 64 + (o - 1 if o >= 2 else 0))
                    for kc in range(8):
                        mmi = t.matmul(pp[o % 2][:, 0:128],
                                       wout[:, kc * 1024 + o * 128: kc * 1024 + (o + 1) * 128],
                                       o_sb[:, kc * 128:(kc + 1) * 128],
                                       start=(kc == 0), stop=(kc == 7))
                    mmi.then_inc(sm_mm, 1)
            @blk.vector
            def _(v):
                for h in range(16):
                    v.wait_ge(sm_mm, mm0 + h + 1)
                    v.scalar_tensor_tensor(ssb_all[:, h * 256:(h + 1) * 256], s_slot(h),
                                           SCALE, mask[:], ALU.mult,
                                           ALU.add).then_inc(sm_ev, 1)
                for h in range(16):
                    v.reduce_max(mc[:, h:h + 1], ssb_all[:, h * 256:(h + 1) * 256], axis=AX)
                for h in range(16):
                    ins = v.tensor_single_scalar(mc[:, 16 + h:17 + h], mc[:, h:h + 1],
                                                 -1.0, ALU.mult)
                ins.then_inc(sm_v, 1)
                v.wait_ge(sm_as, as0 + 1)
                for h in range(16):
                    v.reduce_sum(mc[:, 32 + h:33 + h], P_all[:, h * 256:(h + 1) * 256], axis=AX)
                for h in range(16):
                    v.reciprocal(mc[:, h:h + 1], mc[:, 32 + h:33 + h])
                for h in range(16):
                    ins = v.tensor_tensor(P_all[:, h * 256:(h + 1) * 256],
                                          P_all[:, h * 256:(h + 1) * 256],
                                          bcol(mc[:, h:h + 1], 256), ALU.mult)
                ins.then_inc(sm_v, 1)
                for j in range(32):
                    h, ki = j // 2, j % 2
                    v.wait_ge(sm_mm, mm0 + 16 + j // 4 + 1)
                    v.tensor_copy(PT_all[:, h * 256 + ki * 128: h * 256 + ki * 128 + 128],
                                  t_slot(j)).then_inc(sm_ev, 1)
                for h in range(16):
                    v.wait_ge(sm_mm, mm0 + 24 + h + 1)
                    v.tensor_copy(o_sb[(h % 2) * 64:(h % 2) * 64 + 64,
                                       (h // 2) * 128:(h // 2) * 128 + 128],
                                  o_slot(h)).then_inc(sm_ev, 1)
                for o in range(8):
                    v.wait_ge(sm_mm, mm0 + 40 + o + 1)
                    v.wait_ge(dma, dma0 + 16)
                    if o >= 2:
                        v.wait_ge(st_sem, st0 + 16 * (o - 1))
                    v.tensor_scalar(ost[o % 2][:], pp[o % 2][:, 0:128], bout[:, o:o + 1],
                                    None, ALU.add)
                    v.tensor_add(ost[o % 2][:], ost[o % 2][:],
                                 x2t[:, o * 128:(o + 1) * 128]).then_inc(sm_ev, 1)
            @blk.scalar
            def _(s):
                s.wait_ge(sm_v, v0 + 1)
                for h in range(16):
                    ins = s.activation(P_all[:, h * 256:(h + 1) * 256],
                                       ssb_all[:, h * 256:(h + 1) * 256], AF.Exp,
                                       bias=mc[:, 16 + h:17 + h])
                ins.then_inc(sm_as, 1)
            @blk.sync
            def _(sy):
                for o in range(8):
                    sy.wait_ge(sm_ev, ev0 + 64 + o + 1)
                    sy.dma_start(dout_x3.ap()[o * 128:(o + 1) * 128, qb * 128:(qb + 1) * 128],
                                 ost[o % 2][:]).then_inc(st_sem, 16)
                sy.wait_ge(st_sem, st0 + 16 * 8)
        c.semc["mm"] = mm0 + 48
        c.semc["ev"] = ev0 + 72
        c.semc["as"] = as0 + 1
        c.semc["vv"] = v0 + 2
        c.semc["dma"] = dma0 + 16
        c.semc["st"] = st0 + 16 * 8


def restride(ap2d, off, step, count):
    return bass.AP(ap2d.tensor, ap2d.offset + off, [list(ap2d.ap[0]), [step, count]])


def emit_l0attn(c, dxin, dS0, dW, dout_x1, dstate, dbg=False, dbg_outs=None,
                front=False, dmask=None, onepass=False, cdl2=None):
    nc = c.nc
    c._spill_wait = None
    TH = 1024
    ones_bf = c.sb("b_ones", [128, 128], BF16)
    onesrow_f = c.sb("b_onesrf", [1, 128])
    idf = c.sb("b_idf", [128, 128])
    idb = c.sb("b_idb", [128, 128], BF16)
    triu = c.sb("b_triu", [128, 128])
    convw = c.sb("b_convw", [128, 72])
    convb = c.sb("b_convb", [128, 18])
    dtb = c.sb("b_dtb", [32, 1])
    Acol = c.sb("b_acol", [32, 1])
    Drep = c.sb("b_drep", [128, 32])
    normw = c.sb("b_normw", [128, 16])
    ones32 = c.sb("b_ones32", [32, 128])
    woutr = c.sb("b_woutr", [128, 16 * 1024], BF16)
    carry = c.sb("b_carry", [128, 2048])
    prevb = c.sb("b_prevb", [128, 2048], BF16)
    da32 = c.sb("b_da32", [32, 128])
    cdl = c.sb("b_cdl", [32, 8])
    cdflat = c.sb("b_cdflat", [1, 256])
    Lm = c.sb("b_Lm", [33, 128])
    TD = c.sb("b_TD", [128, 64])
    ETexp = c.sb("b_ETexp", [128, 32])
    ddT = c.sb("b_ddT", [128, 32])
    GmT = c.sb("b_GmT", [128, 128], BF16)
    BT = c.sb("b_BT", [128, 128], BF16)
    rms1 = c.sb("b_rms1", [128, 4])
    ost = [c.sb(f"b_ost{i}", [128, 512]) for i in range(2)]
    xres = c.sb("b_xres", [128, 8 * 128])
    zst = [c.sb(f"b_zst{i}", [128, 512], BF16) for i in range(2)]
    winb = [c.sb(f"b_winb{i}", [128, 8 * 128], BF16) for i in range(2)]
    P = get_pool(c)
    psa = P["psa"]; psb = P["psb"]; pp = P["pp"]; pbf = P["pbf"]
    dz = c.dtmp("b_dz", [2048, 2048], BF16)
    rets = {"dz": dz}
    if onepass:
        rets["dysp"] = c.dtmp("b_dysp", [16, 128, 2048], BF16)
        rets["dcsp"] = c.dtmp("b_dcsp", [16, 128, 128], BF16)
        rets["desp"] = c.dtmp("b_desp", [16, 128, 32])
        CDL2 = cdl2 if cdl2 is not None else c.sb("b_CDL2", [32, 16])
        rets["CDL2"] = CDL2
    sm_mm, sm_ev, sm_as, sm_wl = (c.sem("mm"), c.sem("ev"), c.sem("as"), c.sem("wl"))
    sm_v = c.sem("vv"); smf = c.sem("fv")
    dma = c.sem("dma"); st_sem = c.sem("st")

    loadblk(c, [(ones_bf[:], dW["ones_bf"].ap()), (onesrow_f[:], dW["onesrow_f"].ap()),
                (idf[:], dW["ident_f32"].ap()), (idb[:], dW["ident_bf"].ap()),
                (triu[:], dW["mb_triu"].ap()), (convw[:], dW["mb_convw"].ap()),
                (convb[:], dW["mb_convb"].ap()), (dtb[:], dW["mb_dtb"].ap()),
                (Acol[:], dW["mb_alog"].ap()), (Drep[:], dW["mb_drep"].ap()),
                (normw[:], dW["mb_normw"].ap()), (ones32[:], dW["ones32"].ap()),
                (woutr[:].rearrange("p (kc o) -> p kc o", kc=16),
                 dW["mb_woutT"].ap().rearrange("(kc p) o -> p kc o", p=128))]
            + ([(carry[:], dS0)] if dS0 is not None else []))
    if dS0 is None:
        with nc.Block() as blk:
            @blk.gpsimd
            def _(g):
                g.memset(carry[:], 0.0)
    if dmask is not None:
        mcol = c.sb("b_mcol", [128, 1])
        loadblk(c, [(mcol[:], dmask.ap())])
        with nc.Block() as blk:
            @blk.vector
            def _(v):
                v.tensor_mul(carry[:], carry[:], bcol(mcol[:], 2048))
    with nc.Block() as blk:
        @blk.scalar
        def _(s):
            s.activation(Acol[:], Acol[:], AF.Exp)
        @blk.gpsimd
        def _(g):
            g.memset(Lm[32:33, 0:128], 1.0)
    with nc.Block() as blk:
        @blk.vector
        def _(v):
            v.tensor_single_scalar(Acol[:], Acol[:], -1.0, ALU.mult)

    for half in range(1 if dbg else 2):
        t0g = half * TH
        c.push_scope()
        x4b = c.sb(f"h_x4b{half}", [128, 16 * 1024], BF16)
        BCh = c.sb(f"h_BC{half}", [128, 2 * 1024], BF16)
        CD2 = c.sb(f"h_CD2{half}", [64, 1024])
        CDB = c.sb(f"h_CDB{half}", [128, 256])

        c.push_scope()
        xnh = c.sb(f"p1_xn{half}", [128, 8 * 1027], BF16)
        xbc = c.sb(f"p1_xbc{half}", [128, 18 * 1027], BF16)
        wzb = [c.sb(f"p1_wzb{half}_{i}", [128, 8 * 512], BF16) for i in range(2)]
        S = dict(sq=xbc, ps_a=psa, ps_b=psb,
                 mu=c.sb(f"p1_mu{half}", [128, 512]), va=c.sb(f"p1_va{half}", [128, 512]),
                 rstd=c.sb(f"p1_rstd{half}", [128, 512]), st=ost[0])
        loadblk(c, [(xnh[:].rearrange("p (kc t) -> p kc t", kc=8),
                     fview(dxin, t0g, 1027))])
        for (off, Tt) in ((0, 512), (512, 512), (1024, 3)):
            emit_ln(c, lambda kc, off=off, Tt=Tt: xnh[:, kc * 1027 + off: kc * 1027 + off + Tt],
                    ones_bf, S, Tt)

        tjobs = ((0, 512), (512, 512), (1024, 3))
        mm0, ev0, wl0 = c.semc["mm"], c.semc["ev"], c.semc["wl"]
        with nc.Block() as blk:
            @blk.gpsimd
            def _(g):
                for o in range(19):
                    if o >= 2:
                        g.wait_ge(sm_mm, mm0 + 3 * (o - 1))
                    ow = 128 if o < 18 else 32
                    g.dma_start(winb[o % 2][:, 0:8 * ow],
                                wslice(dW["mb_winT"], 2048 + o * 128, ow)).then_inc(sm_wl, 16)
            @blk.tensor
            def _(t):
                for o in range(19):
                    ow = 128 if o < 18 else 32
                    t.wait_ge(sm_wl, wl0 + 16 * (o + 1))
                    for tj, (tt0, w) in enumerate(tjobs):
                        j = o * 3 + tj
                        if j >= 2:
                            t.wait_ge(sm_ev, ev0 + j - 1)
                        for kc in range(8):
                            mmi = t.matmul(pp[j % 2][0:ow, 0:w],
                                           winb[o % 2][:, kc * ow:(kc + 1) * ow],
                                           xnh[:, kc * 1027 + tt0: kc * 1027 + tt0 + w],
                                           start=(kc == 0), stop=(kc == 7))
                        mmi.then_inc(sm_mm, 1)
            @blk.vector
            def _(v):
                for o in range(19):
                    for tj, (tt0, w) in enumerate(tjobs):
                        j = o * 3 + tj
                        v.wait_ge(sm_mm, mm0 + j + 1)
                        if o < 18:
                            ins = v.tensor_copy(xbc[:, o * 1027 + tt0: o * 1027 + tt0 + w],
                                                pp[j % 2][:, 0:w])
                        else:
                            lo = max(tt0, 3); hi = tt0 + w
                            ins = v.tensor_copy(CD2[0:32, lo - 3:hi - 3],
                                                pp[j % 2][0:32, lo - tt0:hi - tt0])
                        ins.then_inc(sm_ev, 1)
        c.semc["mm"] = mm0 + 57; c.semc["ev"] = ev0 + 57; c.semc["wl"] = wl0 + 19 * 16

        if not front:
            mm0, ev0, wl0, st0 = (c.semc[s] for s in ("mm", "ev", "wl", "st"))
            with nc.Block() as blk:
                @blk.gpsimd
                def _(g):
                    for oz in range(4):
                        if oz >= 2:
                            g.wait_ge(sm_mm, mm0 + 8 * (oz - 1))
                        g.dma_start(wzb[oz % 2][:].rearrange("p (kc o) -> p kc o", kc=8),
                                    wslice(dW["mb_winT"], oz * 512, 512)).then_inc(sm_wl, 16)
                @blk.tensor
                def _(t):
                    for oz in range(4):
                        t.wait_ge(sm_wl, wl0 + 16 * (oz + 1))
                        for ti in range(8):
                            j = oz * 8 + ti
                            if j >= 2:
                                t.wait_ge(sm_ev, ev0 + j - 1)
                            for kc in range(8):
                                mmi = t.matmul(pp[j % 2][:],
                                               xnh[:, kc * 1027 + 3 + ti * 128: kc * 1027 + 3 + (ti + 1) * 128],
                                               wzb[oz % 2][:, kc * 512:(kc + 1) * 512],
                                               start=(kc == 0), stop=(kc == 7))
                            mmi.then_inc(sm_mm, 1)
                @blk.vector
                def _(v):
                    for oz in range(4):
                        for ti in range(8):
                            j = oz * 8 + ti
                            v.wait_ge(sm_mm, mm0 + j + 1)
                            if j >= 2:
                                v.wait_ge(st_sem, st0 + 16 * (j - 1))
                            v.tensor_copy(zst[j % 2][:], pp[j % 2][:]).then_inc(sm_ev, 1)
                @blk.sync
                def _(sy):
                    for oz in range(4):
                        for ti in range(8):
                            j = oz * 8 + ti
                            sy.wait_ge(sm_ev, ev0 + j + 1)
                            sy.dma_start(dz.ap()[t0g + ti * 128:t0g + (ti + 1) * 128,
                                                 oz * 512:(oz + 1) * 512],
                                         zst[j % 2][:]).then_inc(st_sem, 16)
                    sy.wait_ge(st_sem, st0 + 16 * 32)
            c.semc["mm"] = mm0 + 32; c.semc["ev"] = ev0 + 32
            c.semc["wl"] = wl0 + 16 * 4; c.semc["st"] = st0 + 16 * 32

        v0, as0 = c.semc["vv"], c.semc["as"]
        with nc.Block() as blk:
            @blk.vector
            def _(v):
                for ct in range(18):
                    ps = psa if ct % 2 == 0 else psb
                    if ct >= 2:
                        v.wait_ge(sm_as, as0 + ct - 1)
                    v.tensor_scalar_mul(ps[:], xbc[:, ct * 1027 + 0: ct * 1027 + 1024],
                                        convw[:, ct * 4:ct * 4 + 1])
                    for k in range(1, 4):
                        ins = v.scalar_tensor_tensor(ps[:], xbc[:, ct * 1027 + k: ct * 1027 + k + 1024],
                                                     convw[:, ct * 4 + k:ct * 4 + k + 1],
                                                     ps[:], ALU.mult, ALU.add)
                    ins.then_inc(sm_v, 1)
            @blk.scalar
            def _(s):
                for ct in range(18):
                    ps = psa if ct % 2 == 0 else psb
                    s.wait_ge(sm_v, v0 + ct + 1)
                    if ct == 16 and onepass and getattr(c, "_spill_wait", None):
                        s.wait_ge(dma, c._spill_wait)
                    if ct < 16:
                        dst = x4b[:, ct * 1024:(ct + 1) * 1024]
                    else:
                        dst = BCh[:, (ct - 16) * 1024:(ct - 15) * 1024]
                    s.activation(dst, ps[:], AF.Silu,
                                 bias=convb[:, ct:ct + 1]).then_inc(sm_as, 1)
        c.semc["vv"] = v0 + 18; c.semc["as"] = as0 + 18
        if dbg_outs:
            loadblk(c, [(dbg_outs["xnh"].ap(), xnh[:]), (dbg_outs["xbc"].ap(), xbc[:])])
        c.pop_scope()

        c.push_scope()
        E = c.sb(f"p2_E{half}", [128, 4096], BF16)
        Mp = c.sb(f"p2_Mp{half}", [128, 4096], BF16)
        xT = c.sb(f"p2_xT{half}", [128, 2048], BF16)
        xtf = c.sb(f"p2_xtf{half}", [128, 2048], BF16)
        Rmat = c.sb(f"p2_Rmat{half}", [33, 4096])
        IDall = c.sb(f"p2_IDall{half}", [128, 4096], BF16)
        ytmp = c.sb(f"p2_ytmp{half}", [128, 1024])
        ysb = c.sb(f"p2_ysb{half}", [128, 2048])
        sz = c.sb(f"p2_sz{half}", [128, 2048])
        yn = c.sb(f"p2_yn{half}", [128, 2048], BF16)
        ynT = c.sb(f"p2_ynT{half}", [128, 2048], BF16)
        zch = c.sb(f"p2_zch{half}", [128, 2048], BF16)
        loadblk(c, [(Rmat[0:32, :], dW["mb_oh32"].ap())])
        if not front:
            with nc.Block() as blk:
                @blk.vector
                def _(v):
                    for h in range(32):
                        v.tensor_tensor(IDall[:, h * 128:(h + 1) * 128], idb[:],
                                        bcol(Drep[:, h:h + 1], 128), ALU.mult)
        with nc.Block() as blk:
            @blk.vector
            def _(v):
                v.tensor_scalar(CD2[0:32, :], CD2[0:32, :], dtb[:], None, ALU.add)
        with nc.Block() as blk:
            @blk.scalar
            def _(s):
                s.activation(ytmp[0:32, 0:1024], CD2[0:32, :], AF.Abs)
                s.activation(CD2[32:64, :], ytmp[0:32, 0:1024], AF.Exp, scale=f32(-1.0))
                s.activation(ytmp[0:32, 0:1024], CD2[32:64, :], AF.Ln, bias=f32(1.0))
        with nc.Block() as blk:
            @blk.vector
            def _(v):
                v.tensor_scalar(ysb[0:32, 0:1024], CD2[0:32, :], 0.0, None, ALU.max)
                v.tensor_add(CD2[0:32, :], ysb[0:32, 0:1024], ytmp[0:32, 0:1024])
        with nc.Block() as blk:
            @blk.vector
            def _(v):
                for ci in range(8):
                    v.tensor_scalar_mul(da32[:], CD2[0:32, ci * 128:(ci + 1) * 128], Acol[:])
                    v.tensor_tensor_scan(CD2[32:64, ci * 128:(ci + 1) * 128],
                                         da32[:], ones32[:], 0.0, ALU.add, ALU.mult)
        with nc.Block() as blk:
            @blk.scalar
            def _(s):
                s.activation(cdl[:], restride(CD2[32:64, 0:1024], 127, 128, 8), AF.Exp)
            if onepass:
                @blk.vector
                def _(v):
                    v.tensor_copy(CDL2[:, half * 8:(half + 1) * 8],
                                  restride(CD2[32:64, 0:1024], 127, 128, 8))
        dma0 = c.semc["dma"]
        with nc.Block() as blk:
            @blk.gpsimd
            def _(g):
                g.dma_start(cdflat[:].rearrange("p (a b) -> p a b", a=32), cdl[:]).then_inc(dma, 16)
                g.wait_ge(dma, dma0 + 16)
        c.semc["dma"] = dma0 + 16
        mm0 = c.semc["mm"]
        with nc.Block() as blk:
            @blk.tensor
            def _(t):
                t.matmul(psa[:, 0:256], onesrow_f[:], cdflat[:],
                         start=True, stop=True).then_inc(sm_mm, 1)
            @blk.vector
            def _(v):
                v.wait_ge(sm_mm, mm0 + 1)
                v.tensor_copy(CDB[:], psa[:, 0:256])
        c.semc["mm"] = mm0 + 1

        for ci in range(8):
            g0 = t0g + ci * 128
            csl = slice(ci * 128, (ci + 1) * 128)
            mm0, v0 = c.semc["mm"], c.semc["vv"]
            with nc.Block() as blk:
                @blk.tensor
                def _(t):
                    t.transpose(pp[0][:, 0:64], CD2[:, csl], idf[0:64, 0:64]).then_inc(sm_mm, 1)
                    for b in range(17):
                        if b >= 2:
                            t.wait_ge(sm_v, v0 + b - 1)
                        src = (x4b[:, b * 1024 + csl.start:b * 1024 + csl.stop] if b < 16
                               else BCh[:, csl])
                        t.transpose(pbf[:, (b % 2) * 1024:(b % 2) * 1024 + 128],
                                    src, idb[:]).then_inc(sm_mm, 1)
                @blk.vector
                def _(v):
                    v.wait_ge(sm_mm, mm0 + 1)
                    v.tensor_copy(TD[:], pp[0][:, 0:64])
                    for b in range(17):
                        v.wait_ge(sm_mm, mm0 + b + 2)
                        dst = xT[:, b * 128:(b + 1) * 128] if b < 16 else BT[:]
                        v.tensor_copy(dst, pbf[:, (b % 2) * 1024:(b % 2) * 1024 + 128]
                                      ).then_inc(sm_v, 1)
            c.semc["mm"] = mm0 + 18; c.semc["vv"] = v0 + 17

            dma0 = c.semc["dma"]
            with nc.Block() as blk:
                @blk.vector
                def _(v):
                    v.tensor_single_scalar(Lm[0:32, :], CD2[32:64, csl], -1.0, ALU.mult)
                @blk.gpsimd
                def _(g):
                    g.dma_start(Rmat[32:33, :].rearrange("p (a b) -> p a b", a=32),
                                CD2[32:64, csl]).then_inc(dma, 16)
                    g.wait_ge(dma, dma0 + 16)
                @blk.scalar
                def _(s):
                    if onepass and getattr(c, "_spill_wait", None):
                        s.wait_ge(dma, c._spill_wait)
                    s.activation(ETexp[:], TD[:, 32:64], AF.Exp)
            c.semc["dma"] = dma0 + 16

            mm0, v0, as0 = c.semc["mm"], c.semc["vv"], c.semc["as"]
            with nc.Block() as blk:
                @blk.tensor
                def _(t):
                    for j in range(8):
                        if j >= 4:
                            t.wait_ge(sm_v, v0 + j - 3)
                        ds = (psa if j % 4 < 2 else psb)[:, (j % 2) * 512:(j % 2) * 512 + 512]
                        t.matmul(ds, Lm[:], Rmat[:, j * 512:(j + 1) * 512],
                                 start=True, stop=True).then_inc(sm_mm, 1)
                @blk.vector
                def _(v):
                    for j in range(8):
                        v.wait_ge(sm_mm, mm0 + j + 1)
                        if j >= 2:
                            v.wait_ge(sm_as, as0 + j - 1)
                        ds = (psa if j % 4 < 2 else psb)[:, (j % 2) * 512:(j % 2) * 512 + 512]
                        v.tensor_scalar(ytmp[:, (j % 2) * 512:(j % 2) * 512 + 512], ds,
                                        0.0, None, ALU.min).then_inc(sm_v, 1)
                @blk.scalar
                def _(s):
                    for j in range(8):
                        s.wait_ge(sm_v, v0 + j + 1)
                        s.activation(E[:, j * 512:(j + 1) * 512],
                                     ytmp[:, (j % 2) * 512:(j % 2) * 512 + 512],
                                     AF.Exp).then_inc(sm_as, 1)
            c.semc["mm"] = mm0 + 8; c.semc["vv"] = v0 + 8; c.semc["as"] = as0 + 8

            mm0, v0, f0 = c.semc["mm"], c.semc["vv"], c.semc["fv"]
            with nc.Block() as blk:
                if not front:
                    @blk.tensor
                    def _(t):
                        t.matmul(pp[0][:, 0:128], BCh[:, csl], BCh[:, 1024 + csl.start:1024 + csl.stop],
                                 start=True, stop=True).then_inc(sm_mm, 1)
                @blk.vector
                def _(v):
                    if not front:
                        v.wait_ge(sm_mm, mm0 + 1)
                        v.tensor_tensor(GmT[:], pp[0][:, 0:128], triu[:], ALU.mult)
                        v.tensor_tensor(Mp[:], E[:],
                                        bass.AP(GmT[:].tensor, GmT[:].offset,
                                                [list(GmT[:].ap[0]), [0, 32], [1, 128]]),
                                        ALU.mult)
                        v.tensor_tensor(Mp[:], Mp[:],
                                        bass.AP(TD[:].tensor, TD[:].offset,
                                                [list(TD[:].ap[0]), [1, 32], [0, 128]]),
                                        ALU.mult)
                        v.tensor_tensor(Mp[:], Mp[:], IDall[:], ALU.add)
                    v.tensor_tensor(ddT[:], restride(E[:], 127, 128, 32), TD[:, 0:32],
                                    ALU.mult).then_inc(smf, 1)
                    v.wait_ge(smf, f0 + 1)
                    v.tensor_tensor(xtf[:], xT[:],
                                    bass.AP(ddT[:].tensor, ddT[:].offset,
                                            [list(ddT[:].ap[0]), [1, 32], [0, 64]]),
                                    ALU.mult).then_inc(sm_v, 1)
            c.semc["mm"] = mm0 + (0 if front else 1)
            c.semc["vv"] = v0 + 1; c.semc["fv"] = f0 + 1

            mm0, v0 = c.semc["mm"], c.semc["vv"]
            with nc.Block() as blk:
                @blk.tensor
                def _(t):
                    for j in range(4):
                        ds = (psa if j < 2 else psb)[:, (j % 2) * 512:(j % 2) * 512 + 512]
                        t.matmul(ds, BT[:], xtf[:, j * 512:(j + 1) * 512],
                                 start=True, stop=True).then_inc(sm_mm, 1)
                @blk.vector
                def _(v):
                    v.wait_ge(sm_mm, mm0 + 4)
                    v.tensor_copy(prevb[:], carry[:])
                    v.tensor_tensor(sz[:], carry[:],
                                    bass.AP(CDB[:].tensor, CDB[:].offset + ci,
                                            [list(CDB[:].ap[0]), [8, 32], [0, 64]]),
                                    ALU.mult)
                    v.tensor_add(carry[:, 0:1024], sz[:, 0:1024], psa[:])
                    v.tensor_add(carry[:, 1024:2048], sz[:, 1024:2048], psb[:])
            c.semc["mm"] = mm0 + 4

            if not front:
                mm0, v0 = c.semc["mm"], c.semc["vv"]
                with nc.Block() as blk:
                    @blk.tensor
                    def _(t):
                        for h in range(32):
                            ds = (psa if h < 16 else psb)[:, (h % 16) * 64:(h % 16) * 64 + 64]
                            mmi = t.matmul(ds, Mp[:, h * 128:(h + 1) * 128],
                                           xT[:, h * 64:(h + 1) * 64],
                                           start=True, stop=True)
                        mmi.then_inc(sm_mm, 1)
                        for j in range(2):
                            mmi = t.matmul(pp[j][:], BCh[:, 1024 + csl.start:1024 + csl.stop],
                                           prevb[:, j * 512:(j + 1) * 512],
                                           start=True, stop=True)
                        mmi.then_inc(sm_mm, 1)
                        t.wait_ge(sm_v, v0 + 1)
                        for j in range(2):
                            mmi = t.matmul(pp[j][:], BCh[:, 1024 + csl.start:1024 + csl.stop],
                                           prevb[:, 1024 + j * 512:1024 + (j + 1) * 512],
                                           start=True, stop=True)
                        mmi.then_inc(sm_mm, 1)
                    @blk.vector
                    def _(v):
                        if onepass and getattr(c, "_spill_wait", None):
                            v.wait_ge(dma, c._spill_wait)
                        v.wait_ge(sm_mm, mm0 + 2)
                        for j in range(2):
                            v.tensor_tensor(ytmp[:, j * 512:(j + 1) * 512], pp[j][:],
                                            bass.AP(ETexp[:].tensor, ETexp[:].offset + j * 8,
                                                    [list(ETexp[:].ap[0]), [1, 8], [0, 64]]),
                                            ALU.mult)
                        v.tensor_add(ysb[:, 0:1024], ytmp[:], psa[:]).then_inc(sm_v, 1)
                        v.wait_ge(sm_mm, mm0 + 3)
                        for j in range(2):
                            v.tensor_tensor(ytmp[:, j * 512:(j + 1) * 512], pp[j][:],
                                            bass.AP(ETexp[:].tensor, ETexp[:].offset + 16 + j * 8,
                                                    [list(ETexp[:].ap[0]), [1, 8], [0, 64]]),
                                            ALU.mult)
                        v.tensor_add(ysb[:, 1024:2048], ytmp[:], psb[:])
                c.semc["mm"] = mm0 + 3; c.semc["vv"] = v0 + 1

            if onepass:
                g = half * 8 + ci
                dma0s = c.semc["dma"]
                with nc.Block() as blk:
                    @blk.gpsimd
                    def _(gg):
                        gg.dma_start(rets["dysp"].ap()[g], ysb[:]).then_inc(dma, 16)
                        gg.dma_start(rets["dcsp"].ap()[g],
                                     BCh[:, 1024 + csl.start:1024 + csl.stop]).then_inc(dma, 16)
                        gg.dma_start(rets["desp"].ap()[g], ETexp[:]).then_inc(dma, 16)
                c.semc["dma"] = dma0s + 48
                c._spill_wait = c.semc["dma"]

            if not front and not onepass:
                dma0, as0, v0, f0 = (c.semc[s] for s in ("dma", "as", "vv", "fv"))
                with nc.Block() as blk:
                    @blk.gpsimd
                    def _(g):
                        g.dma_start(zch[:], dz.ap()[g0:g0 + 128, :]).then_inc(dma, 16)
                    @blk.scalar
                    def _(s):
                        s.wait_ge(dma, dma0 + 16)
                        s.activation(sz[:], zch[:], AF.Silu).then_inc(sm_as, 1)
                        s.wait_ge(sm_v, v0 + 1)
                        s.activation(rms1[:, 2:3], rms1[:, 1:2], AF.Ln, bias=EPS)
                        s.activation(rms1[:, 3:4], rms1[:, 2:3], AF.Exp,
                                     scale=f32(-0.5)).then_inc(sm_as, 1)
                    @blk.vector
                    def _(v):
                        v.wait_ge(sm_as, as0 + 1)
                        v.tensor_mul(ysb[:], ysb[:], sz[:])
                        v.tensor_mul(sz[:], ysb[:], ysb[:])
                        ins = v.reduce_sum(rms1[:, 0:1], sz[:], axis=AX)
                        ins.then_inc(smf, 1)
                        v.wait_ge(smf, f0 + 1)
                        v.tensor_single_scalar(rms1[:, 1:2], rms1[:, 0:1],
                                               1.0 / 2048.0, ALU.mult).then_inc(sm_v, 1)
                        v.wait_ge(sm_as, as0 + 2)
                        v.tensor_tensor(yn[:], ysb[:], bcol(rms1[:, 3:4], 2048), ALU.mult)
                        v.nop().then_inc(sm_v, 1)
                c.semc["dma"] = dma0 + 16; c.semc["as"] = as0 + 2
                c.semc["vv"] = v0 + 2; c.semc["fv"] = f0 + 1

                mm0, v0 = c.semc["mm"], c.semc["vv"]
                with nc.Block() as blk:
                    @blk.tensor
                    def _(t):
                        for b in range(16):
                            if b >= 2:
                                t.wait_ge(sm_v, v0 + b - 1)
                            t.transpose(pbf[:, (b % 2) * 1024:(b % 2) * 1024 + 128],
                                        yn[:, b * 128:(b + 1) * 128], idb[:]).then_inc(sm_mm, 1)
                    @blk.vector
                    def _(v):
                        for b in range(16):
                            v.wait_ge(sm_mm, mm0 + b + 1)
                            v.tensor_scalar_mul(ynT[:, b * 128:(b + 1) * 128],
                                                pbf[:, (b % 2) * 1024:(b % 2) * 1024 + 128],
                                                normw[:, b:b + 1]).then_inc(sm_v, 1)
                c.semc["mm"] = mm0 + 16; c.semc["vv"] = v0 + 16

                mm0, ev0, dma0, st0 = (c.semc[s] for s in ("mm", "ev", "dma", "st"))
                with nc.Block() as blk:
                    @blk.gpsimd
                    def _(g):
                        g.dma_start(xres[:].rearrange("p (kc t) -> p kc t", kc=8),
                                    fview(dxin, 3 + g0, 128)).then_inc(dma, 16)
                    @blk.tensor
                    def _(t):
                        for o in range(8):
                            if o >= 2:
                                t.wait_ge(sm_ev, ev0 + o - 1)
                            for kc in range(16):
                                mmi = t.matmul(pp[o % 2][:, 0:128],
                                               woutr[:, kc * 1024 + o * 128: kc * 1024 + (o + 1) * 128],
                                               ynT[:, kc * 128:(kc + 1) * 128],
                                               start=(kc == 0), stop=(kc == 15))
                            mmi.then_inc(sm_mm, 1)
                    @blk.vector
                    def _(v):
                        for o in range(8):
                            v.wait_ge(sm_mm, mm0 + o + 1)
                            v.wait_ge(dma, dma0 + 16)
                            if o >= 2:
                                v.wait_ge(st_sem, st0 + 16 * (o - 1))
                            v.tensor_add(ost[o % 2][:, 0:128], pp[o % 2][:, 0:128],
                                         xres[:, o * 128:(o + 1) * 128]).then_inc(sm_ev, 1)
                    @blk.sync
                    def _(sy):
                        for o in range(8):
                            sy.wait_ge(sm_ev, ev0 + o + 1)
                            sy.dma_start(dout_x1.ap()[o * 128:(o + 1) * 128, g0:g0 + 128],
                                         ost[o % 2][:, 0:128]).then_inc(st_sem, 16)
                        sy.wait_ge(st_sem, st0 + 16 * 8)
                c.semc["mm"] = mm0 + 8; c.semc["ev"] = ev0 + 8
                c.semc["dma"] = dma0 + 16; c.semc["st"] = st0 + 16 * 8
        if dbg:
            c.dbg_handles = dict(x4b=x4b, BCh=BCh, CD2=CD2, CDB=CDB, E=E, Mp=Mp,
                                 xT=xT, xtf=xtf, ysb=ysb, yn=yn, ynT=ynT, TD=TD,
                                 ETexp=ETexp, ddT=ddT, GmT=GmT, BT=BT, carry=carry,
                                 prevb=prevb, IDall=IDall, zch=zch, rms1=rms1)
            break
        c.pop_scope()
        c.pop_scope()

    st0 = c.semc["st"]
    with nc.Block() as blk:
        @blk.sync
        def _(sy):
            sy.dma_start(dstate.ap(), carry[:]).then_inc(st_sem, 16)
            sy.wait_ge(st_sem, st0 + 16)
    c.semc["st"] = st0 + 16
    return rets



def emit_l0part2(c, dxin, dS0ap, dmask, dW, rets, dout_x1):
    nc = c.nc
    P = get_pool(c)
    psa = P["psa"]; psb = P["psb"]; pp = P["pp"]; pbf = P["pbf"]
    CDL2 = rets["CDL2"]
    dz, dysp, dcsp, desp = rets["dz"], rets["dysp"], rets["dcsp"], rets["desp"]
    sm_mm, sm_ev, sm_as, sm_v = c.sem("mm"), c.sem("ev"), c.sem("as"), c.sem("vv")
    smf = c.sem("fv"); dma = c.sem("dma"); st_sem = c.sem("st")

    idb = c.sb("q_idb", [128, 128], BF16)
    onesrow_f = c.sb("q_onesr", [1, 128])
    normw = c.sb("q_normw", [128, 16])
    woutr = c.sb("q_woutr", [128, 16 * 1024], BF16)
    ones32 = c.sb("q_ones32", [32, 128])
    S0m = c.sb("q_S0m", [128, 2048], BF16)
    S0f = c.sb("q_S0f", [128, 2048])
    mcol = c.sb("q_mcol", [128, 1])
    CC = c.sb("q_CC", [32, 17])
    cdflat = c.sb("q_cdflat", [1, 512])
    CUMB = c.sb("q_CUMB", [128, 512])
    ysb = c.sb("q_ysb", [128, 2048])
    S0c = c.sb("q_S0c", [128, 2048], BF16)
    Cc = c.sb("q_Cc", [128, 128], BF16)
    ETe = c.sb("q_ETe", [128, 32])
    ytmp = c.sb("q_ytmp", [128, 1024])
    sz = c.sb("q_sz", [128, 2048])
    yn = c.sb("q_yn", [128, 2048], BF16)
    ynT = c.sb("q_ynT", [128, 2048], BF16)
    zch = c.sb("q_zch", [128, 2048], BF16)
    xres = c.sb("q_xres", [128, 8 * 128])
    rms1 = c.sb("q_rms1", [128, 4])
    ost = [c.sb(f"q_ost{i}", [128, 512]) for i in range(2)]
    loadblk(c, [(idb[:], dW["ident_bf"].ap()), (onesrow_f[:], dW["onesrow_f"].ap()),
                (normw[:], dW["mb_normw"].ap()), (ones32[:], dW["ones32"].ap()),
                (woutr[:].rearrange("p (kc o) -> p kc o", kc=16),
                 dW["mb_woutT"].ap().rearrange("(kc p) o -> p kc o", p=128)),
                (S0f[:], dS0ap), (mcol[:], dmask.ap())])
    with nc.Block() as blk:
        @blk.vector
        def _(v):
            v.tensor_tensor(S0m[:], S0f[:], bcol(mcol[:], 2048), ALU.mult)
        @blk.gpsimd
        def _(g):
            g.memset(CC[:, 0:1], 0.0)
    with nc.Block() as blk:
        @blk.vector
        def _(v):
            v.tensor_tensor_scan(CC[:, 1:17], CDL2[:], ones32[:, 0:16], 0.0,
                                 ALU.add, ALU.mult)
    with nc.Block() as blk:
        @blk.scalar
        def _(s):
            s.activation(CC[:, 0:16], CC[:, 0:16], AF.Exp)
    dma0 = c.semc["dma"]
    with nc.Block() as blk:
        @blk.gpsimd
        def _(g):
            g.dma_start(cdflat[:].rearrange("p (a b) -> p a b", a=32), CC[:, 0:16]
                        ).then_inc(dma, 16)
            g.wait_ge(dma, dma0 + 16)
    c.semc["dma"] = dma0 + 16
    mm0 = c.semc["mm"]
    with nc.Block() as blk:
        @blk.tensor
        def _(t):
            t.matmul(psa[:, 0:512], onesrow_f[:], cdflat[:],
                     start=True, stop=True).then_inc(sm_mm, 1)
        @blk.vector
        def _(v):
            v.wait_ge(sm_mm, mm0 + 1)
            v.tensor_copy(CUMB[:], psa[:, 0:512])
    c.semc["mm"] = mm0 + 1

    for g in range(16):
        g0 = g * 128
        dma0, mm0, v0, as0, f0, st0 = (c.semc[s] for s in
                                       ("dma", "mm", "vv", "as", "fv", "st"))
        with nc.Block() as blk:
            @blk.gpsimd
            def _(gg):
                gg.dma_start(ysb[:], dysp.ap()[g]).then_inc(dma, 16)
                gg.dma_start(Cc[:], dcsp.ap()[g]).then_inc(dma, 16)
                gg.dma_start(ETe[:], desp.ap()[g]).then_inc(dma, 16)
                gg.dma_start(zch[:], dz.ap()[g0:g0 + 128, :]).then_inc(dma, 16)
                gg.dma_start(xres[:].rearrange("p (kc t) -> p kc t", kc=8),
                             fview(dxin, 3 + g0, 128)).then_inc(dma, 16)
            @blk.vector
            def _(v):
                v.tensor_tensor(S0c[:], S0m[:],
                                bass.AP(CUMB[:].tensor, CUMB[:].offset + g,
                                        [list(CUMB[:].ap[0]), [16, 32], [0, 64]]),
                                ALU.mult).then_inc(sm_v, 1)
            @blk.tensor
            def _(t):
                t.wait_ge(sm_v, v0 + 1)
                t.wait_ge(dma, dma0 + 32)
                for j in range(4):
                    ds = (psa if j < 2 else psb)[:, (j % 2) * 512:(j % 2) * 512 + 512]
                    t.matmul(ds, Cc[:], S0c[:, j * 512:(j + 1) * 512],
                             start=True, stop=True).then_inc(sm_mm, 1)
        c.semc["dma"] = dma0 + 80
        dma_all = dma0 + 80
        c.semc["mm"] = mm0 + 4; c.semc["vv"] = v0 + 1
        v0, as0 = c.semc["vv"], c.semc["as"]
        with nc.Block() as blk:
            @blk.vector
            def _(v):
                v.wait_ge(dma, dma0 + 48)
                for jh in range(2):
                    ps = psa if jh == 0 else psb
                    v.tensor_tensor(ytmp[:], ps[:],
                                    bass.AP(ETe[:].tensor, ETe[:].offset + jh * 16,
                                            [list(ETe[:].ap[0]), [1, 16], [0, 64]]),
                                    ALU.mult)
                    v.tensor_add(ysb[:, jh * 1024:(jh + 1) * 1024],
                                 ysb[:, jh * 1024:(jh + 1) * 1024], ytmp[:])
                v.nop().then_inc(sm_v, 1)
            @blk.scalar
            def _(s):
                s.wait_ge(dma, dma0 + 64)
                s.activation(sz[:], zch[:], AF.Silu).then_inc(sm_as, 1)
        c.semc["vv"] = v0 + 1; c.semc["as"] = as0 + 1
        v0, as0, f0 = c.semc["vv"], c.semc["as"], c.semc["fv"]
        with nc.Block() as blk:
            @blk.vector
            def _(v):
                v.tensor_mul(ysb[:], ysb[:], sz[:])
                v.tensor_mul(sz[:], ysb[:], ysb[:])
                v.reduce_sum(rms1[:, 0:1], sz[:], axis=AX).then_inc(smf, 1)
                v.wait_ge(smf, f0 + 1)
                v.tensor_single_scalar(rms1[:, 1:2], rms1[:, 0:1],
                                       1.0 / 2048.0, ALU.mult).then_inc(sm_v, 1)
                v.wait_ge(sm_as, as0 + 1)
                v.tensor_tensor(yn[:], ysb[:], bcol(rms1[:, 3:4], 2048), ALU.mult)
                v.nop().then_inc(sm_v, 1)
            @blk.scalar
            def _(s):
                s.wait_ge(sm_v, v0 + 1)
                s.activation(rms1[:, 2:3], rms1[:, 1:2], AF.Ln, bias=EPS)
                s.activation(rms1[:, 3:4], rms1[:, 2:3], AF.Exp,
                             scale=f32(-0.5)).then_inc(sm_as, 1)
        c.semc["vv"] = v0 + 2; c.semc["as"] = as0 + 1; c.semc["fv"] = f0 + 1
        mm0, v0 = c.semc["mm"], c.semc["vv"]
        with nc.Block() as blk:
            @blk.tensor
            def _(t):
                for b in range(16):
                    if b >= 2:
                        t.wait_ge(sm_v, v0 + b - 1)
                    t.transpose(pbf[:, (b % 2) * 1024:(b % 2) * 1024 + 128],
                                yn[:, b * 128:(b + 1) * 128], idb[:]).then_inc(sm_mm, 1)
            @blk.vector
            def _(v):
                for b in range(16):
                    v.wait_ge(sm_mm, mm0 + b + 1)
                    v.tensor_scalar_mul(ynT[:, b * 128:(b + 1) * 128],
                                        pbf[:, (b % 2) * 1024:(b % 2) * 1024 + 128],
                                        normw[:, b:b + 1]).then_inc(sm_v, 1)
        c.semc["mm"] = mm0 + 16; c.semc["vv"] = v0 + 16
        mm0, ev0, dma0, st0 = (c.semc[s] for s in ("mm", "ev", "dma", "st"))
        with nc.Block() as blk:
            @blk.tensor
            def _(t):
                for o in range(8):
                    if o >= 2:
                        t.wait_ge(sm_ev, ev0 + o - 1)
                    for kc in range(16):
                        mmi = t.matmul(pp[o % 2][:, 0:128],
                                       woutr[:, kc * 1024 + o * 128: kc * 1024 + (o + 1) * 128],
                                       ynT[:, kc * 128:(kc + 1) * 128],
                                       start=(kc == 0), stop=(kc == 15))
                    mmi.then_inc(sm_mm, 1)
            @blk.vector
            def _(v):
                v.wait_ge(dma, dma_all)
                for o in range(8):
                    v.wait_ge(sm_mm, mm0 + o + 1)
                    if o >= 2:
                        v.wait_ge(st_sem, st0 + 16 * (o - 1))
                    v.tensor_add(ost[o % 2][:, 0:128], pp[o % 2][:, 0:128],
                                 xres[:, o * 128:(o + 1) * 128]).then_inc(sm_ev, 1)
            @blk.sync
            def _(sy):
                for o in range(8):
                    sy.wait_ge(sm_ev, ev0 + o + 1)
                    sy.dma_start(dout_x1.ap()[o * 128:(o + 1) * 128, g0:g0 + 128],
                                 ost[o % 2][:, 0:128]).then_inc(st_sem, 16)
                sy.wait_ge(st_sem, st0 + 16 * 8)
        c.semc["mm"] = mm0 + 8; c.semc["ev"] = ev0 + 8
        c.semc["st"] = st0 + 16 * 8


def _percol(v):
    return np.ascontiguousarray(v.reshape(-1, 128).T)


def prep_weights(I):
    W = {}
    W["ones_bf"] = np.ones((128, 128), dtype=bfd)
    W["ident_bf"] = np.eye(128, dtype=np.float32).astype(bfd)
    W["ident_f32"] = np.eye(128, dtype=np.float32)
    W["l1f_w1T"] = np.ascontiguousarray(I["L1_ffn_w1_w"].T)
    W["l1f_w2T"] = np.ascontiguousarray(I["L1_ffn_w2_w"].T)
    W["l1f_h1T"] = np.ascontiguousarray(I["L1_ffn_h1_w"].T)
    W["l1f_h2T"] = np.ascontiguousarray(I["L1_ffn_h2_w"].T)
    W["l1f_w3T"] = np.ascontiguousarray(I["L1_ffn_w3_w"].T)
    W["l1f_s1"] = _percol(I["L1_ffn_w1_s"])
    W["l1f_s2"] = _percol(I["L1_ffn_w2_s"])
    W["l1f_h1s"] = _percol(I["L1_ffn_h1_s"])
    W["l1f_h2s"] = _percol(I["L1_ffn_h2_s"])
    W["l1f_w3s"] = _percol(I["L1_ffn_w3_s"])
    W["onescol"] = np.ones((128, 1), dtype=np.float32)
    W["moe_gw"] = np.ascontiguousarray(I["L0_ffn_gate_W"])
    oh8 = np.zeros((8, 8 * 128), dtype=np.float32)
    for e in range(8):
        oh8[e, e * 128:(e + 1) * 128] = 1.0
    W["moe_oh8"] = oh8
    W["moe_b1c"] = np.concatenate([_percol(I["L0_ffn_b1"][e]) for e in range(8)], axis=1)
    W["moe_b2"] = np.ascontiguousarray(I["L0_ffn_b2"])
    W["moe_W1f"] = np.ascontiguousarray(I["L0_ffn_W1"].reshape(8192, 2048))
    W["moe_W2f"] = np.ascontiguousarray(I["L0_ffn_W2"].reshape(16384, 1024))
    W["onesrow_bf"] = np.ones((1, 128), dtype=bfd)
    wqkvT = np.ascontiguousarray(I["L1_attn_Wqkv"].T)
    W["l1a_wqT"] = np.ascontiguousarray(wqkvT[:, 0:1024])
    W["l1a_wkT"] = np.ascontiguousarray(wqkvT[:, 1024:2048])
    W["l1a_wvT"] = np.ascontiguousarray(wqkvT[:, 2048:3072])
    W["l1a_woutT"] = np.ascontiguousarray(I["L1_attn_Wout"].T)
    W["l1a_bq"] = _percol(I["L1_attn_bqkv"][0:1024])
    W["l1a_bk"] = _percol(I["L1_attn_bqkv"][1024:2048])
    W["l1a_bv"] = I["L1_attn_bqkv"][2048:3072].reshape(1, 1024).astype(bfd)
    W["l1a_bout"] = _percol(I["L1_attn_bout"])
    ll = np.arange(128)[:, None]; jj = np.arange(256)[None, :]
    mR = np.where((jj >= ll + 65) & (jj <= ll + 128), 0.0, -1e9).astype(np.float32)
    m0 = np.where((jj >= ll + 65) & (jj <= ll + 128) & (jj >= 128), 0.0, -1e9).astype(np.float32)
    W["l1a_maskR"] = mR
    W["l1a_mask0"] = m0
    W["onesrow_f"] = np.ones((1, 128), dtype=np.float32)
    W["ones32"] = np.ones((32, 128), dtype=np.float32)
    W["mb_winT"] = np.ascontiguousarray(I["L0_attn_W_in"].T)
    cw4 = I["L0_attn_conv_w"]
    W["mb_convw"] = np.ascontiguousarray(
        np.concatenate([cw4[ct * 128:(ct + 1) * 128, :] for ct in range(18)], axis=1))
    W["mb_convb"] = _percol(I["L0_attn_conv_b"])
    W["mb_dtb"] = np.ascontiguousarray(I["L0_attn_dt_bias"][:, None])
    W["mb_alog"] = np.ascontiguousarray(I["L0_attn_A_log"][:, None])
    W["mb_drep"] = np.ascontiguousarray(np.tile(I["L0_attn_D"][None, :], (128, 1)))
    W["mb_normw"] = _percol(I["L0_attn_norm_w"])
    W["mb_woutT"] = np.ascontiguousarray(I["L0_attn_W_out"].T)
    W["mb_oh32"] = np.ascontiguousarray(
        np.repeat(np.eye(32, dtype=np.float32), 128, axis=1))
    W["mb_triu"] = np.triu(np.ones((128, 128), dtype=np.float32))
    return W


def _declare(c, W, names):
    d = {}
    for nm in names:
        a = W[nm]
        dt = BF16 if a.dtype == bfd else F32
        d[nm] = c.din(nm, a.shape, dt)
    return d


MOE_W = ["ones_bf", "onescol", "ident_f32", "moe_gw", "moe_oh8", "moe_b1c",
         "moe_b2", "moe_W1f", "moe_W2f"]


def launch_l0ffn(x1_shards, W):
    c = Ctx()
    dx1 = c.din("x1", [D, T])
    dW = _declare(c, W, MOE_W)
    do = c.dout("x2", [D, T])
    emit_l0ffn(c, dx1, dW, do)
    c.nc.finalize()
    in_maps = [dict({nm: W[nm] for nm in MOE_W}, x1=x1_shards[i]) for i in range(N_CORES)]
    res = run_bass_kernel_spmd(c.nc, in_maps, core_ids=list(range(N_CORES)))
    return [r["x2"] for r in res.results]


MAMBA_W = ["ones_bf", "onesrow_f", "ident_f32", "ident_bf", "ones32",
           "mb_winT", "mb_convw", "mb_convb", "mb_dtb", "mb_alog", "mb_drep",
           "mb_normw", "mb_woutT", "mb_oh32", "mb_triu"]


def shard_xin(x_full):
    out = []
    for cid in range(N_CORES):
        bi, hi = cid // 2, cid % 2
        if hi == 0:
            rows = np.concatenate([np.zeros((3, D), np.float32),
                                   x_full[bi, 0:T, :]], axis=0)
        else:
            rows = x_full[bi, T - 3:2 * T, :]
        out.append(np.ascontiguousarray(rows.T))
    return out


def launch_l0attn(xin_shards, S0_list, W):
    c = Ctx()
    dxin = c.din("xin", [D, 2051])
    dS0 = c.din("S0", [128, 2048])
    dW = _declare(c, W, MAMBA_W)
    do = c.dout("x1", [D, T])
    dst = c.dout("fstate", [128, 2048])
    emit_l0attn(c, dxin, dS0.ap(), dW, do, dst)
    c.nc.finalize()
    in_maps = [dict({nm: W[nm] for nm in MAMBA_W}, xin=xin_shards[i], S0=S0_list[i])
               for i in range(N_CORES)]
    res = run_bass_kernel_spmd(c.nc, in_maps, core_ids=list(range(N_CORES)))
    return [r["x1"] for r in res.results], [r["fstate"] for r in res.results]


ATTN_W = ["ones_bf", "onesrow_bf", "ident_bf", "l1a_wqT", "l1a_wkT", "l1a_wvT",
          "l1a_woutT", "l1a_bq", "l1a_bk", "l1a_bv", "l1a_bout", "l1a_maskR",
          "l1a_mask0"]


def launch_l1attn(x2_shards, W):
    c = Ctx()
    dx2 = c.din("x2", [D, T])
    dxh = c.din("xh", [D, 63])
    dW = _declare(c, W, ATTN_W)
    do = c.dout("x3", [D, T])
    emit_l1attn(c, dx2, dxh.ap(), dW, do)
    c.nc.finalize()
    in_maps = []
    for i in range(N_CORES):
        xh = np.zeros((D, 63), dtype=np.float32)
        m = dict({nm: W[nm] for nm in ATTN_W}, x2=x2_shards[i], xh=xh)
        if i % 2 == 1:
            m["xh"] = np.ascontiguousarray(x2_shards[i - 1][:, 2048 - 63:2048])
            m["l1a_mask0"] = W["l1a_maskR"]
        in_maps.append(m)
    res = run_bass_kernel_spmd(c.nc, in_maps, core_ids=list(range(N_CORES)))
    return [r["x3"] for r in res.results]


L1FFN_W = ["ones_bf", "l1f_w1T", "l1f_w2T", "l1f_h1T", "l1f_h2T", "l1f_w3T",
           "l1f_s1", "l1f_s2", "l1f_h1s", "l1f_h2s", "l1f_w3s"]


def launch_l1ffn(x3_shards, W):
    c = Ctx()
    dx3 = c.din("x3", [D, T])
    dW = _declare(c, W, L1FFN_W)
    do = c.dout("x4", [D, T])
    emit_l1ffn(c, dx3, dW, do)
    c.nc.finalize()
    in_maps = [dict({nm: W[nm] for nm in L1FFN_W}, x3=x3_shards[i]) for i in range(N_CORES)]
    res = run_bass_kernel_spmd(c.nc, in_maps, core_ids=list(range(N_CORES)))
    return [r["x4"] for r in res.results]


def shard_rows(full):
    out = []
    for cid in range(N_CORES):
        bi, hi = cid // 2, cid % 2
        out.append(np.ascontiguousarray(full[bi, hi * T:(hi + 1) * T, :].T))
    return out


def unshard_rows(shards):
    full = np.empty((4, 4096, D), dtype=np.float32)
    for cid in range(N_CORES):
        bi, hi = cid // 2, cid % 2
        full[bi, hi * T:(hi + 1) * T, :] = np.asarray(shards[cid]).T
    return full


ALL_W = None


def build_fused(W, dev=False):
    global ALL_W
    ALL_W = sorted(set(MAMBA_W) | set(MOE_W) | set(ATTN_W) | set(L1FFN_W))
    c = Ctx()
    nc = c.nc
    dxin = c.din("xin", [D, 2051])
    dmask = c.din("smask", [128, 1])
    dW = _declare(c, W, ALL_W)
    dout = c.dout("out", [D, T])
    mk = c.dout if dev else c.dtmp
    dx1 = mk("f_x1", [D, T])
    dx2 = mk("f_x2", [D, T])
    dx3 = mk("f_x3", [D, T])
    dsend = c.dtmp("f_send", [128, 2048])
    drecv = c.dtmp("f_recv", [256, 2048])
    dsend2 = c.dtmp("f_send2", [D, 63])
    drecv2 = c.dtmp("f_recv2", [2 * D, 63])
    ddum = c.dtmp("f_dumstate", [128, 2048])
    cc_sem = c.sem("cc")

    cdl2 = c.sb("f_CDL2", [32, 16])
    c.push_scope()
    with nc.named_scope("mamba_main"):
        rets = emit_l0attn(c, dxin, None, dW, dx1, dsend, onepass=True, cdl2=cdl2)
    c.pop_scope()
    with nc.Block() as blk:
        @blk.gpsimd
        def _(g):
            g.collective_compute(
                "AllGather", ALU.bypass,
                replica_groups=[[0, 1], [2, 3], [4, 5], [6, 7]],
                ins=[dsend.ap().opt()], outs=[drecv.ap().opt()],
            ).then_inc(cc_sem)
            g.wait_ge(cc_sem, 1)
    c.semc["cc"] = 1
    c.push_scope()
    with nc.named_scope("mamba_post"):
        emit_l0part2(c, dxin, drecv.ap()[0:128, :], dmask, dW, rets, dx1)
    c.pop_scope()
    c.push_scope()
    with nc.named_scope("moe"):
        emit_l0ffn(c, dx1, dW, dx2)
    c.pop_scope()
    dma, st_sem = c.sem("dma"), c.sem("st")
    dma0 = c.semc["dma"]
    with nc.Block() as blk:
        @blk.gpsimd
        def _(g):
            g.dma_start(dsend2.ap(), dx2.ap()[:, T - 63:T]).then_inc(dma, 16)
            g.wait_ge(dma, dma0 + 16)
            g.collective_compute(
                "AllGather", ALU.bypass,
                replica_groups=[[0, 1], [2, 3], [4, 5], [6, 7]],
                ins=[dsend2.ap().opt()], outs=[drecv2.ap().opt()],
            ).then_inc(cc_sem)
            g.wait_ge(cc_sem, 2)
    c.semc["dma"] = dma0 + 16
    c.semc["cc"] = 2
    c.push_scope()
    with nc.named_scope("attn"):
        emit_l1attn(c, dx2, drecv2.ap()[0:D, :], dW, dx3)
    c.pop_scope()
    c.push_scope()
    with nc.named_scope("l1ffn"):
        emit_l1ffn(c, dx3, dW, dout)
    c.pop_scope()
    return c


def kernel_fused(**inputs):
    W = prep_weights(inputs)
    c = build_fused(W)
    c.nc.finalize()
    xin_shards = shard_xin(inputs["L0_attn_inputs"])
    in_maps = []
    for i in range(N_CORES):
        m = {nm: W[nm] for nm in ALL_W}
        m["xin"] = xin_shards[i]
        m["smask"] = np.full((128, 1), 1.0 if i % 2 == 1 else 0.0, np.float32)
        if i % 2 == 1:
            m["l1a_mask0"] = W["l1a_maskR"]
        in_maps.append(m)
    res = run_bass_kernel_spmd(c.nc, in_maps, core_ids=list(range(N_CORES)))
    return unshard_rows([r["out"] for r in res.results])


def kernel_unfused(**inputs):
    W = prep_weights(inputs)
    xin_shards = shard_xin(inputs["L0_attn_inputs"])
    zeros = [np.zeros((128, 2048), np.float32) for _ in range(N_CORES)]
    _, F = launch_l0attn(xin_shards, zeros, W)
    S0 = [zeros[i] if i % 2 == 0 else np.ascontiguousarray(np.asarray(F[i - 1]))
          for i in range(N_CORES)]
    x1_shards, _ = launch_l0attn(xin_shards, S0, W)
    x1_shards = [np.ascontiguousarray(np.asarray(a)) for a in x1_shards]
    x2_shards = [np.ascontiguousarray(np.asarray(a)) for a in launch_l0ffn(x1_shards, W)]
    x3_shards = [np.ascontiguousarray(np.asarray(a)) for a in launch_l1attn(x2_shards, W)]
    x4_shards = launch_l1ffn(x3_shards, W)
    return unshard_rows(x4_shards)


def kernel(**inputs):
    return kernel_fused(**inputs)
